# version 12
# baseline (speedup 1.0000x reference)
import math
import numpy as np
import ml_dtypes
from contextlib import ExitStack

import concourse.bass as bass
import concourse.mybir as mybir
from concourse.bass_utils import run_bass_kernel_spmd

F32 = mybir.dt.float32
BF16 = mybir.dt.bfloat16
AF = mybir.ActivationFunctionType
ALU = mybir.AluOpType
AX = mybir.AxisListType
BF = ml_dtypes.bfloat16

B_, S, D = 4, 4096, 2048
NH, DN, DR, DV = 16, 128, 64, 128
QR, KVR, INTER = 1536, 512, 5632
NDC = D // 128
NRQ = QR // 128
NRKV = KVR // 128
NIC = INTER // 128
SCALE = 1.0 / math.sqrt(DN + DR)
EPS_LN, EPS_RMS = 1e-5, 1e-6
NEG = -1.0e30
THETA = 10000.0
OWN = S // 2



def _own_blocks(par):
    return [j for j in range(32) if ((j // 4 + j % 4) % 2) == par]


def _rope_perm_q():
    perm = []
    for c in range(NH // 2):
        for h in (2 * c, 2 * c + 1):
            perm += [h * DR + 2 * i for i in range(32)]
            perm += [h * DR + 2 * i + 1 for i in range(32)]
    return perm


def _rope_perm_k():
    ev = [2 * i for i in range(32)]
    od = [2 * i + 1 for i in range(32)]
    return ev + od + ev + od


def _cs_table(positions):
    invf = THETA ** (-np.arange(0, DR, 2, dtype=np.float64) / DR)
    t = np.asarray(positions, dtype=np.float64)
    ang = t[None, :] * invf[:, None]
    c = np.cos(ang).astype(np.float32)
    s = np.sin(ang).astype(np.float32)
    return np.ascontiguousarray(np.concatenate([c, c, s, s], axis=0))


def _prep(inputs):
    a = lambda x: np.ascontiguousarray(np.asarray(x, dtype=np.float32))
    w = {}
    w["wqdT"] = a(np.asarray(inputs["L0_attn_wq_down"]).T)
    w["wquT"] = a(np.asarray(inputs["L0_attn_wq_up"]).T)
    w["wqrT"] = a(np.asarray(inputs["L0_attn_wq_rope"])[_rope_perm_q()].T)
    w["wkvdT"] = a(np.asarray(inputs["L0_attn_wkv_down"]).T)
    kvu = np.asarray(inputs["L0_attn_wkv_up"])
    k_rows = [h * (DN + DV) + d for h in range(NH) for d in range(DN)]
    v_rows = [h * (DN + DV) + DN + d for h in range(NH) for d in range(DV)]
    w["wkvukT"] = a(kvu[k_rows].T)
    w["wkvuvT"] = a(kvu[v_rows].T)
    w["wkrT"] = a(np.asarray(inputs["L0_attn_wk_rope"])[_rope_perm_k()].T)
    w["woT"] = a(np.asarray(inputs["L0_attn_wo"]).T)
    w["gateT"] = a(np.asarray(inputs["L0_ffn_gate_w"]).T)
    w["upT"] = a(np.asarray(inputs["L0_ffn_up_w"]).T)
    w["downT"] = a(np.asarray(inputs["L0_ffn_down_w"]).T)
    w["qnw"] = a(np.asarray(inputs["L0_attn_q_norm_w"]).reshape(NRQ, 128).T)
    w["kvnw"] = a(np.asarray(inputs["L0_attn_kv_norm_w"]).reshape(NRKV, 128).T)
    w["gs"] = a(np.asarray(inputs["L0_ffn_gate_s"]).reshape(NIC, 128).T)
    w["us"] = a(np.asarray(inputs["L0_ffn_up_s"]).reshape(NIC, 128).T)
    w["ds"] = a(np.asarray(inputs["L0_ffn_down_s"]).reshape(NDC, 128).T)
    w["cs_tab"] = _cs_table(np.arange(S))
    w["ident16"] = np.eye(128, dtype=BF)
    w["ident32"] = np.eye(128, dtype=np.float32)
    x = np.asarray(inputs["L0_attn_x"], dtype=np.float32)
    maps, metas = [], []
    for c in range(8):
        b, par = c // 2, c % 2
        blocks = _own_blocks(par)
        rows = np.concatenate([np.arange(j * 128, (j + 1) * 128) for j in blocks])
        m = dict(w)
        m["xb"] = np.ascontiguousarray(x[b])
        m["x_own"] = np.ascontiguousarray(x[b][rows])
        m["cs_q"] = _cs_table(rows)
        mask = np.empty((16 * 128, 512), dtype=np.float32)
        for si, j in enumerate(blocks):
            nc_ = j // 4 + 1
            kcol = (nc_ - 1) * 512 + np.arange(512)[None, :]
            qrow = j * 128 + np.arange(128)[:, None]
            mask[si * 128:(si + 1) * 128] = np.where(kcol <= qrow, 0.0, NEG)
        m["masks"] = mask
        maps.append(m)
        metas.append(rows)
    return maps, metas



class KB:
    SEMS = (["ph", "p", "v", "a", "gq", "qc", "qw",
             "xb0", "xb1", "stk0", "stk1", "stv0", "stv1",
             "stn0", "stn1", "stf0", "stf1",
             "xnf", "csq0", "csq1", "stg0", "stg1",
             "k00", "k01", "k10", "k11", "v00", "v01", "v10", "v11",
             "q0", "q1", "qm", "sto0", "sto1",
             "ob0", "ob1", "sts0", "sts1",
             "cxn", "sx0", "sx1", "gu0", "gu1", "dw0", "dw1", "w16a", "w16b",
             "out0", "out1"])

    def __init__(self):
        self.dry = True
        self.n = {s: 0 for s in self.SEMS}
        self.ms = {}
        self.sh = {}
        self.lastwait = {}

    def inc(self, instr, sem, k=1):
        self.n[sem] += k
        if not self.dry:
            instr.then_inc(self.sh[sem], k)

    def mark(self, key, sem):
        if self.dry:
            self.ms[key] = self.n[sem]

    def wg(self, eng, sem, val):
        if self.dry or val is None or val <= 0:
            return
        k = (id(eng), sem)
        if self.lastwait.get(k, -1) >= val:
            return
        self.lastwait[k] = val
        eng.wait_ge(self.sh[sem], val)

    def wgm(self, eng, sem, key):
        if self.dry:
            return
        self.wg(eng, sem, self.ms.get(key))

    def selfw(self, eng, sem):
        self.wg(eng, sem, self.n[sem])


class _DryInstr:
    def then_inc(self, *a, **k):
        return self


class DryEng:
    def __getattr__(self, name):
        return lambda *a, **k: _DryInstr()


def r3(ap2d, nchunks):
    return ap2d.rearrange("p (a f) -> p a f", a=nchunks)


def dr3(dram_ap, p=128):
    return dram_ap.rearrange("(a p) f -> p a f", p=p)



def build(debug=False, upto=99):
    nc = bass.Bass("TRN2", target_bir_lowering=False)
    dt = lambda name, shape, dtype, kind: nc.dram_tensor(name, shape, dtype, kind=kind).ap()

    xb_d = dt("xb", [S, D], F32, "ExternalInput")
    xown_d = dt("x_own", [OWN, D], F32, "ExternalInput")
    csq_d = dt("cs_q", [128, OWN], F32, "ExternalInput")
    cst_d = dt("cs_tab", [128, S], F32, "ExternalInput")
    mask_d = dt("masks", [16 * 128, 512], F32, "ExternalInput")
    wqdT_d = dt("wqdT", [D, QR], F32, "ExternalInput")
    wquT_d = dt("wquT", [QR, D], F32, "ExternalInput")
    wqrT_d = dt("wqrT", [QR, NH * DR], F32, "ExternalInput")
    wkvdT_d = dt("wkvdT", [D, KVR], F32, "ExternalInput")
    wkvukT_d = dt("wkvukT", [KVR, NH * DN], F32, "ExternalInput")
    wkvuvT_d = dt("wkvuvT", [KVR, NH * DV], F32, "ExternalInput")
    wkrT_d = dt("wkrT", [D, 128], F32, "ExternalInput")
    woT_d = dt("woT", [D, D], F32, "ExternalInput")
    gateT_d = dt("gateT", [D, INTER], F32, "ExternalInput")
    upT_d = dt("upT", [D, INTER], F32, "ExternalInput")
    downT_d = dt("downT", [INTER, D], F32, "ExternalInput")
    qnw_d = dt("qnw", [128, NRQ], F32, "ExternalInput")
    kvnw_d = dt("kvnw", [128, NRKV], F32, "ExternalInput")
    gs_d = dt("gs", [128, NIC], F32, "ExternalInput")
    us_d = dt("us", [128, NIC], F32, "ExternalInput")
    ds_d = dt("ds", [128, NDC], F32, "ExternalInput")
    i16_d = dt("ident16", [128, 128], BF16, "ExternalInput")
    i32_d = dt("ident32", [128, 128], F32, "ExternalInput")
    out_d = dt("out", [OWN, D], F32, "ExternalOutput")

    sk = "ExternalOutput" if debug else "Internal"
    kscr = dt("kscr", [NH * DN, S], BF16, sk)
    vscr = dt("vscr", [NH * 32 * 128, 128], BF16, sk)
    qnscr = dt("qnscr", [NH * DN, OWN], BF16, sk)
    qpscr = dt("qpscr", [(NH // 2) * 128, OWN], BF16, sk)
    xnscr = dt("xnscr", [D, OWN], BF16, sk)
    xfscr = dt("xfscr", [D, OWN], F32, sk)
    oscr = dt("oscr", [NH * DV, OWN], BF16, sk)
    sscr = dt("sscr", [D, OWN], F32, sk)
    g16 = dt("g16", [D, INTER], BF16, "Internal")
    u16 = dt("u16", [D, INTER], BF16, "Internal")
    d16 = dt("d16", [INTER, D], BF16, "Internal")
    wo16 = dt("wo16", [D, D], BF16, "Internal")

    kb = KB()

    with ExitStack() as gctx:
        kro = gctx.enter_context(nc.sbuf_tensor("kro", [128, S], BF16))
        i16 = gctx.enter_context(nc.sbuf_tensor("i16", [128, 128], BF16))
        i32 = gctx.enter_context(nc.sbuf_tensor("i32", [128, 128], F32))
        ones_c = gctx.enter_context(nc.sbuf_tensor("ones_c", [128, 1], F32))
        ones_r = gctx.enter_context(nc.sbuf_tensor("ones_r", [1, 128], F32))
        for s in KB.SEMS:
            kb.sh[s] = gctx.enter_context(nc.semaphore("sem_" + s))
        blk = gctx.enter_context(nc.Block())

        phase_idx = [0]

        DMA_SEMS = [s for s in KB.SEMS if s not in ("ph", "p", "v", "a", "gq")]

        OWN_SEM = {"sync": None, "gpsimd": "gq", "tensor": "p",
                   "vector": "v", "scalar": "a"}

        def run_phase(sections):
            def wrap(name, fn):
                own = OWN_SEM[name]

                def g(e):
                    bw(e)
                    pre = dict(kb.n)
                    fn(e)
                    for s_ in DMA_SEMS:
                        if kb.n[s_] != pre[s_]:
                            kb.selfw(e, s_)
                    if own:
                        kb.selfw(e, own)
                    bi(e)
                return g
            wsecs = [(n_, wrap(n_, f_)) for n_, f_ in sections]
            snap = dict(kb.n)
            kb.dry = True
            for n_, f_ in wsecs:
                f_(DryEng())
            kb.n = dict(snap)
            kb.dry = False
            for n_, f_ in wsecs:
                getattr(blk, n_)(f_)
            kb.dry = True
            phase_idx[0] += 1

        def bw(eng):
            kb.wg(eng, "ph", 5 * phase_idx[0])

        def bi(eng):
            kb.inc(eng.nop(), "ph")

        def ch(e, instr, sem="v"):
            kb.inc(instr, sem)

        def ln_stats_vec(e, xb_ap, xsq_ap, stt_h, tag):
            stt = {k: v[:] for k, v in stt_h.items()}
            kb.selfw(e, "v")
            ch(e, e.reduce_sum(stt["s1"], xb_ap, AX.X))
            kb.wgm(e, "a", f"{tag}_xsq")
            ch(e, e.reduce_sum(stt["s2"], xsq_ap, AX.X))
            kb.selfw(e, "v")
            ch(e, e.tensor_scalar(stt["mean"], stt["s1"], 1.0 / D, None,
                                  ALU.mult, ALU.bypass))
            ch(e, e.tensor_scalar(stt["s2"], stt["s2"], 1.0 / D, None,
                                  ALU.mult, ALU.bypass))
            kb.selfw(e, "v")
            ch(e, e.tensor_mul(stt["t2"], stt["mean"], stt["mean"]))
            kb.selfw(e, "v")
            ch(e, e.scalar_tensor_tensor(stt["u"], stt["s2"], EPS_LN, stt["t2"],
                                         ALU.add, ALU.subtract))
            kb.selfw(e, "v")
            kb.inc(e.reciprocal(stt["w1"], stt["u"]), "v")
            kb.mark(f"{tag}_w1", "v")
            kb.wgm(e, "a", f"{tag}_r0")
            ch(e, e.tensor_mul(stt["t2"], stt["r0"], stt["r0"]))
            kb.selfw(e, "v")
            ch(e, e.tensor_mul(stt["t2"], stt["t2"], stt["u"]))
            kb.selfw(e, "v")
            ch(e, e.tensor_scalar(stt["t2"], stt["t2"], -0.5, 1.5, ALU.mult, ALU.add))
            kb.selfw(e, "v")
            ch(e, e.tensor_mul(stt["rstd"], stt["r0"], stt["t2"]))
            kb.selfw(e, "v")
            ch(e, e.tensor_mul(stt["nb"], stt["mean"], stt["rstd"]))
            kb.selfw(e, "v")
            kb.inc(e.tensor_scalar(stt["nb"], stt["nb"], -1.0, None, ALU.mult,
                                   ALU.bypass), "v")
            kb.mark(f"{tag}_ln", "v")

        def rms_row_vec(e, R, ps_st, rank, tag):
            kb.selfw(e, "v")
            ch(e, e.tensor_scalar(R(0), ps_st[:], 1.0 / rank, EPS_RMS, ALU.mult, ALU.add))
            kb.selfw(e, "v")
            kb.inc(e.reciprocal(R(1), R(0)), "v")
            kb.mark(f"{tag}_w1", "v")
            kb.wgm(e, "a", f"{tag}_r0")
            ch(e, e.tensor_mul(R(3), R(2), R(2)))
            kb.selfw(e, "v")
            ch(e, e.tensor_mul(R(3), R(3), R(0)))
            kb.selfw(e, "v")
            ch(e, e.tensor_scalar(R(3), R(3), -0.5, 1.5, ALU.mult, ALU.add))
            kb.selfw(e, "v")
            kb.inc(e.tensor_mul(R(4), R(2), R(3)), "v")
            kb.mark(f"{tag}_rstd", "v")

        def rope_apply(e, dst, ps_ap, cc, ss, tmp1, tmp2):
            t1, t2 = tmp1[0:32, :], tmp2[0:32, :]
            last = None
            for half in (0, 64):
                xe = ps_ap[half:half + 32, :]
                xo = ps_ap[half + 32:half + 64, :]
                kb.selfw(e, "v")
                ch(e, e.tensor_mul(t1, xe, cc))
                ch(e, e.tensor_mul(t2, xo, ss))
                kb.selfw(e, "v")
                ch(e, e.tensor_sub(dst[half:half + 32, :], t1, t2))
                kb.selfw(e, "v")
                ch(e, e.tensor_mul(t1, xo, cc))
                ch(e, e.tensor_mul(t2, xe, ss))
                kb.selfw(e, "v")
                last = e.tensor_add(dst[half + 32:half + 64, :], t1, t2)
                if half == 0:
                    ch(e, last)
            return last

        def ph0_sync(e):
            kb.inc(e.dma_start(out=i16[:], in_=i16_d), "qc", 16)
            kb.inc(e.dma_start(out=i32[:], in_=i32_d), "qc", 16)
            kb.mark("consts", "qc")

        def ph0_vec(e):
            kb.wgm(e, "qc", "consts")
            e.memset(ones_c[:], 1.0)
            kb.inc(e.memset(ones_r[:], 1.0), "v")
            kb.mark("ones", "v")

        def ph0_other(e):
            pass

        run_phase([("sync", ph0_sync), ("vector", ph0_vec), ("tensor", ph0_other),
                   ("scalar", ph0_other), ("gpsimd", ph0_other)])

        with ExitStack() as ctx:
            sb = lambda n, sh, d: ctx.enter_context(nc.sbuf_tensor(n, sh, d))
            ps = lambda n, sh, d: ctx.enter_context(nc.psum_tensor(n, sh, d))
            xbuf = [sb(f"a1_xbuf{i}", [128, D], F32) for i in range(2)]
            xsq = [sb(f"a1_xsq{i}", [128, D], F32) for i in range(2)]
            xnt = [sb(f"a1_xnt{i}", [128, D], BF16) for i in range(2)]
            stt_t = [{k: sb(f"a1_st_{k}{i}", [128, 1], F32)
                      for k in ["s1", "s2", "mean", "u", "w1", "r0", "t2", "rstd", "nb"]}
                     for i in range(2)]
            xnf = sb("a1_xnf", [128, NDC * 512], BF16)
            ykv = sb("a1_ykv", [128, NRKV * 512], F32)
            ropesc = [sb(f"a1_rsc{i}", [64, 512], F32) for i in range(2)]
            sqkv = [sb(f"a1_sqkv{i}", [128, 512], F32) for i in range(2)]
            kvcn = sb("a1_kvcn", [128, NRKV * 512], BF16)
            rrow = sb("a1_rrow", [1, 512 * 5], F32)
            Rr = lambda i: rrow[:, i * 512:(i + 1) * 512]
            kst = [sb(f"a1_kst{i}", [128, 512], BF16) for i in range(2)]
            vst = [sb(f"a1_vst{i}", [128, NH * 128], BF16) for i in range(2)]
            wkvd = sb("a1_wkvd", [128, NDC * KVR], BF16)
            wkvuk = sb("a1_wkvuk", [128, NRKV * 2048], BF16)
            wkvuv = sb("a1_wkvuv", [128, NRKV * 2048], BF16)
            wkr = sb("a1_wkr", [128, NDC * 128], BF16)
            kvnw = sb("a1_kvnw", [128, NRKV], F32)
            cst = sb("a1_cst", [128, S], F32)
            ps_tr = [ps(f"a1_pstr{i}", [128, 512], BF16) for i in range(2)]
            ps_mm = [ps(f"a1_psmm{i}", [128, 512], F32) for i in range(3)]
            ps_st = ps("a1_psst", [1, 512], F32)
            ps_bc = ps("a1_psbc", [128, 512], F32)

            def mm_jobs():
                out = [("kv", rc) for rc in range(NRKV)]
                out.append(("kr",))
                out += [("k", hc) for hc in range(NH)]
                out += [("v", kt, dvc) for kt in range(4) for dvc in range(4)]
                return out

            JPC = 37

            def a1_sync(e):
                kb.inc(e.dma_start(out=cst[:], in_=cst_d), "qc", 16)
                kb.inc(e.dma_start(out=kvnw[:], in_=kvnw_d), "qc", 16)
                kb.mark("a1_cst", "qc")
                for bt in range(32):
                    kb.wgm(e, "a", f"a1t{bt - 2}_xn")
                    sx = f"xb{bt % 2}"
                    kb.inc(e.dma_start(out=xbuf[bt % 2][:],
                                       in_=xb_d[bt * 128:(bt + 1) * 128, :]), sx, 16)
                    kb.mark(f"a1_x_{bt}", sx)

            def a1_gp(e):
                kb.inc(e.dma_start(out=r3(wkvd[:], NDC), in_=dr3(wkvdT_d)), "qw", 16)
                kb.inc(e.dma_start(out=r3(wkvuk[:], NRKV), in_=dr3(wkvukT_d)), "qw", 16)
                kb.inc(e.dma_start(out=r3(wkvuv[:], NRKV), in_=dr3(wkvuvT_d)), "qw", 16)
                kb.inc(e.dma_start(out=r3(wkr[:], NDC), in_=dr3(wkrT_d)), "qw", 16)
                kb.mark("a1_w", "qw")

            def a1_tensor(e):
                kb.wgm(e, "qw", "a1_w")
                for c in range(8):
                    for t in range(4):
                        bt = 4 * c + t
                        kb.wgm(e, "a", f"a1t{bt}_xn")
                        for q in range(4):
                            tb = bt * 4 + q
                            kb.wgm(e, "v", f"a1_trev_{tb - 2}")
                            for u in range(4):
                                dc = q * 4 + u
                                mm_ = e.transpose(ps_tr[tb % 2][:, u * 128:(u + 1) * 128],
                                                  xnt[bt % 2][:, dc * 128:(dc + 1) * 128],
                                                  i16[:])
                            kb.inc(mm_, "p")
                            kb.mark(f"a1_tr_{tb}", "p")
                    kb.wgm(e, "v", f"a1_xnf_{c}")
                    for i, job in enumerate(mm_jobs()):
                        g = c * JPC + i
                        slot = ps_mm[g % 3]
                        if job[0] == "k" and job[1] == 0:
                            kb.wgm(e, "v", f"a1_rmsrow{c - 1}_rstd")
                            for rc in range(NRKV):
                                kb.wgm(e, "a", f"a1_sq_{c}_{rc}")
                                mm_ = e.matmul(ps_st[:], ones_c[:], sqkv[rc % 2][:],
                                               start=(rc == 0), stop=(rc == NRKV - 1))
                                kb.inc(mm_, "p")
                                kb.mark(f"a1_stp_{c}_{rc}", "p")
                            kb.mark(f"a1_stmm_{c}", "p")
                            kb.wgm(e, "v", f"a1_rmsrow{c}_rstd")
                            kb.wgm(e, "v", f"a1_kvcn_{c - 1}")
                            kb.inc(e.matmul(ps_bc[:], ones_r[:], Rr(4),
                                            start=True, stop=True), "p")
                            kb.mark(f"a1_bc_{c}", "p")
                        kb.wgm(e, "v", f"a1_mmev_{g - 3}")
                        if job[0] == "kv":
                            rc = job[1]
                            for dc in range(NDC):
                                mm_ = e.matmul(slot[:],
                                               wkvd[:, dc * KVR + rc * 128:dc * KVR + (rc + 1) * 128],
                                               xnf[:, dc * 512:(dc + 1) * 512],
                                               start=(dc == 0), stop=(dc == NDC - 1))
                        elif job[0] == "kr":
                            for dc in range(NDC):
                                mm_ = e.matmul(slot[:],
                                               wkr[:, dc * 128:(dc + 1) * 128],
                                               xnf[:, dc * 512:(dc + 1) * 512],
                                               start=(dc == 0), stop=(dc == NDC - 1))
                        elif job[0] == "k":
                            hc = job[1]
                            if hc == 0:
                                kb.wgm(e, "v", f"a1_kvcn_{c}")
                            for rc in range(NRKV):
                                mm_ = e.matmul(slot[:],
                                               wkvuk[:, rc * 2048 + hc * 128:rc * 2048 + (hc + 1) * 128],
                                               kvcn[:, rc * 512:(rc + 1) * 512],
                                               start=(rc == 0), stop=(rc == NRKV - 1))
                        else:
                            _, kt, dvc = job
                            for rc in range(NRKV):
                                mm_ = e.matmul(slot[:],
                                               kvcn[:, rc * 512 + kt * 128:rc * 512 + (kt + 1) * 128],
                                               wkvuv[:, rc * 2048 + dvc * 512:rc * 2048 + (dvc + 1) * 512],
                                               start=(rc == 0), stop=(rc == NRKV - 1))
                        kb.inc(mm_, "p")
                        kb.mark(f"a1_mm_{g}", "p")

            def a1_vec(e):
                kb.wgm(e, "v", "ones")
                for c in range(8):
                    for t in range(4):
                        bt = 4 * c + t
                        kb.wgm(e, f"xb{bt % 2}", f"a1_x_{bt}")
                        ln_stats_vec(e, xbuf[bt % 2][:], xsq[bt % 2][:],
                                     stt_t[bt % 2], f"a1t{bt}")
                        for q in range(4):
                            tb = bt * 4 + q
                            kb.wgm(e, "p", f"a1_tr_{tb}")
                            kb.wgm(e, "p", f"a1_mm_{(c - 1) * JPC + 4}")
                            kb.inc(e.tensor_copy(
                                r3(xnf[:], NDC)[:, q * 4:(q + 1) * 4, t * 128:(t + 1) * 128],
                                r3(ps_tr[tb % 2][:], 4)), "v")
                            kb.mark(f"a1_trev_{tb}", "v")
                    kb.mark(f"a1_xnf_{c}", "v")
                    for i, job in enumerate(mm_jobs()):
                        g = c * JPC + i
                        slot = ps_mm[g % 3]
                        kb.wgm(e, "p", f"a1_mm_{g}")
                        if job[0] == "kv":
                            rc = job[1]
                            kb.inc(e.tensor_copy(ykv[:, rc * 512:(rc + 1) * 512], slot[:]),
                                   "v")
                            kb.mark(f"a1_ykv_{c}_{rc}", "v")
                            if rc == NRKV - 1:
                                pass
                        elif job[0] == "kr":
                            kb.wgm(e, "p", f"a1_stmm_{c}")
                            rms_row_vec(e, Rr, ps_st, KVR, f"a1_rmsrow{c}")
                            kb.wgm(e, "p", f"a1_bc_{c}")
                            kb.wgm(e, "qc", "a1_cst")
                            kb.selfw(e, "v")
                            for rc in range(NRKV):
                                kb.inc(e.scalar_tensor_tensor(
                                    kvcn[:, rc * 512:(rc + 1) * 512],
                                    ykv[:, rc * 512:(rc + 1) * 512],
                                    kvnw[:, rc:rc + 1], ps_bc[:],
                                    ALU.mult, ALU.mult), "v")
                            kb.mark(f"a1_kvcn_{c}", "v")
                            cc = cst[0:32, c * 512:(c + 1) * 512]
                            ss = cst[64:96, c * 512:(c + 1) * 512]
                            kb.inc(rope_apply(e, kro[:, c * 512:(c + 1) * 512], slot[:],
                                              cc, ss, ropesc[0][:], ropesc[1][:]), "v")
                            kb.mark(f"a1_kro_{c}", "v")
                        elif job[0] == "k":
                            hc = job[1]
                            kidx = c * NH + hc
                            kb.wgm(e, f"stk{kidx % 2}", f"a1_kstf_{kidx - 2}")
                            kb.inc(e.tensor_copy(kst[hc % 2][:], slot[:]), "v")
                            kb.mark(f"a1_kev_{kidx}", "v")
                        else:
                            _, kt, dvc = job
                            vidx = c * 4 + kt
                            kb.wgm(e, f"stv{vidx % 2}", f"a1_vstf_{vidx - 2}")
                            kb.inc(e.tensor_copy(
                                vst[kt % 2][:, dvc * 512:(dvc + 1) * 512], slot[:]), "v")
                            if dvc == 3:
                                kb.mark(f"a1_vrdy_{vidx}", "v")
                        kb.mark(f"a1_mmev_{g}", "v")

            def a1_scalar(e):
                for c in range(8):
                    for t in range(4):
                        bt = 4 * c + t
                        kb.wgm(e, f"xb{bt % 2}", f"a1_x_{bt}")
                        kb.wgm(e, "v", f"a1t{bt - 2}_w1")
                        kb.inc(e.activation(xsq[bt % 2][:], xbuf[bt % 2][:], AF.Square),
                               "a")
                        kb.mark(f"a1t{bt}_xsq", "a")
                        kb.wgm(e, "v", f"a1t{bt}_w1")
                        kb.inc(e.sqrt(stt_t[bt % 2]["r0"][:], stt_t[bt % 2]["w1"][:]), "a")
                        kb.mark(f"a1t{bt}_r0", "a")
                        kb.wgm(e, "v", f"a1t{bt}_ln")
                        kb.wgm(e, "p", f"a1_tr_{(bt - 2) * 4 + 3}")
                        kb.inc(e.activation(xnt[bt % 2][:], xbuf[bt % 2][:], AF.Identity,
                                            bias=stt_t[bt % 2]["nb"][:],
                                            scale=stt_t[bt % 2]["rstd"][:]), "a")
                        kb.mark(f"a1t{bt}_xn", "a")
                    for rc in range(NRKV):
                        kb.wgm(e, "v", f"a1_ykv_{c}_{rc}")
                        if rc >= 2:
                            kb.wgm(e, "p", f"a1_stp_{c}_{rc - 2}")
                        else:
                            kb.wgm(e, "p", f"a1_stp_{c - 1}_{rc + 2}")
                        kb.inc(e.activation(sqkv[rc % 2][:], ykv[:, rc * 512:(rc + 1) * 512],
                                            AF.Square), "a")
                        kb.mark(f"a1_sq_{c}_{rc}", "a")
                    kb.wgm(e, "v", f"a1_rmsrow{c}_w1")
                    kb.inc(e.sqrt(Rr(2), Rr(1)), "a")
                    kb.mark(f"a1_rmsrow{c}_r0", "a")
                    for hc in range(NH):
                        kidx = c * NH + hc
                        kb.wgm(e, "v", f"a1_kev_{kidx}")
                        kb.inc(e.dma_start(
                            out=kscr[hc * 128:(hc + 1) * 128, c * 512:(c + 1) * 512],
                            in_=kst[hc % 2][:]), f"stk{kidx % 2}", 16)
                        kb.mark(f"a1_kstf_{kidx}", f"stk{kidx % 2}")
                    for kt in range(4):
                        vidx = c * 4 + kt
                        kb.wgm(e, "v", f"a1_vrdy_{vidx}")
                        ktg = c * 4 + kt
                        for h in range(NH):
                            kb.inc(e.dma_start(
                                out=vscr[(h * 32 + ktg) * 128:(h * 32 + ktg + 1) * 128, :],
                                in_=vst[kt % 2][:, h * 128:(h + 1) * 128]),
                                f"stv{vidx % 2}", 16)
                        kb.mark(f"a1_vstf_{vidx}", f"stv{vidx % 2}")
                for s_ in ["stk0", "stk1", "stv0", "stv1"]:
                    kb.mark(f"a1_done_{s_}", s_)

            if upto >= 1:
                run_phase([("sync", a1_sync), ("gpsimd", a1_gp), ("tensor", a1_tensor),
                           ("vector", a1_vec), ("scalar", a1_scalar)])

        with ExitStack() as ctx:
            sb = lambda n, sh, d: ctx.enter_context(nc.sbuf_tensor(n, sh, d))
            ps = lambda n, sh, d: ctx.enter_context(nc.psum_tensor(n, sh, d))
            xbuf = [sb(f"a2_xbuf{i}", [128, D], F32) for i in range(2)]
            xsq = [sb(f"a2_xsq{i}", [128, D], F32) for i in range(2)]
            xnt = [sb(f"a2_xnt{i}", [128, D], BF16) for i in range(2)]
            stt_t = [{k: sb(f"a2_st_{k}{i}", [128, 1], F32)
                      for k in ["s1", "s2", "mean", "u", "w1", "r0", "t2", "rstd", "nb"]}
                     for i in range(2)]
            xnstg = [sb(f"a2_xnstg{i}", [128, 512], BF16) for i in range(2)]
            xfstg = [sb(f"a2_xfstg{i}", [128, 512], F32) for i in range(2)]
            ps_tr = [ps(f"a2_pstr{i}", [128, 512], BF16) for i in range(2)]
            ps_tr32 = [ps(f"a2_pstr32{i}", [128, 512], F32) for i in range(2)]

            def a2a_sync(e):
                for bt in range(16):
                    kb.wgm(e, "a", f"a2t{bt - 2}_xn")
                    kb.wgm(e, "p", f"a2_tf_{(bt - 2) * 4 + 3}")
                    sx = f"xb{bt % 2}"
                    kb.inc(e.dma_start(out=xbuf[bt % 2][:],
                                       in_=xown_d[bt * 128:(bt + 1) * 128, :]), sx, 16)
                    kb.mark(f"a2_x_{bt}", sx)

            def a2a_tensor(e):
                for bt in range(16):
                    kb.wgm(e, "a", f"a2t{bt}_xn")
                    for q in range(4):
                        tb = bt * 4 + q
                        kb.wgm(e, "v", f"a2_trev_{tb - 2}")
                        for u in range(4):
                            dc = q * 4 + u
                            mm_ = e.transpose(ps_tr[tb % 2][:, u * 128:(u + 1) * 128],
                                              xnt[bt % 2][:, dc * 128:(dc + 1) * 128],
                                              i16[:])
                        kb.inc(mm_, "p")
                        kb.mark(f"a2_tr_{tb}", "p")
                        kb.wgm(e, "v", f"a2_tfev_{tb - 2}")
                        for u in range(4):
                            dc = q * 4 + u
                            mm_ = e.transpose(ps_tr32[tb % 2][:, u * 128:(u + 1) * 128],
                                              xbuf[bt % 2][:, dc * 128:(dc + 1) * 128],
                                              i32[:])
                        kb.inc(mm_, "p")
                        kb.mark(f"a2_tf_{tb}", "p")

            def a2a_vec(e):
                for bt in range(16):
                    kb.wgm(e, f"xb{bt % 2}", f"a2_x_{bt}")
                    ln_stats_vec(e, xbuf[bt % 2][:], xsq[bt % 2][:], stt_t[bt % 2],
                                 f"a2t{bt}")
                    for q in range(4):
                        tb = bt * 4 + q
                        kb.wgm(e, "p", f"a2_tr_{tb}")
                        kb.wgm(e, f"stn{tb % 2}", f"a2_xnst_{tb - 2}")
                        kb.inc(e.tensor_copy(xnstg[tb % 2][:], ps_tr[tb % 2][:]), "v")
                        kb.mark(f"a2_trev_{tb}", "v")
                        kb.wgm(e, "p", f"a2_tf_{tb}")
                        kb.wgm(e, f"stf{tb % 2}", f"a2_xfst_{tb - 2}")
                        kb.inc(e.tensor_copy(xfstg[tb % 2][:], ps_tr32[tb % 2][:]), "v")
                        kb.mark(f"a2_tfev_{tb}", "v")

            def a2a_scalar(e):
                for bt in range(16):
                    oc, t = bt // 4, bt % 4
                    kb.wgm(e, f"xb{bt % 2}", f"a2_x_{bt}")
                    kb.wgm(e, "v", f"a2t{bt - 2}_w1")
                    kb.inc(e.activation(xsq[bt % 2][:], xbuf[bt % 2][:], AF.Square), "a")
                    kb.mark(f"a2t{bt}_xsq", "a")
                    kb.wgm(e, "v", f"a2t{bt}_w1")
                    kb.inc(e.sqrt(stt_t[bt % 2]["r0"][:], stt_t[bt % 2]["w1"][:]), "a")
                    kb.mark(f"a2t{bt}_r0", "a")
                    kb.wgm(e, "v", f"a2t{bt}_ln")
                    kb.wgm(e, "p", f"a2_tr_{(bt - 2) * 4 + 3}")
                    kb.inc(e.activation(xnt[bt % 2][:], xbuf[bt % 2][:], AF.Identity,
                                        bias=stt_t[bt % 2]["nb"][:],
                                        scale=stt_t[bt % 2]["rstd"][:]), "a")
                    kb.mark(f"a2t{bt}_xn", "a")
                    for q in range(4):
                        tb = bt * 4 + q
                        kb.wgm(e, "v", f"a2_trev_{tb}")
                        dst = dr3(xnscr)
                        kb.inc(e.dma_start(
                            out=dst[:, q * 4:(q + 1) * 4, bt * 128:(bt + 1) * 128],
                            in_=r3(xnstg[tb % 2][:], 4)), f"stn{tb % 2}", 16)
                        kb.mark(f"a2_xnst_{tb}", f"stn{tb % 2}")
                        kb.wgm(e, "v", f"a2_tfev_{tb}")
                        dstf = dr3(xfscr)
                        kb.inc(e.dma_start(
                            out=dstf[:, q * 4:(q + 1) * 4, bt * 128:(bt + 1) * 128],
                            in_=r3(xfstg[tb % 2][:], 4)), f"stf{tb % 2}", 16)
                        kb.mark(f"a2_xfst_{tb}", f"stf{tb % 2}")
                for s_ in ["stn0", "stn1", "stf0", "stf1"]:
                    kb.mark(f"a2_done_{s_}", s_)

            def a2a_gp(e):
                pass

            if upto >= 2:
                run_phase([("sync", a2a_sync), ("gpsimd", a2a_gp), ("tensor", a2a_tensor),
                           ("vector", a2a_vec), ("scalar", a2a_scalar)])

        with ExitStack() as ctx:
            sb = lambda n, sh, d: ctx.enter_context(nc.sbuf_tensor(n, sh, d))
            ps = lambda n, sh, d: ctx.enter_context(nc.psum_tensor(n, sh, d))
            wqd = sb("a2b_wqd", [128, NDC * QR], BF16)
            wqu = sb("a2b_wqu", [128, NRQ * D], BF16)
            wqr = sb("a2b_wqr", [128, NRQ * NH * DR], BF16)
            qnw = sb("a2b_qnw", [128, NRQ], F32)
            csq = [sb(f"a2b_csq{i}", [128, 512], F32) for i in range(2)]
            xnfi = sb("a2b_xnfi", [128, NDC * 512], BF16)
            yq = sb("a2b_yq", [128, NRQ * 512], F32)
            sqq = [sb(f"a2b_sq{i}", [128, 512], F32) for i in range(2)]
            qcn = sb("a2b_qcn", [128, NRQ * 512], BF16)
            rrow = sb("a2b_rrow", [1, 512 * 5], F32)
            Rr = lambda i: rrow[:, i * 512:(i + 1) * 512]
            qst = [sb(f"a2b_qst{i}", [128, 512], BF16) for i in range(2)]
            ropesc = [sb(f"a2b_rsc{i}", [64, 512], F32) for i in range(2)]
            ps_mm = [ps(f"a2b_psmm{i}", [128, 512], F32) for i in range(4)]
            ps_st = ps("a2b_psst", [1, 512], F32)
            ps_bc = ps("a2b_psbc", [128, 512], F32)

            def q_jobs():
                return ([("qc", rc) for rc in range(NRQ)]
                        + [("qn", hc) for hc in range(NH)]
                        + [("qp", pc) for pc in range(NH // 2)])

            JQ = 36

            def a2b_sync(e):
                kb.inc(e.dma_start(out=qnw[:], in_=qnw_d), "qc", 16)
                kb.mark("a2b_c", "qc")
                for oc in range(4):
                    kb.wgm(e, "p", f"a2b_mm_{(oc - 1) * JQ + JQ - 1}")
                    for s_ in ["stn0", "stn1"]:
                        kb.wgm(e, s_, f"a2_done_{s_}")
                    kb.inc(e.dma_start(
                        out=r3(xnfi[:], NDC),
                        in_=dr3(xnscr)[:, :, oc * 512:(oc + 1) * 512]), "xnf", 16)
                    kb.mark(f"a2b_xn_{oc}", "xnf")
                    kb.wgm(e, "v", f"a2b_qpev_{(oc - 2) * 24 + 16 + 7}")
                    sc_ = f"csq{oc % 2}"
                    kb.inc(e.dma_start(out=csq[oc % 2][:],
                                       in_=csq_d[:, oc * 512:(oc + 1) * 512]), sc_, 16)
                    kb.mark(f"a2b_csq_{oc}", sc_)

            def a2b_gp(e):
                kb.inc(e.dma_start(out=r3(wqd[:], NDC), in_=dr3(wqdT_d)), "qw", 16)
                kb.inc(e.dma_start(out=r3(wqu[:], NRQ), in_=dr3(wquT_d)), "qw", 16)
                kb.inc(e.dma_start(out=r3(wqr[:], NRQ), in_=dr3(wqrT_d)), "qw", 16)
                kb.mark("a2b_w", "qw")

            def a2b_tensor(e):
                kb.wgm(e, "qw", "a2b_w")
                for oc in range(4):
                    kb.wgm(e, "xnf", f"a2b_xn_{oc}")
                    for i, job in enumerate(q_jobs()):
                        g = oc * JQ + i
                        slot = ps_mm[g % 4]
                        if job[0] == "qn" and job[1] == 0:
                            kb.wgm(e, "v", f"a2b_rms{oc - 1}_rstd")
                            for rc in range(NRQ):
                                kb.wgm(e, "a", f"a2b_sq_{oc}_{rc}")
                                mm_ = e.matmul(ps_st[:], ones_c[:], sqq[rc % 2][:],
                                               start=(rc == 0), stop=(rc == NRQ - 1))
                                kb.inc(mm_, "p")
                                kb.mark(f"a2b_stp_{oc}_{rc}", "p")
                            kb.mark(f"a2b_stmm_{oc}", "p")
                            kb.wgm(e, "v", f"a2b_rms{oc}_rstd")
                            kb.wgm(e, "v", f"a2b_qcn_{oc - 1}")
                            kb.inc(e.matmul(ps_bc[:], ones_r[:], Rr(4),
                                            start=True, stop=True), "p")
                            kb.mark(f"a2b_bc_{oc}", "p")
                        kb.wgm(e, "v", f"a2b_mmev_{g - 4}")
                        if job[0] == "qc":
                            rc = job[1]
                            for dc in range(NDC):
                                mm_ = e.matmul(slot[:],
                                               wqd[:, dc * QR + rc * 128:dc * QR + (rc + 1) * 128],
                                               xnfi[:, dc * 512:(dc + 1) * 512],
                                               start=(dc == 0), stop=(dc == NDC - 1))
                        elif job[0] == "qn":
                            hc = job[1]
                            if hc == 0:
                                kb.wgm(e, "v", f"a2b_qcn_{oc}")
                            for rc in range(NRQ):
                                mm_ = e.matmul(slot[:],
                                               wqu[:, rc * D + hc * 128:rc * D + (hc + 1) * 128],
                                               qcn[:, rc * 512:(rc + 1) * 512],
                                               start=(rc == 0), stop=(rc == NRQ - 1))
                        else:
                            pc = job[1]
                            for rc in range(NRQ):
                                mm_ = e.matmul(slot[:],
                                               wqr[:, rc * NH * DR + pc * 128:rc * NH * DR + (pc + 1) * 128],
                                               qcn[:, rc * 512:(rc + 1) * 512],
                                               start=(rc == 0), stop=(rc == NRQ - 1))
                        kb.inc(mm_, "p")
                        kb.mark(f"a2b_mm_{g}", "p")

            def a2b_vec(e):
                for oc in range(4):
                    for i, job in enumerate(q_jobs()):
                        g = oc * JQ + i
                        slot = ps_mm[g % 4]
                        kb.wgm(e, "p", f"a2b_mm_{g}")
                        if job[0] == "qc":
                            rc = job[1]
                            kb.inc(e.tensor_copy(yq[:, rc * 512:(rc + 1) * 512], slot[:]),
                                   "v")
                            kb.mark(f"a2b_yq_{oc}_{rc}", "v")
                            if rc == NRQ - 1:
                                kb.wgm(e, "p", f"a2b_stmm_{oc}")
                                rms_row_vec(e, Rr, ps_st, QR, f"a2b_rms{oc}")
                                kb.wgm(e, "p", f"a2b_bc_{oc}")
                                kb.wgm(e, "qc", "a2b_c")
                                kb.selfw(e, "v")
                                for rr in range(NRQ):
                                    kb.inc(e.scalar_tensor_tensor(
                                        qcn[:, rr * 512:(rr + 1) * 512],
                                        yq[:, rr * 512:(rr + 1) * 512],
                                        qnw[:, rr:rr + 1], ps_bc[:],
                                        ALU.mult, ALU.mult), "v")
                                kb.mark(f"a2b_qcn_{oc}", "v")
                        elif job[0] == "qn":
                            hc = job[1]
                            lin = oc * 24 + hc
                            kb.wgm(e, f"stg{lin % 2}", f"a2b_stg_{lin - 2}")
                            kb.inc(e.tensor_copy(qst[lin % 2][:], slot[:]), "v")
                            kb.mark(f"a2b_qnev_{lin}", "v")
                        else:
                            pc = job[1]
                            lin = oc * 24 + 16 + pc
                            kb.wgm(e, f"stg{lin % 2}", f"a2b_stg_{lin - 2}")
                            kb.wgm(e, f"csq{oc % 2}", f"a2b_csq_{oc}")
                            cc = csq[oc % 2][0:32, :]
                            ss = csq[oc % 2][64:96, :]
                            kb.inc(rope_apply(e, qst[lin % 2][:], slot[:], cc, ss,
                                              ropesc[0][:], ropesc[1][:]), "v")
                            kb.mark(f"a2b_qpev_{lin}", "v")
                        kb.mark(f"a2b_mmev_{g}", "v")

            def a2b_scalar(e):
                for oc in range(4):
                    for rc in range(NRQ):
                        kb.wgm(e, "v", f"a2b_yq_{oc}_{rc}")
                        if rc >= 2:
                            kb.wgm(e, "p", f"a2b_stp_{oc}_{rc - 2}")
                        else:
                            kb.wgm(e, "p", f"a2b_stp_{oc - 1}_{rc + 10}")
                        kb.inc(e.activation(sqq[rc % 2][:], yq[:, rc * 512:(rc + 1) * 512],
                                            AF.Square), "a")
                        kb.mark(f"a2b_sq_{oc}_{rc}", "a")
                    kb.wgm(e, "v", f"a2b_rms{oc}_w1")
                    kb.inc(e.sqrt(Rr(2), Rr(1)), "a")
                    kb.mark(f"a2b_rms{oc}_r0", "a")
                    for hc in range(NH):
                        lin = oc * 24 + hc
                        kb.wgm(e, "v", f"a2b_qnev_{lin}")
                        kb.inc(e.dma_start(
                            out=qnscr[hc * 128:(hc + 1) * 128, oc * 512:(oc + 1) * 512],
                            in_=qst[lin % 2][:]), f"stg{lin % 2}", 16)
                        kb.mark(f"a2b_stg_{lin}", f"stg{lin % 2}")
                    for pc in range(NH // 2):
                        lin = oc * 24 + 16 + pc
                        kb.wgm(e, "v", f"a2b_qpev_{lin}")
                        kb.inc(e.dma_start(
                            out=qpscr[pc * 128:(pc + 1) * 128, oc * 512:(oc + 1) * 512],
                            in_=qst[lin % 2][:]), f"stg{lin % 2}", 16)
                        kb.mark(f"a2b_stg_{lin}", f"stg{lin % 2}")
                for s_ in ["stg0", "stg1"]:
                    kb.mark(f"a2b_done_{s_}", s_)

            if upto >= 3:
                run_phase([("sync", a2b_sync), ("gpsimd", a2b_gp), ("tensor", a2b_tensor),
                           ("vector", a2b_vec), ("scalar", a2b_scalar)])

        with ExitStack() as ctx:
            sb = lambda n, sh, d: ctx.enter_context(nc.sbuf_tensor(n, sh, d))
            ps = lambda n, sh, d: ctx.enter_context(nc.psum_tensor(n, sh, d))
            qng = [sb(f"b_qng{i}", [128, NH * 256], BF16) for i in range(2)]
            qpg = [sb(f"b_qpg{i}", [128, (NH // 2) * 256], BF16) for i in range(2)]
            kbuf = [[sb(f"b_kb{i}_{h}", [128, S], BF16) for h in range(2)] for i in range(2)]
            vbuf = [[sb(f"b_vb0_{h}", [128, S], BF16) for h in range(2)]]
            srow = [sb(f"b_srow{i}", [128, S], F32) for i in range(4)]
            mxb = [sb(f"b_mxb{i}", [128, 8], F32) for i in range(4)]
            mcol = [sb(f"b_mcol{i}", [128, 1], F32) for i in range(4)]
            bias = [sb(f"b_bias{i}", [128, 1], F32) for i in range(4)]
            ell = [sb(f"b_ell{i}", [128, 1], F32) for i in range(4)]
            linv = [sb(f"b_linv{i}", [128, 1], F32) for i in range(4)]
            pb = [sb(f"b_pb{i}", [128, S], BF16) for i in range(4)]
            ptb = sb("b_ptb", [128, 2 * S], BF16)
            msk = sb("b_msk", [128, 1024], F32)
            ostg = [sb(f"b_ostg{i}", [128, 256], BF16) for i in range(2)]
            ps_s = [ps(f"b_pss{i}", [128, 512], F32) for i in range(4)]
            ps_pt = [ps(f"b_pspt{i}", [128, 512], BF16) for i in range(2)]
            ps_pv = [ps(f"b_pspv{i}", [128, 256], F32) for i in range(2)]

            def ncount(gi):
                return gi + 1

            def b_sync(e):
                for s_ in ["stg0", "stg1"]:
                    kb.wgm(e, s_, f"a2b_done_{s_}")
                for s_ in ["stk0", "stk1", "stv0", "stv1"]:
                    kb.wgm(e, s_, f"a1_done_{s_}")
                for gi in range(8):
                    kb.wgm(e, "p", f"b_smmd_{(gi - 2) * 8 + 7}_1")
                    sq_ = f"q{gi % 2}"
                    kb.inc(e.dma_start(
                        out=r3(qng[gi % 2][:], NH),
                        in_=dr3(qnscr)[:, :, gi * 256:(gi + 1) * 256]), sq_, 16)
                    kb.inc(e.dma_start(
                        out=r3(qpg[gi % 2][:], NH // 2),
                        in_=dr3(qpscr)[:, :, gi * 256:(gi + 1) * 256]), sq_, 16)
                    kb.mark(f"b_q_{gi}", sq_)
                    kb.wgm(e, "v", f"b_maskd_{gi - 1}")
                    kb.inc(e.dma_start(
                        out=r3(msk[:], 2),
                        in_=dr3(mask_d)[:, 2 * gi:2 * gi + 2, :]), "qm", 16)
                    kb.mark(f"b_m_{gi}", "qm")
                    nc_ = ncount(gi)
                    for hp in range(8):
                        gih = gi * 8 + hp
                        for hh in range(2):
                            h = 2 * hp + hh
                            sk_ = f"k{hp % 2}{hh}"
                            kb.wgm(e, "p", f"b_smmd_{gih - 2}_{hh}")
                            kb.inc(e.dma_start(
                                out=kbuf[hp % 2][hh][:, 0:nc_ * 512],
                                in_=kscr[h * 128:(h + 1) * 128, 0:nc_ * 512]), sk_, 16)
                            kb.mark(f"b_k_{gih}_{hh}", sk_)
                        for hh in range(2):
                            h = 2 * hp + hh
                            sv_ = f"v0{hh}"
                            kb.wgm(e, "p", f"b_pv_{gih - 1}_{hh}")
                            kb.inc(e.dma_start(
                                out=r3(vbuf[0][hh][:, 0:nc_ * 4 * 128], nc_ * 4),
                                in_=dr3(vscr[h * 32 * 128:(h * 32 + nc_ * 4) * 128, :])),
                                sv_, 16)
                            kb.mark(f"b_v_{gih}_{hh}", sv_)

            def b_tensor(e):
                for gi in range(8):
                    nc_ = ncount(gi)
                    for hp in range(8):
                        gih = gi * 8 + hp
                        kb.wgm(e, f"q{gi % 2}", f"b_q_{gi}")
                        for hh in range(2):
                            kb.wgm(e, f"k{hp % 2}{hh}", f"b_k_{gih}_{hh}")
                        for c in range(nc_):
                            for s_ in range(2):
                                prev = (f"b_sev_{gih}_{c - 1}_{s_ * 2 + 1}" if c > 0 else
                                        f"b_sev_{gih - 1}_{ncount((gih - 1) // 8) - 1}_3"
                                        if gih > 0 else None)
                                if prev:
                                    kb.wgm(e, "v", prev)
                                qe = qpg[gi % 2]
                                base = hp * 256 + s_ * 128
                                kv_c = slice(c * 512, (c + 1) * 512)
                                for hh in range(2):
                                    h = 2 * hp + hh
                                    qn_t = qng[gi % 2][:, h * 256 + s_ * 128:h * 256 + (s_ + 1) * 128]
                                    e.matmul(ps_s[s_ * 2 + hh][:], qn_t,
                                             kbuf[hp % 2][hh][:, kv_c],
                                             start=True, stop=False,
                                             skip_group_check=True)
                                e.matmul(ps_s[s_ * 2][:], qe[0:64, base:base + 128],
                                         kro[0:64, kv_c], start=False, stop=True,
                                         skip_group_check=True)
                                mm_ = e.matmul(ps_s[s_ * 2 + 1][:],
                                               qe[64:128, base:base + 128],
                                               kro[64:128, kv_c], start=False, stop=True,
                                               skip_group_check=True)
                                kb.inc(mm_, "p")
                                kb.mark(f"b_smm_{gih}_{c}_{s_ * 2}", "p")
                                kb.mark(f"b_smm_{gih}_{c}_{s_ * 2 + 1}", "p")
                        kb.mark(f"b_smmd_{gih}_0", "p")
                        kb.mark(f"b_smmd_{gih}_1", "p")
                        combo_order = [0, 2, 1, 3]
                        for ci_, combo in enumerate(combo_order):
                            s_, hh = combo // 2, combo % 2
                            if True:
                                kb.wgm(e, "v", f"b_pn_{gih}_{combo}")
                                for bq in range(nc_):
                                    if bq < 2:
                                        pc_ = (combo_order[ci_ - 1] if ci_ > 0 else 3)
                                        pg_ = gih if ci_ > 0 else gih - 1
                                        pn_ = ncount(pg_ // 8) if pg_ >= 0 else 1
                                        kb.wgm(e, "v", f"b_ptev_{pg_}_{pc_}_{pn_ - 1}")
                                    else:
                                        kb.wgm(e, "v", f"b_ptev_{gih}_{combo}_{bq - 2}")
                                    for u in range(4):
                                        kc = bq * 4 + u
                                        mm_ = e.transpose(
                                            ps_pt[bq % 2][:, u * 128:(u + 1) * 128],
                                            pb[combo][:, kc * 128:(kc + 1) * 128], i16[:])
                                    kb.inc(mm_, "p")
                                    kb.mark(f"b_tr_{gih}_{combo}_{bq}", "p")
                                kb.mark(f"b_trd_{gih}_{combo}", "p")
                            if s_ == 1:
                                kb.wgm(e, "v", f"b_ptev_{gih}_{2 + hh}_{nc_ - 1}")
                                kb.wgm(e, "v", f"b_ptev_{gih}_{0 + hh}_{nc_ - 1}")
                                kb.wgm(e, f"v0{hh}", f"b_v_{gih}_{hh}")
                                kb.wgm(e, "v", f"b_oev_{gih - 1}_{hh}")
                                ptr = ptb[:].rearrange("p (s c q) -> p c s q",
                                                       s=2, c=32, q=128)
                                for kc in range(nc_ * 4):
                                    mm_ = e.matmul(ps_pv[hh][:],
                                                   vbuf[0][hh][:, kc * 128:(kc + 1) * 128],
                                                   ptr[:, kc],
                                                   start=(kc == 0),
                                                   stop=(kc == nc_ * 4 - 1))
                                kb.inc(mm_, "p")
                                kb.mark(f"b_pv_{gih}_{hh}", "p")

            def b_vec(e):
                for gi in range(8):
                    nc_ = ncount(gi)
                    for hp in range(8):
                        gih = gi * 8 + hp
                        for c in range(nc_):
                            for combo in range(4):
                                kb.wgm(e, "p", f"b_smm_{gih}_{c}_{combo}")
                                if c == 0:
                                    kb.wgm(e, "a", f"b_exp_{gih - 1}_{combo}")
                                kb.inc(e.tensor_scalar(
                                    srow[combo][:, c * 512:(c + 1) * 512],
                                    ps_s[combo][:], SCALE, None, ALU.mult, ALU.max,
                                    accum_out=mxb[combo][:, c:c + 1]), "v")
                                kb.mark(f"b_sev_{gih}_{c}_{combo}", "v")
                        for combo in range(4):
                            s_ = combo // 2
                            kb.wgm(e, "qm", f"b_m_{gi}")
                            kb.selfw(e, "v")
                            ch(e, e.tensor_add(
                                srow[combo][:, (nc_ - 1) * 512:nc_ * 512],
                                srow[combo][:, (nc_ - 1) * 512:nc_ * 512],
                                msk[:, s_ * 512:(s_ + 1) * 512]))
                            ch(e, e.reduce_max(mcol[combo][:], mxb[combo][:, 0:nc_],
                                               AX.X))
                            kb.selfw(e, "v")
                            kb.inc(e.tensor_scalar(bias[combo][:], mcol[combo][:], -1.0,
                                                   None, ALU.mult, ALU.bypass), "v")
                            kb.mark(f"b_bias_{gih}_{combo}", "v")
                        if hp == 7:
                            kb.mark(f"b_maskd_{gi}", "v")
                        combo_order = [0, 2, 1, 3]
                        for ci_, combo in enumerate(combo_order):
                            s_, hh = combo // 2, combo % 2
                            kb.wgm(e, "a", f"b_exp_{gih}_{combo}")
                            kb.inc(e.reciprocal(linv[combo][:], ell[combo][:]), "v")
                            kb.mark(f"b_linv_{gih}_{combo}", "v")
                            kb.selfw(e, "v")
                            kb.inc(e.tensor_scalar(pb[combo][:, 0:nc_ * 512],
                                                   pb[combo][:, 0:nc_ * 512],
                                                   linv[combo][:], None,
                                                   ALU.mult, ALU.bypass), "v")
                            kb.mark(f"b_pn_{gih}_{combo}", "v")
                            for bq in range(nc_):
                                kb.wgm(e, "p", f"b_tr_{gih}_{combo}_{bq}")
                                if bq == 0 and s_ == 0:
                                    kb.wgm(e, "p", (f"b_pv_{gih - 1}_1" if hh == 0
                                                    else f"b_pv_{gih}_0"))
                                kb.inc(e.tensor_copy(
                                    ptb[:, s_ * S + bq * 512:s_ * S + (bq + 1) * 512],
                                    ps_pt[bq % 2][:]), "v")
                                kb.mark(f"b_ptev_{gih}_{combo}_{bq}", "v")
                            if s_ == 1:
                                kb.wgm(e, "p", f"b_pv_{gih}_{hh}")
                                kb.wgm(e, f"sto{hh}", f"b_ost_{gih - 1}_{hh}")
                                kb.inc(e.tensor_copy(ostg[hh][:], ps_pv[hh][:]), "v")
                                kb.mark(f"b_oev_{gih}_{hh}", "v")

            def b_scalar(e):
                for gi in range(8):
                    nc_ = ncount(gi)
                    for hp in range(8):
                        gih = gi * 8 + hp
                        for combo in [0, 2, 1, 3]:
                            s_, hh = combo // 2, combo % 2
                            kb.wgm(e, "v", f"b_bias_{gih}_{combo}")
                            kb.wgm(e, "p", f"b_trd_{gih - 1}_{combo}")
                            kb.inc(e.activation(pb[combo][:, 0:nc_ * 512],
                                                srow[combo][:, 0:nc_ * 512], AF.Exp,
                                                bias=bias[combo][:], scale=1.0,
                                                accum_out=ell[combo][:]), "a")
                            kb.mark(f"b_exp_{gih}_{combo}", "a")
                        for hh in range(2):
                            h = 2 * hp + hh
                            kb.wgm(e, "v", f"b_oev_{gih}_{hh}")
                            kb.inc(e.dma_start(
                                out=oscr[h * 128:(h + 1) * 128, gi * 256:(gi + 1) * 256],
                                in_=ostg[hh][:]), f"sto{hh}", 16)
                            kb.mark(f"b_ost_{gih}_{hh}", f"sto{hh}")
                for s_ in ["sto0", "sto1"]:
                    kb.mark(f"b_done_{s_}", s_)

            def b_gp(e):
                kb.inc(e.dma_start(out=wo16, in_=woT_d), "w16a", 16)
                kb.mark("a1_wo16", "w16a")
                kb.inc(e.dma_start(out=g16, in_=gateT_d), "w16b", 16)
                kb.inc(e.dma_start(out=u16, in_=upT_d), "w16b", 16)
                kb.inc(e.dma_start(out=d16, in_=downT_d), "w16b", 16)
                kb.mark("a1_ffn16", "w16b")

            if upto >= 4:
                run_phase([("sync", b_sync), ("gpsimd", b_gp), ("tensor", b_tensor),
                           ("vector", b_vec), ("scalar", b_scalar)])

        with ExitStack() as ctx:
            sb = lambda n, sh, d: ctx.enter_context(nc.sbuf_tensor(n, sh, d))
            ps = lambda n, sh, d: ctx.enter_context(nc.psum_tensor(n, sh, d))
            wo = sb("b2_wo", [128, NH * D], BF16)
            ob = [sb(f"b2_ob{i}", [128, NH * 512], BF16) for i in range(2)]
            sstg = [sb(f"b2_sstg{i}", [128, 512], F32) for i in range(2)]
            ps_wo = [ps(f"b2_pswo{i}", [128, 512], F32) for i in range(2)]

            def b2_sync(e):
                for s_ in ["sto0", "sto1"]:
                    kb.wgm(e, s_, f"b_done_{s_}")
                for tc in range(4):
                    kb.wgm(e, "p", f"b2_mmd_{tc - 2}")
                    so_ = f"ob{tc % 2}"
                    kb.inc(e.dma_start(
                        out=r3(ob[tc % 2][:], NH),
                        in_=dr3(oscr)[:, :, tc * 512:(tc + 1) * 512]), so_, 16)
                    kb.mark(f"b2_o_{tc}", so_)

            def b2_gp(e):
                kb.wgm(e, "w16a", "a1_wo16")
                kb.inc(e.dma_start(out=r3(wo[:], NH), in_=dr3(wo16)), "qw", 16)
                kb.mark("b2_w", "qw")

            def b2_tensor(e):
                kb.wgm(e, "qw", "b2_w")
                for tc in range(4):
                    kb.wgm(e, f"ob{tc % 2}", f"b2_o_{tc}")
                    for do in range(NDC):
                        g = tc * NDC + do
                        kb.wgm(e, "v", f"b2_sev_{g - 2}")
                        for hd in range(NH):
                            mm_ = e.matmul(ps_wo[g % 2][:],
                                           wo[:, hd * D + do * 128:hd * D + (do + 1) * 128],
                                           ob[tc % 2][:, hd * 512:(hd + 1) * 512],
                                           start=(hd == 0), stop=(hd == NH - 1))
                        kb.inc(mm_, "p")
                        kb.mark(f"b2_mm_{g}", "p")
                    kb.mark(f"b2_mmd_{tc}", "p")

            def b2_vec(e):
                for tc in range(4):
                    for do in range(NDC):
                        g = tc * NDC + do
                        kb.wgm(e, "p", f"b2_mm_{g}")
                        kb.wgm(e, f"sts{g % 2}", f"b2_sst_{g - 2}")
                        kb.inc(e.tensor_copy(sstg[g % 2][:], ps_wo[g % 2][:]), "v")
                        kb.mark(f"b2_sev_{g}", "v")

            def b2_scalar(e):
                for tc in range(4):
                    for do in range(NDC):
                        g = tc * NDC + do
                        kb.wgm(e, "v", f"b2_sev_{g}")
                        kb.inc(e.dma_start(
                            out=sscr[do * 128:(do + 1) * 128, tc * 512:(tc + 1) * 512],
                            in_=sstg[g % 2][:]), f"sts{g % 2}", 16)
                        kb.mark(f"b2_sst_{g}", f"sts{g % 2}")
                for s_ in ["sts0", "sts1"]:
                    kb.mark(f"b2_done_{s_}", s_)

            if upto >= 5:
                run_phase([("sync", b2_sync), ("gpsimd", b2_gp), ("tensor", b2_tensor),
                           ("vector", b2_vec), ("scalar", b2_scalar)])

        with ExitStack() as ctx:
            sb = lambda n, sh, d: ctx.enter_context(nc.sbuf_tensor(n, sh, d))
            ps = lambda n, sh, d: ctx.enter_context(nc.psum_tensor(n, sh, d))
            xn = sb("c_xn", [128, NDC * 1024], BF16)
            hdn = sb("c_hdn", [128, NIC * 1024], BF16)
            gw = [sb(f"c_gw{i}", [128, NDC * 128], BF16) for i in range(2)]
            uw = [sb(f"c_uw{i}", [128, NDC * 128], BF16) for i in range(2)]
            dw = [sb(f"c_dw{i}", [128, NIC * 128], BF16) for i in range(2)]
            gsb = sb("c_gs", [128, NIC], F32)
            usb = sb("c_us", [128, NIC], F32)
            dsb = sb("c_ds", [128, NDC], F32)
            sg = [sb(f"c_sg{i}", [128, 512], F32) for i in range(2)]
            su = [sb(f"c_su{i}", [128, 512], F32) for i in range(2)]
            tsil = [sb(f"c_tsil{i}", [128, 512], F32) for i in range(2)]
            ssb = [sb(f"c_ssb{i}", [128, 1024], F32) for i in range(2)]
            xfb = [sb(f"c_xfb{i}", [128, 1024], F32) for i in range(2)]
            ee = [sb(f"c_ee{i}", [128, 512], F32) for i in range(2)]
            ostg = [sb(f"c_ostg{i}", [128, 512], F32) for i in range(2)]
            ps_g = [ps(f"c_psg{i}", [128, 512], F32) for i in range(2)]
            ps_u = [ps(f"c_psu{i}", [128, 512], F32) for i in range(2)]
            ps_d = [ps(f"c_psd{i}", [128, 512], F32) for i in range(2)]
            ps_t = [ps(f"c_pst{i}", [128, 512], F32) for i in range(2)]

            def c_sync(e):
                kb.inc(e.dma_start(out=gsb[:], in_=gs_d), "qc", 16)
                kb.inc(e.dma_start(out=usb[:], in_=us_d), "qc", 16)
                kb.inc(e.dma_start(out=dsb[:], in_=ds_d), "qc", 16)
                kb.mark("c_sc", "qc")
                for s_ in ["stn0", "stn1", "stf0", "stf1"]:
                    kb.wgm(e, s_, f"a2_done_{s_}")
                for s_ in ["sts0", "sts1"]:
                    kb.wgm(e, s_, f"b2_done_{s_}")
                for ft in range(2):
                    kb.wgm(e, "p", f"c_gumm_{(ft - 1) * NIC + NIC - 1}_1")
                    kb.inc(e.dma_start(
                        out=r3(xn[:], NDC),
                        in_=dr3(xnscr)[:, :, ft * 1024:(ft + 1) * 1024]), "cxn", 16)
                    kb.mark(f"c_xn_{ft}", "cxn")
                    for do in range(NDC):
                        g = ft * NDC + do
                        sx_ = f"sx{do % 2}"
                        kb.wgm(e, "v", f"c_ef_{g - 2}_1")
                        kb.inc(e.dma_start(
                            out=ssb[do % 2][:],
                            in_=sscr[do * 128:(do + 1) * 128, ft * 1024:(ft + 1) * 1024]),
                            sx_, 16)
                        kb.inc(e.dma_start(
                            out=xfb[do % 2][:],
                            in_=xfscr[do * 128:(do + 1) * 128, ft * 1024:(ft + 1) * 1024]),
                            sx_, 16)
                        kb.mark(f"c_sx_{g}", sx_)

            def c_gp(e):
                for ft in range(2):
                    for i in range(NIC):
                        g = ft * NIC + i
                        sg_ = f"gu{i % 2}"
                        kb.wgm(e, "w16b", "a1_ffn16")
                        kb.wgm(e, "p", f"c_gumm_{g - 2}_1")
                        kb.inc(e.dma_start(
                            out=r3(gw[i % 2][:], NDC),
                            in_=dr3(g16)[:, :, i * 128:(i + 1) * 128]), sg_, 16)
                        kb.inc(e.dma_start(
                            out=r3(uw[i % 2][:], NDC),
                            in_=dr3(u16)[:, :, i * 128:(i + 1) * 128]), sg_, 16)
                        kb.mark(f"c_guw_{g}", sg_)
                    for do in range(NDC):
                        g = ft * NDC + do
                        sd_ = f"dw{do % 2}"
                        kb.wgm(e, "w16b", "a1_ffn16")
                        kb.wgm(e, "p", f"c_dmm_{g - 2}_1")
                        kb.inc(e.dma_start(
                            out=r3(dw[do % 2][:], NIC),
                            in_=dr3(d16)[:, :, do * 128:(do + 1) * 128]), sd_, 16)
                        kb.mark(f"c_dw_{g}", sd_)

            def c_tensor(e):
                for ft in range(2):
                    kb.wgm(e, "cxn", f"c_xn_{ft}")
                    for i in range(NIC):
                        g = ft * NIC + i
                        kb.wgm(e, f"gu{i % 2}", f"c_guw_{g}")
                        for half in range(2):
                            kb.wgm(e, "a", f"c_gu_{g - 1}_{half}")
                            for dc in range(NDC):
                                e.matmul(ps_g[half][:],
                                         gw[i % 2][:, dc * 128:(dc + 1) * 128],
                                         xn[:, dc * 1024 + half * 512:dc * 1024 + (half + 1) * 512],
                                         start=(dc == 0), stop=(dc == NDC - 1))
                            for dc in range(NDC):
                                mm_ = e.matmul(ps_u[half][:],
                                               uw[i % 2][:, dc * 128:(dc + 1) * 128],
                                               xn[:, dc * 1024 + half * 512:dc * 1024 + (half + 1) * 512],
                                               start=(dc == 0), stop=(dc == NDC - 1))
                            kb.inc(mm_, "p")
                            kb.mark(f"c_gumm_{g}_{half}", "p")
                    for do in range(NDC):
                        g = ft * NDC + do
                        kb.wgm(e, f"dw{do % 2}", f"c_dw_{g}")
                        kb.wgm(e, "v", f"c_hdn_{ft}")
                        for half in range(2):
                            kb.wgm(e, "a", f"c_sil_{g - 1}_{half}")
                            for ic in range(NIC):
                                mm_ = e.matmul(ps_d[half][:],
                                               dw[do % 2][:, ic * 128:(ic + 1) * 128],
                                               hdn[:, ic * 1024 + half * 512:ic * 1024 + (half + 1) * 512],
                                               start=(ic == 0), stop=(ic == NIC - 1))
                            kb.inc(mm_, "p")
                            kb.mark(f"c_dmm_{g}_{half}", "p")
                        for half in range(2):
                            kb.wgm(e, "v", f"c_ef_{g}_{half}")
                            kb.wgm(e, "v", f"c_ep_{g - 1}_{half}_3")
                            for u in range(4):
                                mm_ = e.transpose(ps_t[half][:, u * 128:(u + 1) * 128],
                                                  ee[half][:, u * 128:(u + 1) * 128],
                                                  i32[:])
                            kb.inc(mm_, "p")
                            kb.mark(f"c_tr_{g}_{half}", "p")

            def c_vec(e):
                for ft in range(2):
                    for i in range(NIC):
                        g = ft * NIC + i
                        for half in range(2):
                            kb.wgm(e, "a", f"c_gu_{g}_{half}")
                            kb.wgm(e, "p", f"c_dmm_{(ft - 1) * NDC + NDC - 1}_1")
                            kb.inc(e.tensor_mul(
                                hdn[:, i * 1024 + half * 512:i * 1024 + (half + 1) * 512],
                                sg[half][:], su[half][:]), "v")
                            kb.mark(f"c_hd_{g}_{half}", "v")
                    kb.mark(f"c_hdn_{ft}", "v")
                    for do in range(NDC):
                        g = ft * NDC + do
                        for half in range(2):
                            kb.wgm(e, "a", f"c_sil_{g}_{half}")
                            kb.wgm(e, f"sx{do % 2}", f"c_sx_{g}")
                            kb.wgm(e, "p", f"c_tr_{g - 1}_{half}")
                            ch(e, e.tensor_mul(ee[half][:],
                                               ssb[do % 2][:, half * 512:(half + 1) * 512],
                                               tsil[half][:]))
                            kb.selfw(e, "v")
                            kb.inc(e.tensor_add(ee[half][:], ee[half][:],
                                                xfb[do % 2][:, half * 512:(half + 1) * 512]),
                                   "v")
                            kb.mark(f"c_ef_{g}_{half}", "v")
                        for half in range(2):
                            kb.wgm(e, "p", f"c_tr_{g}_{half}")
                            kb.wgm(e, f"out{half}", f"c_ost_{g - 1}_{half}_3")
                            for j in range(4):
                                kb.inc(e.tensor_copy(
                                    ostg[half][:, j * 128:(j + 1) * 128],
                                    ps_t[half][:, j * 128:(j + 1) * 128]), "v")
                                kb.mark(f"c_ep_{g}_{half}_{j}", "v")

            def c_scalar(e):
                kb.wgm(e, "qc", "c_sc")
                for ft in range(2):
                    for i in range(NIC):
                        g = ft * NIC + i
                        for half in range(2):
                            kb.wgm(e, "p", f"c_gumm_{g}_{half}")
                            kb.wgm(e, "v", f"c_hd_{g - 1}_{half}")
                            e.activation(sg[half][:], ps_g[half][:], AF.Silu,
                                         bias=0.0, scale=gsb[:, i:i + 1])
                            kb.inc(e.activation(su[half][:], ps_u[half][:], AF.Copy,
                                                bias=0.0, scale=usb[:, i:i + 1]), "a")
                            kb.mark(f"c_gu_{g}_{half}", "a")
                    for do in range(NDC):
                        g = ft * NDC + do
                        for half in range(2):
                            kb.wgm(e, "p", f"c_dmm_{g}_{half}")
                            kb.wgm(e, "v", f"c_ef_{g - 1}_{half}")
                            kb.inc(e.activation(tsil[half][:], ps_d[half][:], AF.Silu,
                                                bias=0.0, scale=dsb[:, do:do + 1]), "a")
                            kb.mark(f"c_sil_{g}_{half}", "a")
                        for half in range(2):
                            for j in range(4):
                                kb.wgm(e, "v", f"c_ep_{g}_{half}_{j}")
                                row = ft * 1024 + half * 512 + j * 128
                                kb.inc(e.dma_start(
                                    out=out_d[row:row + 128, do * 128:(do + 1) * 128],
                                    in_=ostg[half][:, j * 128:(j + 1) * 128]),
                                    f"out{half}", 16)
                                kb.mark(f"c_ost_{g}_{half}_{j}", f"out{half}")
                for s_ in ["out0", "out1"]:
                    kb.mark(f"c_done_{s_}", s_)

            if upto >= 6:
                run_phase([("sync", c_sync), ("gpsimd", c_gp), ("tensor", c_tensor),
                           ("vector", c_vec), ("scalar", c_scalar)])

        def fin_sync(e):
            for s_ in ["out0", "out1"]:
                kb.wgm(e, s_, f"c_done_{s_}")

        run_phase([("sync", fin_sync)])

    return nc, kb


_BUILD_CACHE = {}


def kernel(**inputs):
    maps, metas = _prep(inputs)
    if "nc" not in _BUILD_CACHE:
        _BUILD_CACHE["nc"] = build(debug=False)[0]
    nc = _BUILD_CACHE["nc"]
    res = run_bass_kernel_spmd(nc, maps, core_ids=list(range(8)))
    y = np.empty((B_, S, D), dtype=np.float32)
    for c in range(8):
        y[c // 2][metas[c]] = res.results[c]["out"]
    return y


# version 13
# speedup vs baseline: 1.0438x; 1.0438x over previous
import math
import numpy as np
import ml_dtypes
from contextlib import ExitStack

import concourse.bass as bass
import concourse.mybir as mybir
from concourse.bass_utils import run_bass_kernel_spmd

F32 = mybir.dt.float32
BF16 = mybir.dt.bfloat16
AF = mybir.ActivationFunctionType
ALU = mybir.AluOpType
AX = mybir.AxisListType
BF = ml_dtypes.bfloat16

B_, S, D = 4, 4096, 2048
NH, DN, DR, DV = 16, 128, 64, 128
QR, KVR, INTER = 1536, 512, 5632
NDC = D // 128
NRQ = QR // 128
NRKV = KVR // 128
NIC = INTER // 128
SCALE = 1.0 / math.sqrt(DN + DR)
EPS_LN, EPS_RMS = 1e-5, 1e-6
NEG = -1.0e30
THETA = 10000.0
OWN = S // 2



def _own_blocks(par):
    return [j for j in range(32) if ((j // 4 + j % 4) % 2) == par]


def _rope_perm_q():
    perm = []
    for c in range(NH // 2):
        for h in (2 * c, 2 * c + 1):
            perm += [h * DR + 2 * i for i in range(32)]
            perm += [h * DR + 2 * i + 1 for i in range(32)]
    return perm


def _rope_perm_k():
    ev = [2 * i for i in range(32)]
    od = [2 * i + 1 for i in range(32)]
    return ev + od + ev + od


def _cs_table(positions):
    invf = THETA ** (-np.arange(0, DR, 2, dtype=np.float64) / DR)
    t = np.asarray(positions, dtype=np.float64)
    ang = t[None, :] * invf[:, None]
    c = np.cos(ang).astype(np.float32)
    s = np.sin(ang).astype(np.float32)
    return np.ascontiguousarray(np.concatenate([c, c, s, s], axis=0))


def _prep(inputs):
    a = lambda x: np.ascontiguousarray(np.asarray(x, dtype=np.float32))
    w = {}
    w["wqdT"] = a(np.asarray(inputs["L0_attn_wq_down"]).T)
    w["wquT"] = a(np.asarray(inputs["L0_attn_wq_up"]).T)
    w["wqrT"] = a(np.asarray(inputs["L0_attn_wq_rope"])[_rope_perm_q()].T)
    w["wkvdT"] = a(np.asarray(inputs["L0_attn_wkv_down"]).T)
    kvu = np.asarray(inputs["L0_attn_wkv_up"])
    k_rows = [h * (DN + DV) + d for h in range(NH) for d in range(DN)]
    v_rows = [h * (DN + DV) + DN + d for h in range(NH) for d in range(DV)]
    w["wkvukT"] = a(kvu[k_rows].T)
    w["wkvuvT"] = a(kvu[v_rows].T)
    w["wkrT"] = a(np.asarray(inputs["L0_attn_wk_rope"])[_rope_perm_k()].T)
    w["woT"] = a(np.asarray(inputs["L0_attn_wo"]).T)
    w["gateT"] = a(np.asarray(inputs["L0_ffn_gate_w"]).T)
    w["upT"] = a(np.asarray(inputs["L0_ffn_up_w"]).T)
    w["downT"] = a(np.asarray(inputs["L0_ffn_down_w"]).T)
    w["qnw"] = a(np.asarray(inputs["L0_attn_q_norm_w"]).reshape(NRQ, 128).T)
    w["kvnw"] = a(np.asarray(inputs["L0_attn_kv_norm_w"]).reshape(NRKV, 128).T)
    w["gs"] = a(np.asarray(inputs["L0_ffn_gate_s"]).reshape(NIC, 128).T)
    w["us"] = a(np.asarray(inputs["L0_ffn_up_s"]).reshape(NIC, 128).T)
    w["ds"] = a(np.asarray(inputs["L0_ffn_down_s"]).reshape(NDC, 128).T)
    w["cs_tab"] = _cs_table(np.arange(S))
    w["ident16"] = np.eye(128, dtype=BF)
    w["ident32"] = np.eye(128, dtype=np.float32)
    x = np.asarray(inputs["L0_attn_x"], dtype=np.float32)
    maps, metas = [], []
    for c in range(8):
        b, par = c // 2, c % 2
        blocks = _own_blocks(par)
        rows = np.concatenate([np.arange(j * 128, (j + 1) * 128) for j in blocks])
        m = dict(w)
        m["xb"] = np.ascontiguousarray(x[b])
        m["x_own"] = np.ascontiguousarray(x[b][rows])
        m["cs_q"] = _cs_table(rows)
        mask = np.empty((16 * 128, 512), dtype=np.float32)
        for si, j in enumerate(blocks):
            nc_ = j // 4 + 1
            kcol = (nc_ - 1) * 512 + np.arange(512)[None, :]
            qrow = j * 128 + np.arange(128)[:, None]
            mask[si * 128:(si + 1) * 128] = np.where(kcol <= qrow, 0.0, NEG)
        m["masks"] = mask
        maps.append(m)
        metas.append(rows)
    return maps, metas



class KB:
    SEMS = (["ph", "p", "v", "a", "gq", "qc", "qw",
             "xb0", "xb1", "stk0", "stk1", "stv0", "stv1",
             "stn0", "stn1", "stf0", "stf1",
             "xnf", "csq0", "csq1", "stg0", "stg1",
             "k00", "k01", "k10", "k11", "v00", "v01", "v10", "v11",
             "q0", "q1", "qm", "sto0", "sto1",
             "ob0", "ob1", "sts0", "sts1",
             "cxn", "sx0", "sx1", "gu0", "gu1", "dw0", "dw1", "w16a", "w16b",
             "out0", "out1"])

    def __init__(self):
        self.dry = True
        self.n = {s: 0 for s in self.SEMS}
        self.ms = {}
        self.sh = {}
        self.lastwait = {}

    def inc(self, instr, sem, k=1):
        self.n[sem] += k
        if not self.dry:
            instr.then_inc(self.sh[sem], k)

    def mark(self, key, sem):
        if self.dry:
            self.ms[key] = self.n[sem]

    def wg(self, eng, sem, val):
        if self.dry or val is None or val <= 0:
            return
        k = (id(eng), sem)
        if self.lastwait.get(k, -1) >= val:
            return
        self.lastwait[k] = val
        eng.wait_ge(self.sh[sem], val)

    def wgm(self, eng, sem, key):
        if self.dry:
            return
        self.wg(eng, sem, self.ms.get(key))

    def selfw(self, eng, sem):
        self.wg(eng, sem, self.n[sem])


class _DryInstr:
    def then_inc(self, *a, **k):
        return self


class DryEng:
    def __getattr__(self, name):
        return lambda *a, **k: _DryInstr()


def r3(ap2d, nchunks):
    return ap2d.rearrange("p (a f) -> p a f", a=nchunks)


def dr3(dram_ap, p=128):
    return dram_ap.rearrange("(a p) f -> p a f", p=p)



def build(debug=False, upto=99):
    nc = bass.Bass("TRN2", target_bir_lowering=False)
    dt = lambda name, shape, dtype, kind: nc.dram_tensor(name, shape, dtype, kind=kind).ap()

    xb_d = dt("xb", [S, D], F32, "ExternalInput")
    xown_d = dt("x_own", [OWN, D], F32, "ExternalInput")
    csq_d = dt("cs_q", [128, OWN], F32, "ExternalInput")
    cst_d = dt("cs_tab", [128, S], F32, "ExternalInput")
    mask_d = dt("masks", [16 * 128, 512], F32, "ExternalInput")
    wqdT_d = dt("wqdT", [D, QR], F32, "ExternalInput")
    wquT_d = dt("wquT", [QR, D], F32, "ExternalInput")
    wqrT_d = dt("wqrT", [QR, NH * DR], F32, "ExternalInput")
    wkvdT_d = dt("wkvdT", [D, KVR], F32, "ExternalInput")
    wkvukT_d = dt("wkvukT", [KVR, NH * DN], F32, "ExternalInput")
    wkvuvT_d = dt("wkvuvT", [KVR, NH * DV], F32, "ExternalInput")
    wkrT_d = dt("wkrT", [D, 128], F32, "ExternalInput")
    woT_d = dt("woT", [D, D], F32, "ExternalInput")
    gateT_d = dt("gateT", [D, INTER], F32, "ExternalInput")
    upT_d = dt("upT", [D, INTER], F32, "ExternalInput")
    downT_d = dt("downT", [INTER, D], F32, "ExternalInput")
    qnw_d = dt("qnw", [128, NRQ], F32, "ExternalInput")
    kvnw_d = dt("kvnw", [128, NRKV], F32, "ExternalInput")
    gs_d = dt("gs", [128, NIC], F32, "ExternalInput")
    us_d = dt("us", [128, NIC], F32, "ExternalInput")
    ds_d = dt("ds", [128, NDC], F32, "ExternalInput")
    i16_d = dt("ident16", [128, 128], BF16, "ExternalInput")
    i32_d = dt("ident32", [128, 128], F32, "ExternalInput")
    out_d = dt("out", [OWN, D], F32, "ExternalOutput")

    sk = "ExternalOutput" if debug else "Internal"
    kscr = dt("kscr", [NH * DN, S], BF16, sk)
    vscr = dt("vscr", [NH * 32 * 128, 128], BF16, sk)
    qnscr = dt("qnscr", [NH * DN, OWN], BF16, sk)
    qpscr = dt("qpscr", [(NH // 2) * 128, OWN], BF16, sk)
    xnscr = dt("xnscr", [D, OWN], BF16, sk)
    xfscr = dt("xfscr", [D, OWN], F32, sk)
    oscr = dt("oscr", [NH * DV, OWN], BF16, sk)
    sscr = dt("sscr", [D, OWN], F32, sk)
    g16 = dt("g16", [D, INTER], BF16, "Internal")
    u16 = dt("u16", [D, INTER], BF16, "Internal")
    d16 = dt("d16", [INTER, D], BF16, "Internal")
    wo16 = dt("wo16", [D, D], BF16, "Internal")

    kb = KB()

    with ExitStack() as gctx:
        kro = gctx.enter_context(nc.sbuf_tensor("kro", [128, S], BF16))
        i16 = gctx.enter_context(nc.sbuf_tensor("i16", [128, 128], BF16))
        i32 = gctx.enter_context(nc.sbuf_tensor("i32", [128, 128], F32))
        ones_c = gctx.enter_context(nc.sbuf_tensor("ones_c", [128, 1], F32))
        ones_r = gctx.enter_context(nc.sbuf_tensor("ones_r", [1, 128], F32))
        for s in KB.SEMS:
            kb.sh[s] = gctx.enter_context(nc.semaphore("sem_" + s))
        blk = gctx.enter_context(nc.Block())

        phase_idx = [0]

        DMA_SEMS = [s for s in KB.SEMS if s not in ("ph", "p", "v", "a", "gq")]

        OWN_SEM = {"sync": None, "gpsimd": "gq", "tensor": "p",
                   "vector": "v", "scalar": "a"}

        def run_phase(sections):
            def wrap(name, fn):
                own = OWN_SEM[name]

                def g(e):
                    bw(e)
                    pre = dict(kb.n)
                    fn(e)
                    for s_ in DMA_SEMS:
                        if kb.n[s_] != pre[s_]:
                            kb.selfw(e, s_)
                    if own:
                        kb.selfw(e, own)
                    bi(e)
                return g
            wsecs = [(n_, wrap(n_, f_)) for n_, f_ in sections]
            snap = dict(kb.n)
            kb.dry = True
            for n_, f_ in wsecs:
                f_(DryEng())
            kb.n = dict(snap)
            kb.dry = False
            for n_, f_ in wsecs:
                getattr(blk, n_)(f_)
            kb.dry = True
            phase_idx[0] += 1

        def bw(eng):
            kb.wg(eng, "ph", 5 * phase_idx[0])

        def bi(eng):
            kb.inc(eng.nop(), "ph")

        def ch(e, instr, sem="v"):
            kb.inc(instr, sem)

        def ln_stats_vec(e, xb_ap, xsq_ap, stt_h, tag):
            stt = {k: v[:] for k, v in stt_h.items()}
            kb.selfw(e, "v")
            ch(e, e.reduce_sum(stt["s1"], xb_ap, AX.X))
            kb.wgm(e, "a", f"{tag}_xsq")
            ch(e, e.reduce_sum(stt["s2"], xsq_ap, AX.X))
            kb.selfw(e, "v")
            ch(e, e.tensor_scalar(stt["mean"], stt["s1"], 1.0 / D, None,
                                  ALU.mult, ALU.bypass))
            ch(e, e.tensor_scalar(stt["s2"], stt["s2"], 1.0 / D, None,
                                  ALU.mult, ALU.bypass))
            kb.selfw(e, "v")
            ch(e, e.tensor_mul(stt["t2"], stt["mean"], stt["mean"]))
            kb.selfw(e, "v")
            ch(e, e.scalar_tensor_tensor(stt["u"], stt["s2"], EPS_LN, stt["t2"],
                                         ALU.add, ALU.subtract))
            kb.selfw(e, "v")
            kb.inc(e.reciprocal(stt["w1"], stt["u"]), "v")
            kb.mark(f"{tag}_w1", "v")
            kb.wgm(e, "a", f"{tag}_r0")
            ch(e, e.tensor_mul(stt["t2"], stt["r0"], stt["r0"]))
            kb.selfw(e, "v")
            ch(e, e.tensor_mul(stt["t2"], stt["t2"], stt["u"]))
            kb.selfw(e, "v")
            ch(e, e.tensor_scalar(stt["t2"], stt["t2"], -0.5, 1.5, ALU.mult, ALU.add))
            kb.selfw(e, "v")
            ch(e, e.tensor_mul(stt["rstd"], stt["r0"], stt["t2"]))
            kb.selfw(e, "v")
            ch(e, e.tensor_mul(stt["nb"], stt["mean"], stt["rstd"]))
            kb.selfw(e, "v")
            kb.inc(e.tensor_scalar(stt["nb"], stt["nb"], -1.0, None, ALU.mult,
                                   ALU.bypass), "v")
            kb.mark(f"{tag}_ln", "v")

        def rms_row_vec(e, R, ps_st, rank, tag):
            kb.selfw(e, "v")
            ch(e, e.tensor_scalar(R(0), ps_st[:], 1.0 / rank, EPS_RMS, ALU.mult, ALU.add))
            kb.selfw(e, "v")
            kb.inc(e.reciprocal(R(1), R(0)), "v")
            kb.mark(f"{tag}_w1", "v")
            kb.wgm(e, "a", f"{tag}_r0")
            ch(e, e.tensor_mul(R(3), R(2), R(2)))
            kb.selfw(e, "v")
            ch(e, e.tensor_mul(R(3), R(3), R(0)))
            kb.selfw(e, "v")
            ch(e, e.tensor_scalar(R(3), R(3), -0.5, 1.5, ALU.mult, ALU.add))
            kb.selfw(e, "v")
            kb.inc(e.tensor_mul(R(4), R(2), R(3)), "v")
            kb.mark(f"{tag}_rstd", "v")

        def rope_apply(e, dst, ps_ap, cc, ss, tmp1, tmp2):
            t1, t2 = tmp1[0:32, :], tmp2[0:32, :]
            last = None
            for half in (0, 64):
                xe = ps_ap[half:half + 32, :]
                xo = ps_ap[half + 32:half + 64, :]
                kb.selfw(e, "v")
                ch(e, e.tensor_mul(t1, xe, cc))
                ch(e, e.tensor_mul(t2, xo, ss))
                kb.selfw(e, "v")
                ch(e, e.tensor_sub(dst[half:half + 32, :], t1, t2))
                kb.selfw(e, "v")
                ch(e, e.tensor_mul(t1, xo, cc))
                ch(e, e.tensor_mul(t2, xe, ss))
                kb.selfw(e, "v")
                last = e.tensor_add(dst[half + 32:half + 64, :], t1, t2)
                if half == 0:
                    ch(e, last)
            return last

        def ph0_sync(e):
            kb.inc(e.dma_start(out=i16[:], in_=i16_d), "qc", 16)
            kb.inc(e.dma_start(out=i32[:], in_=i32_d), "qc", 16)
            kb.mark("consts", "qc")

        def ph0_vec(e):
            kb.wgm(e, "qc", "consts")
            e.memset(ones_c[:], 1.0)
            kb.inc(e.memset(ones_r[:], 1.0), "v")
            kb.mark("ones", "v")

        def ph0_other(e):
            pass

        run_phase([("sync", ph0_sync), ("vector", ph0_vec), ("tensor", ph0_other),
                   ("scalar", ph0_other), ("gpsimd", ph0_other)])

        with ExitStack() as ctx:
            sb = lambda n, sh, d: ctx.enter_context(nc.sbuf_tensor(n, sh, d))
            ps = lambda n, sh, d: ctx.enter_context(nc.psum_tensor(n, sh, d))
            xbuf = [sb(f"a1_xbuf{i}", [128, D], F32) for i in range(2)]
            xsq = [sb(f"a1_xsq{i}", [128, D], F32) for i in range(2)]
            xnt = [sb(f"a1_xnt{i}", [128, D], BF16) for i in range(2)]
            stt_t = [{k: sb(f"a1_st_{k}{i}", [128, 1], F32)
                      for k in ["s1", "s2", "mean", "u", "w1", "r0", "t2", "rstd", "nb"]}
                     for i in range(2)]
            xnf = sb("a1_xnf", [128, NDC * 512], BF16)
            ykv = sb("a1_ykv", [128, NRKV * 512], F32)
            ropesc = [sb(f"a1_rsc{i}", [64, 512], F32) for i in range(2)]
            sqkv = [sb(f"a1_sqkv{i}", [128, 512], F32) for i in range(2)]
            kvcn = sb("a1_kvcn", [128, NRKV * 512], BF16)
            rrow = sb("a1_rrow", [1, 512 * 5], F32)
            Rr = lambda i: rrow[:, i * 512:(i + 1) * 512]
            kst = [sb(f"a1_kst{i}", [128, 512], BF16) for i in range(2)]
            vst = [sb(f"a1_vst{i}", [128, NH * 128], BF16) for i in range(2)]
            wkvd = sb("a1_wkvd", [128, NDC * KVR], BF16)
            wkvuk = sb("a1_wkvuk", [128, NRKV * 2048], BF16)
            wkvuv = sb("a1_wkvuv", [128, NRKV * 2048], BF16)
            wkr = sb("a1_wkr", [128, NDC * 128], BF16)
            kvnw = sb("a1_kvnw", [128, NRKV], F32)
            cst = sb("a1_cst", [128, S], F32)
            ps_tr = [ps(f"a1_pstr{i}", [128, 512], BF16) for i in range(2)]
            ps_mm = [ps(f"a1_psmm{i}", [128, 512], F32) for i in range(3)]
            ps_st = ps("a1_psst", [1, 512], F32)
            ps_bc = ps("a1_psbc", [128, 512], F32)

            def mm_jobs():
                out = [("kv", rc) for rc in range(NRKV)]
                out.append(("kr",))
                out += [("k", hc) for hc in range(NH)]
                out += [("v", kt, dvc) for kt in range(4) for dvc in range(4)]
                return out

            JPC = 37

            def a1_sync(e):
                kb.inc(e.dma_start(out=cst[:], in_=cst_d), "qc", 16)
                kb.inc(e.dma_start(out=kvnw[:], in_=kvnw_d), "qc", 16)
                kb.mark("a1_cst", "qc")
                for bt in range(32):
                    kb.wgm(e, "a", f"a1t{bt - 2}_xn")
                    sx = f"xb{bt % 2}"
                    kb.inc(e.dma_start(out=xbuf[bt % 2][:],
                                       in_=xb_d[bt * 128:(bt + 1) * 128, :]), sx, 16)
                    kb.mark(f"a1_x_{bt}", sx)

            def a1_gp(e):
                kb.inc(e.dma_start(out=r3(wkvd[:], NDC), in_=dr3(wkvdT_d)), "qw", 16)
                kb.inc(e.dma_start(out=r3(wkvuk[:], NRKV), in_=dr3(wkvukT_d)), "qw", 16)
                kb.inc(e.dma_start(out=r3(wkvuv[:], NRKV), in_=dr3(wkvuvT_d)), "qw", 16)
                kb.inc(e.dma_start(out=r3(wkr[:], NDC), in_=dr3(wkrT_d)), "qw", 16)
                kb.mark("a1_w", "qw")

            def a1_tensor(e):
                kb.wgm(e, "qw", "a1_w")
                for c in range(8):
                    for t in range(4):
                        bt = 4 * c + t
                        kb.wgm(e, "a", f"a1t{bt}_xn")
                        for q in range(4):
                            tb = bt * 4 + q
                            kb.wgm(e, "v", f"a1_trev_{tb - 2}")
                            for u in range(4):
                                dc = q * 4 + u
                                mm_ = e.transpose(ps_tr[tb % 2][:, u * 128:(u + 1) * 128],
                                                  xnt[bt % 2][:, dc * 128:(dc + 1) * 128],
                                                  i16[:])
                            kb.inc(mm_, "p")
                            kb.mark(f"a1_tr_{tb}", "p")
                    kb.wgm(e, "v", f"a1_xnf_{c}")
                    for i, job in enumerate(mm_jobs()):
                        g = c * JPC + i
                        slot = ps_mm[g % 3]
                        if job[0] == "k" and job[1] == 0:
                            kb.wgm(e, "v", f"a1_rmsrow{c - 1}_rstd")
                            for rc in range(NRKV):
                                kb.wgm(e, "a", f"a1_sq_{c}_{rc}")
                                mm_ = e.matmul(ps_st[:], ones_c[:], sqkv[rc % 2][:],
                                               start=(rc == 0), stop=(rc == NRKV - 1))
                                kb.inc(mm_, "p")
                                kb.mark(f"a1_stp_{c}_{rc}", "p")
                            kb.mark(f"a1_stmm_{c}", "p")
                            kb.wgm(e, "v", f"a1_rmsrow{c}_rstd")
                            kb.wgm(e, "v", f"a1_kvcn_{c - 1}")
                            kb.inc(e.matmul(ps_bc[:], ones_r[:], Rr(4),
                                            start=True, stop=True), "p")
                            kb.mark(f"a1_bc_{c}", "p")
                        kb.wgm(e, "v", f"a1_mmev_{g - 3}")
                        if job[0] == "kv":
                            rc = job[1]
                            for dc in range(NDC):
                                mm_ = e.matmul(slot[:],
                                               wkvd[:, dc * KVR + rc * 128:dc * KVR + (rc + 1) * 128],
                                               xnf[:, dc * 512:(dc + 1) * 512],
                                               start=(dc == 0), stop=(dc == NDC - 1))
                        elif job[0] == "kr":
                            for dc in range(NDC):
                                mm_ = e.matmul(slot[:],
                                               wkr[:, dc * 128:(dc + 1) * 128],
                                               xnf[:, dc * 512:(dc + 1) * 512],
                                               start=(dc == 0), stop=(dc == NDC - 1))
                        elif job[0] == "k":
                            hc = job[1]
                            if hc == 0:
                                kb.wgm(e, "v", f"a1_kvcn_{c}")
                            for rc in range(NRKV):
                                mm_ = e.matmul(slot[:],
                                               wkvuk[:, rc * 2048 + hc * 128:rc * 2048 + (hc + 1) * 128],
                                               kvcn[:, rc * 512:(rc + 1) * 512],
                                               start=(rc == 0), stop=(rc == NRKV - 1))
                        else:
                            _, kt, dvc = job
                            for rc in range(NRKV):
                                mm_ = e.matmul(slot[:],
                                               kvcn[:, rc * 512 + kt * 128:rc * 512 + (kt + 1) * 128],
                                               wkvuv[:, rc * 2048 + dvc * 512:rc * 2048 + (dvc + 1) * 512],
                                               start=(rc == 0), stop=(rc == NRKV - 1))
                        kb.inc(mm_, "p")
                        kb.mark(f"a1_mm_{g}", "p")

            def a1_vec(e):
                kb.wgm(e, "v", "ones")
                for c in range(8):
                    for t in range(4):
                        bt = 4 * c + t
                        kb.wgm(e, f"xb{bt % 2}", f"a1_x_{bt}")
                        ln_stats_vec(e, xbuf[bt % 2][:], xsq[bt % 2][:],
                                     stt_t[bt % 2], f"a1t{bt}")
                        for q in range(4):
                            tb = bt * 4 + q
                            kb.wgm(e, "p", f"a1_tr_{tb}")
                            kb.wgm(e, "p", f"a1_mm_{(c - 1) * JPC + 4}")
                            kb.inc(e.tensor_copy(
                                r3(xnf[:], NDC)[:, q * 4:(q + 1) * 4, t * 128:(t + 1) * 128],
                                r3(ps_tr[tb % 2][:], 4)), "v")
                            kb.mark(f"a1_trev_{tb}", "v")
                    kb.mark(f"a1_xnf_{c}", "v")
                    for i, job in enumerate(mm_jobs()):
                        g = c * JPC + i
                        slot = ps_mm[g % 3]
                        kb.wgm(e, "p", f"a1_mm_{g}")
                        if job[0] == "kv":
                            rc = job[1]
                            kb.inc(e.tensor_copy(ykv[:, rc * 512:(rc + 1) * 512], slot[:]),
                                   "v")
                            kb.mark(f"a1_ykv_{c}_{rc}", "v")
                            if rc == NRKV - 1:
                                pass
                        elif job[0] == "kr":
                            kb.wgm(e, "p", f"a1_stmm_{c}")
                            rms_row_vec(e, Rr, ps_st, KVR, f"a1_rmsrow{c}")
                            kb.wgm(e, "p", f"a1_bc_{c}")
                            kb.wgm(e, "qc", "a1_cst")
                            kb.selfw(e, "v")
                            for rc in range(NRKV):
                                kb.inc(e.scalar_tensor_tensor(
                                    kvcn[:, rc * 512:(rc + 1) * 512],
                                    ykv[:, rc * 512:(rc + 1) * 512],
                                    kvnw[:, rc:rc + 1], ps_bc[:],
                                    ALU.mult, ALU.mult), "v")
                            kb.mark(f"a1_kvcn_{c}", "v")
                            cc = cst[0:32, c * 512:(c + 1) * 512]
                            ss = cst[64:96, c * 512:(c + 1) * 512]
                            kb.inc(rope_apply(e, kro[:, c * 512:(c + 1) * 512], slot[:],
                                              cc, ss, ropesc[0][:], ropesc[1][:]), "v")
                            kb.mark(f"a1_kro_{c}", "v")
                        elif job[0] == "k":
                            hc = job[1]
                            kidx = c * NH + hc
                            kb.wgm(e, f"stk{kidx % 2}", f"a1_kstf_{kidx - 2}")
                            kb.inc(e.tensor_copy(kst[hc % 2][:], slot[:]), "v")
                            kb.mark(f"a1_kev_{kidx}", "v")
                        else:
                            _, kt, dvc = job
                            vidx = c * 4 + kt
                            kb.wgm(e, f"stv{vidx % 2}", f"a1_vstf_{vidx - 2}")
                            kb.inc(e.tensor_copy(
                                vst[kt % 2][:, dvc * 512:(dvc + 1) * 512], slot[:]), "v")
                            if dvc == 3:
                                kb.mark(f"a1_vrdy_{vidx}", "v")
                        kb.mark(f"a1_mmev_{g}", "v")

            def a1_scalar(e):
                for c in range(8):
                    for t in range(4):
                        bt = 4 * c + t
                        kb.wgm(e, f"xb{bt % 2}", f"a1_x_{bt}")
                        kb.wgm(e, "v", f"a1t{bt - 2}_w1")
                        kb.inc(e.activation(xsq[bt % 2][:], xbuf[bt % 2][:], AF.Square),
                               "a")
                        kb.mark(f"a1t{bt}_xsq", "a")
                        kb.wgm(e, "v", f"a1t{bt}_w1")
                        kb.inc(e.sqrt(stt_t[bt % 2]["r0"][:], stt_t[bt % 2]["w1"][:]), "a")
                        kb.mark(f"a1t{bt}_r0", "a")
                        kb.wgm(e, "v", f"a1t{bt}_ln")
                        kb.wgm(e, "p", f"a1_tr_{(bt - 2) * 4 + 3}")
                        kb.inc(e.activation(xnt[bt % 2][:], xbuf[bt % 2][:], AF.Identity,
                                            bias=stt_t[bt % 2]["nb"][:],
                                            scale=stt_t[bt % 2]["rstd"][:]), "a")
                        kb.mark(f"a1t{bt}_xn", "a")
                    for rc in range(NRKV):
                        kb.wgm(e, "v", f"a1_ykv_{c}_{rc}")
                        if rc >= 2:
                            kb.wgm(e, "p", f"a1_stp_{c}_{rc - 2}")
                        else:
                            kb.wgm(e, "p", f"a1_stp_{c - 1}_{rc + 2}")
                        kb.inc(e.activation(sqkv[rc % 2][:], ykv[:, rc * 512:(rc + 1) * 512],
                                            AF.Square), "a")
                        kb.mark(f"a1_sq_{c}_{rc}", "a")
                    kb.wgm(e, "v", f"a1_rmsrow{c}_w1")
                    kb.inc(e.sqrt(Rr(2), Rr(1)), "a")
                    kb.mark(f"a1_rmsrow{c}_r0", "a")
                    for hc in range(NH):
                        kidx = c * NH + hc
                        kb.wgm(e, "v", f"a1_kev_{kidx}")
                        kb.inc(e.dma_start(
                            out=kscr[hc * 128:(hc + 1) * 128, c * 512:(c + 1) * 512],
                            in_=kst[hc % 2][:]), f"stk{kidx % 2}", 16)
                        kb.mark(f"a1_kstf_{kidx}", f"stk{kidx % 2}")
                    for kt in range(4):
                        vidx = c * 4 + kt
                        kb.wgm(e, "v", f"a1_vrdy_{vidx}")
                        ktg = c * 4 + kt
                        for h in range(NH):
                            kb.inc(e.dma_start(
                                out=vscr[(h * 32 + ktg) * 128:(h * 32 + ktg + 1) * 128, :],
                                in_=vst[kt % 2][:, h * 128:(h + 1) * 128]),
                                f"stv{vidx % 2}", 16)
                        kb.mark(f"a1_vstf_{vidx}", f"stv{vidx % 2}")
                for s_ in ["stk0", "stk1", "stv0", "stv1"]:
                    kb.mark(f"a1_done_{s_}", s_)

            if upto >= 1:
                run_phase([("sync", a1_sync), ("gpsimd", a1_gp), ("tensor", a1_tensor),
                           ("vector", a1_vec), ("scalar", a1_scalar)])

        with ExitStack() as ctx:
            sb = lambda n, sh, d: ctx.enter_context(nc.sbuf_tensor(n, sh, d))
            ps = lambda n, sh, d: ctx.enter_context(nc.psum_tensor(n, sh, d))
            xbuf = [sb(f"a2_xbuf{i}", [128, D], F32) for i in range(2)]
            xsq = [sb(f"a2_xsq{i}", [128, D], F32) for i in range(2)]
            xnt = [sb(f"a2_xnt{i}", [128, D], BF16) for i in range(2)]
            stt_t = [{k: sb(f"a2_st_{k}{i}", [128, 1], F32)
                      for k in ["s1", "s2", "mean", "u", "w1", "r0", "t2", "rstd", "nb"]}
                     for i in range(2)]
            xnstg = [sb(f"a2_xnstg{i}", [128, 512], BF16) for i in range(2)]
            xfstg = [sb(f"a2_xfstg{i}", [128, 512], F32) for i in range(2)]
            ps_tr = [ps(f"a2_pstr{i}", [128, 512], BF16) for i in range(2)]
            ps_tr32 = [ps(f"a2_pstr32{i}", [128, 512], F32) for i in range(2)]

            def a2a_sync(e):
                for bt in range(16):
                    kb.wgm(e, "a", f"a2t{bt - 2}_xn")
                    kb.wgm(e, "p", f"a2_tf_{(bt - 2) * 4 + 3}")
                    sx = f"xb{bt % 2}"
                    kb.inc(e.dma_start(out=xbuf[bt % 2][:],
                                       in_=xown_d[bt * 128:(bt + 1) * 128, :]), sx, 16)
                    kb.mark(f"a2_x_{bt}", sx)

            def a2a_tensor(e):
                for bt in range(16):
                    kb.wgm(e, "a", f"a2t{bt}_xn")
                    for q in range(4):
                        tb = bt * 4 + q
                        kb.wgm(e, "v", f"a2_trev_{tb - 2}")
                        for u in range(4):
                            dc = q * 4 + u
                            mm_ = e.transpose(ps_tr[tb % 2][:, u * 128:(u + 1) * 128],
                                              xnt[bt % 2][:, dc * 128:(dc + 1) * 128],
                                              i16[:])
                        kb.inc(mm_, "p")
                        kb.mark(f"a2_tr_{tb}", "p")
                        kb.wgm(e, "v", f"a2_tfev_{tb - 2}")
                        for u in range(4):
                            dc = q * 4 + u
                            mm_ = e.transpose(ps_tr32[tb % 2][:, u * 128:(u + 1) * 128],
                                              xbuf[bt % 2][:, dc * 128:(dc + 1) * 128],
                                              i32[:])
                        kb.inc(mm_, "p")
                        kb.mark(f"a2_tf_{tb}", "p")

            def a2a_vec(e):
                for bt in range(16):
                    kb.wgm(e, f"xb{bt % 2}", f"a2_x_{bt}")
                    ln_stats_vec(e, xbuf[bt % 2][:], xsq[bt % 2][:], stt_t[bt % 2],
                                 f"a2t{bt}")
                    for q in range(4):
                        tb = bt * 4 + q
                        kb.wgm(e, "p", f"a2_tr_{tb}")
                        kb.wgm(e, f"stn{tb % 2}", f"a2_xnst_{tb - 2}")
                        kb.inc(e.tensor_copy(xnstg[tb % 2][:], ps_tr[tb % 2][:]), "v")
                        kb.mark(f"a2_trev_{tb}", "v")
                        kb.wgm(e, "p", f"a2_tf_{tb}")
                        kb.wgm(e, f"stf{tb % 2}", f"a2_xfst_{tb - 2}")
                        kb.inc(e.tensor_copy(xfstg[tb % 2][:], ps_tr32[tb % 2][:]), "v")
                        kb.mark(f"a2_tfev_{tb}", "v")

            def a2a_scalar(e):
                for bt in range(16):
                    oc, t = bt // 4, bt % 4
                    kb.wgm(e, f"xb{bt % 2}", f"a2_x_{bt}")
                    kb.wgm(e, "v", f"a2t{bt - 2}_w1")
                    kb.inc(e.activation(xsq[bt % 2][:], xbuf[bt % 2][:], AF.Square), "a")
                    kb.mark(f"a2t{bt}_xsq", "a")
                    kb.wgm(e, "v", f"a2t{bt}_w1")
                    kb.inc(e.sqrt(stt_t[bt % 2]["r0"][:], stt_t[bt % 2]["w1"][:]), "a")
                    kb.mark(f"a2t{bt}_r0", "a")
                    kb.wgm(e, "v", f"a2t{bt}_ln")
                    kb.wgm(e, "p", f"a2_tr_{(bt - 2) * 4 + 3}")
                    kb.inc(e.activation(xnt[bt % 2][:], xbuf[bt % 2][:], AF.Identity,
                                        bias=stt_t[bt % 2]["nb"][:],
                                        scale=stt_t[bt % 2]["rstd"][:]), "a")
                    kb.mark(f"a2t{bt}_xn", "a")
                    for q in range(4):
                        tb = bt * 4 + q
                        kb.wgm(e, "v", f"a2_trev_{tb}")
                        dst = dr3(xnscr)
                        kb.inc(e.dma_start(
                            out=dst[:, q * 4:(q + 1) * 4, bt * 128:(bt + 1) * 128],
                            in_=r3(xnstg[tb % 2][:], 4)), f"stn{tb % 2}", 16)
                        kb.mark(f"a2_xnst_{tb}", f"stn{tb % 2}")
                        kb.wgm(e, "v", f"a2_tfev_{tb}")
                        dstf = dr3(xfscr)
                        kb.inc(e.dma_start(
                            out=dstf[:, q * 4:(q + 1) * 4, bt * 128:(bt + 1) * 128],
                            in_=r3(xfstg[tb % 2][:], 4)), f"stf{tb % 2}", 16)
                        kb.mark(f"a2_xfst_{tb}", f"stf{tb % 2}")
                for s_ in ["stn0", "stn1", "stf0", "stf1"]:
                    kb.mark(f"a2_done_{s_}", s_)

            def a2a_gp(e):
                pass

            if upto >= 2:
                run_phase([("sync", a2a_sync), ("gpsimd", a2a_gp), ("tensor", a2a_tensor),
                           ("vector", a2a_vec), ("scalar", a2a_scalar)])

        with ExitStack() as ctx:
            sb = lambda n, sh, d: ctx.enter_context(nc.sbuf_tensor(n, sh, d))
            ps = lambda n, sh, d: ctx.enter_context(nc.psum_tensor(n, sh, d))
            wqd = sb("a2b_wqd", [128, NDC * QR], BF16)
            wqu = sb("a2b_wqu", [128, NRQ * D], BF16)
            wqr = sb("a2b_wqr", [128, NRQ * NH * DR], BF16)
            qnw = sb("a2b_qnw", [128, NRQ], F32)
            csq = [sb(f"a2b_csq{i}", [128, 512], F32) for i in range(2)]
            xnfi = sb("a2b_xnfi", [128, NDC * 512], BF16)
            yq = sb("a2b_yq", [128, NRQ * 512], F32)
            sqq = [sb(f"a2b_sq{i}", [128, 512], F32) for i in range(2)]
            qcn = sb("a2b_qcn", [128, NRQ * 512], BF16)
            rrow = sb("a2b_rrow", [1, 512 * 5], F32)
            Rr = lambda i: rrow[:, i * 512:(i + 1) * 512]
            qst = [sb(f"a2b_qst{i}", [128, 512], BF16) for i in range(2)]
            ropesc = [sb(f"a2b_rsc{i}", [64, 512], F32) for i in range(2)]
            ps_mm = [ps(f"a2b_psmm{i}", [128, 512], F32) for i in range(4)]
            ps_st = ps("a2b_psst", [1, 512], F32)
            ps_bc = ps("a2b_psbc", [128, 512], F32)

            def q_jobs():
                return ([("qc", rc) for rc in range(NRQ)]
                        + [("qn", hc) for hc in range(NH)]
                        + [("qp", pc) for pc in range(NH // 2)])

            JQ = 36

            def a2b_sync(e):
                kb.inc(e.dma_start(out=qnw[:], in_=qnw_d), "qc", 16)
                kb.mark("a2b_c", "qc")
                for oc in range(4):
                    kb.wgm(e, "p", f"a2b_mm_{(oc - 1) * JQ + JQ - 1}")
                    for s_ in ["stn0", "stn1"]:
                        kb.wgm(e, s_, f"a2_done_{s_}")
                    kb.inc(e.dma_start(
                        out=r3(xnfi[:], NDC),
                        in_=dr3(xnscr)[:, :, oc * 512:(oc + 1) * 512]), "xnf", 16)
                    kb.mark(f"a2b_xn_{oc}", "xnf")
                    kb.wgm(e, "v", f"a2b_qpev_{(oc - 2) * 24 + 16 + 7}")
                    sc_ = f"csq{oc % 2}"
                    kb.inc(e.dma_start(out=csq[oc % 2][:],
                                       in_=csq_d[:, oc * 512:(oc + 1) * 512]), sc_, 16)
                    kb.mark(f"a2b_csq_{oc}", sc_)

            def a2b_gp(e):
                kb.inc(e.dma_start(out=r3(wqd[:], NDC), in_=dr3(wqdT_d)), "qw", 16)
                kb.inc(e.dma_start(out=r3(wqu[:], NRQ), in_=dr3(wquT_d)), "qw", 16)
                kb.inc(e.dma_start(out=r3(wqr[:], NRQ), in_=dr3(wqrT_d)), "qw", 16)
                kb.mark("a2b_w", "qw")
                kb.inc(e.dma_start(out=wo16, in_=woT_d), "w16a", 16)
                kb.mark("a1_wo16", "w16a")
                kb.inc(e.dma_start(out=d16, in_=downT_d), "w16b", 16)
                kb.mark("a1_d16", "w16b")

            def a2b_tensor(e):
                kb.wgm(e, "qw", "a2b_w")
                for oc in range(4):
                    kb.wgm(e, "xnf", f"a2b_xn_{oc}")
                    for i, job in enumerate(q_jobs()):
                        g = oc * JQ + i
                        slot = ps_mm[g % 4]
                        if job[0] == "qn" and job[1] == 0:
                            kb.wgm(e, "v", f"a2b_rms{oc - 1}_rstd")
                            for rc in range(NRQ):
                                kb.wgm(e, "a", f"a2b_sq_{oc}_{rc}")
                                mm_ = e.matmul(ps_st[:], ones_c[:], sqq[rc % 2][:],
                                               start=(rc == 0), stop=(rc == NRQ - 1))
                                kb.inc(mm_, "p")
                                kb.mark(f"a2b_stp_{oc}_{rc}", "p")
                            kb.mark(f"a2b_stmm_{oc}", "p")
                            kb.wgm(e, "v", f"a2b_rms{oc}_rstd")
                            kb.wgm(e, "v", f"a2b_qcn_{oc - 1}")
                            kb.inc(e.matmul(ps_bc[:], ones_r[:], Rr(4),
                                            start=True, stop=True), "p")
                            kb.mark(f"a2b_bc_{oc}", "p")
                        kb.wgm(e, "v", f"a2b_mmev_{g - 4}")
                        if job[0] == "qc":
                            rc = job[1]
                            for dc in range(NDC):
                                mm_ = e.matmul(slot[:],
                                               wqd[:, dc * QR + rc * 128:dc * QR + (rc + 1) * 128],
                                               xnfi[:, dc * 512:(dc + 1) * 512],
                                               start=(dc == 0), stop=(dc == NDC - 1))
                        elif job[0] == "qn":
                            hc = job[1]
                            if hc == 0:
                                kb.wgm(e, "v", f"a2b_qcn_{oc}")
                            for rc in range(NRQ):
                                mm_ = e.matmul(slot[:],
                                               wqu[:, rc * D + hc * 128:rc * D + (hc + 1) * 128],
                                               qcn[:, rc * 512:(rc + 1) * 512],
                                               start=(rc == 0), stop=(rc == NRQ - 1))
                        else:
                            pc = job[1]
                            for rc in range(NRQ):
                                mm_ = e.matmul(slot[:],
                                               wqr[:, rc * NH * DR + pc * 128:rc * NH * DR + (pc + 1) * 128],
                                               qcn[:, rc * 512:(rc + 1) * 512],
                                               start=(rc == 0), stop=(rc == NRQ - 1))
                        kb.inc(mm_, "p")
                        kb.mark(f"a2b_mm_{g}", "p")

            def a2b_vec(e):
                for oc in range(4):
                    for i, job in enumerate(q_jobs()):
                        g = oc * JQ + i
                        slot = ps_mm[g % 4]
                        kb.wgm(e, "p", f"a2b_mm_{g}")
                        if job[0] == "qc":
                            rc = job[1]
                            kb.inc(e.tensor_copy(yq[:, rc * 512:(rc + 1) * 512], slot[:]),
                                   "v")
                            kb.mark(f"a2b_yq_{oc}_{rc}", "v")
                            if rc == NRQ - 1:
                                kb.wgm(e, "p", f"a2b_stmm_{oc}")
                                rms_row_vec(e, Rr, ps_st, QR, f"a2b_rms{oc}")
                                kb.wgm(e, "p", f"a2b_bc_{oc}")
                                kb.wgm(e, "qc", "a2b_c")
                                kb.selfw(e, "v")
                                for rr in range(NRQ):
                                    kb.inc(e.scalar_tensor_tensor(
                                        qcn[:, rr * 512:(rr + 1) * 512],
                                        yq[:, rr * 512:(rr + 1) * 512],
                                        qnw[:, rr:rr + 1], ps_bc[:],
                                        ALU.mult, ALU.mult), "v")
                                kb.mark(f"a2b_qcn_{oc}", "v")
                        elif job[0] == "qn":
                            hc = job[1]
                            lin = oc * 24 + hc
                            kb.wgm(e, f"stg{lin % 2}", f"a2b_stg_{lin - 2}")
                            kb.inc(e.tensor_copy(qst[lin % 2][:], slot[:]), "v")
                            kb.mark(f"a2b_qnev_{lin}", "v")
                        else:
                            pc = job[1]
                            lin = oc * 24 + 16 + pc
                            kb.wgm(e, f"stg{lin % 2}", f"a2b_stg_{lin - 2}")
                            kb.wgm(e, f"csq{oc % 2}", f"a2b_csq_{oc}")
                            cc = csq[oc % 2][0:32, :]
                            ss = csq[oc % 2][64:96, :]
                            kb.inc(rope_apply(e, qst[lin % 2][:], slot[:], cc, ss,
                                              ropesc[0][:], ropesc[1][:]), "v")
                            kb.mark(f"a2b_qpev_{lin}", "v")
                        kb.mark(f"a2b_mmev_{g}", "v")

            def a2b_scalar(e):
                for oc in range(4):
                    for rc in range(NRQ):
                        kb.wgm(e, "v", f"a2b_yq_{oc}_{rc}")
                        if rc >= 2:
                            kb.wgm(e, "p", f"a2b_stp_{oc}_{rc - 2}")
                        else:
                            kb.wgm(e, "p", f"a2b_stp_{oc - 1}_{rc + 10}")
                        kb.inc(e.activation(sqq[rc % 2][:], yq[:, rc * 512:(rc + 1) * 512],
                                            AF.Square), "a")
                        kb.mark(f"a2b_sq_{oc}_{rc}", "a")
                    kb.wgm(e, "v", f"a2b_rms{oc}_w1")
                    kb.inc(e.sqrt(Rr(2), Rr(1)), "a")
                    kb.mark(f"a2b_rms{oc}_r0", "a")
                    for hc in range(NH):
                        lin = oc * 24 + hc
                        kb.wgm(e, "v", f"a2b_qnev_{lin}")
                        kb.inc(e.dma_start(
                            out=qnscr[hc * 128:(hc + 1) * 128, oc * 512:(oc + 1) * 512],
                            in_=qst[lin % 2][:]), f"stg{lin % 2}", 16)
                        kb.mark(f"a2b_stg_{lin}", f"stg{lin % 2}")
                    for pc in range(NH // 2):
                        lin = oc * 24 + 16 + pc
                        kb.wgm(e, "v", f"a2b_qpev_{lin}")
                        kb.inc(e.dma_start(
                            out=qpscr[pc * 128:(pc + 1) * 128, oc * 512:(oc + 1) * 512],
                            in_=qst[lin % 2][:]), f"stg{lin % 2}", 16)
                        kb.mark(f"a2b_stg_{lin}", f"stg{lin % 2}")
                for s_ in ["stg0", "stg1"]:
                    kb.mark(f"a2b_done_{s_}", s_)

            if upto >= 3:
                run_phase([("sync", a2b_sync), ("gpsimd", a2b_gp), ("tensor", a2b_tensor),
                           ("vector", a2b_vec), ("scalar", a2b_scalar)])

        with ExitStack() as ctx:
            sb = lambda n, sh, d: ctx.enter_context(nc.sbuf_tensor(n, sh, d))
            ps = lambda n, sh, d: ctx.enter_context(nc.psum_tensor(n, sh, d))
            qng = [sb(f"b_qng{i}", [128, NH * 256], BF16) for i in range(2)]
            qpg = [sb(f"b_qpg{i}", [128, (NH // 2) * 256], BF16) for i in range(2)]
            kbuf = [[sb(f"b_kb{i}_{h}", [128, S], BF16) for h in range(2)] for i in range(2)]
            vbuf = [[sb(f"b_vb0_{h}", [128, S], BF16) for h in range(2)]]
            srow = [sb(f"b_srow{i}", [128, S], F32) for i in range(4)]
            mxb = [sb(f"b_mxb{i}", [128, 8], F32) for i in range(4)]
            mcol = [sb(f"b_mcol{i}", [128, 1], F32) for i in range(4)]
            bias = [sb(f"b_bias{i}", [128, 1], F32) for i in range(4)]
            ell = [sb(f"b_ell{i}", [128, 1], F32) for i in range(4)]
            linv = [sb(f"b_linv{i}", [128, 1], F32) for i in range(4)]
            pb = [sb(f"b_pb{i}", [128, S], BF16) for i in range(4)]
            ptb = sb("b_ptb", [128, 2 * S], BF16)
            msk = sb("b_msk", [128, 1024], F32)
            ostg = [sb(f"b_ostg{i}", [128, 256], BF16) for i in range(2)]
            ps_s = [ps(f"b_pss{i}", [128, 512], F32) for i in range(4)]
            ps_pt = [ps(f"b_pspt{i}", [128, 512], BF16) for i in range(2)]
            ps_pv = [ps(f"b_pspv{i}", [128, 256], F32) for i in range(2)]

            def ncount(gi):
                return gi + 1

            def b_sync(e):
                for s_ in ["stg0", "stg1"]:
                    kb.wgm(e, s_, f"a2b_done_{s_}")
                for s_ in ["stk0", "stk1", "stv0", "stv1"]:
                    kb.wgm(e, s_, f"a1_done_{s_}")
                for gi in range(8):
                    kb.wgm(e, "p", f"b_smmd_{(gi - 2) * 8 + 7}_1")
                    sq_ = f"q{gi % 2}"
                    kb.inc(e.dma_start(
                        out=r3(qng[gi % 2][:], NH),
                        in_=dr3(qnscr)[:, :, gi * 256:(gi + 1) * 256]), sq_, 16)
                    kb.inc(e.dma_start(
                        out=r3(qpg[gi % 2][:], NH // 2),
                        in_=dr3(qpscr)[:, :, gi * 256:(gi + 1) * 256]), sq_, 16)
                    kb.mark(f"b_q_{gi}", sq_)
                    kb.wgm(e, "v", f"b_maskd_{gi - 1}")
                    kb.inc(e.dma_start(
                        out=r3(msk[:], 2),
                        in_=dr3(mask_d)[:, 2 * gi:2 * gi + 2, :]), "qm", 16)
                    kb.mark(f"b_m_{gi}", "qm")
                    nc_ = ncount(gi)
                    for hp in range(8):
                        gih = gi * 8 + hp
                        for hh in range(2):
                            h = 2 * hp + hh
                            sk_ = f"k{hp % 2}{hh}"
                            kb.wgm(e, "p", f"b_smmd_{gih - 2}_{hh}")
                            kb.inc(e.dma_start(
                                out=kbuf[hp % 2][hh][:, 0:nc_ * 512],
                                in_=kscr[h * 128:(h + 1) * 128, 0:nc_ * 512]), sk_, 16)
                            kb.mark(f"b_k_{gih}_{hh}", sk_)
                        for hh in range(2):
                            h = 2 * hp + hh
                            sv_ = f"v0{hh}"
                            kb.wgm(e, "p", f"b_pv_{gih - 1}_{hh}")
                            kb.inc(e.dma_start(
                                out=r3(vbuf[0][hh][:, 0:nc_ * 4 * 128], nc_ * 4),
                                in_=dr3(vscr[h * 32 * 128:(h * 32 + nc_ * 4) * 128, :])),
                                sv_, 16)
                            kb.mark(f"b_v_{gih}_{hh}", sv_)

            def b_tensor(e):
                for gi in range(8):
                    nc_ = ncount(gi)
                    for hp in range(8):
                        gih = gi * 8 + hp
                        kb.wgm(e, f"q{gi % 2}", f"b_q_{gi}")
                        for hh in range(2):
                            kb.wgm(e, f"k{hp % 2}{hh}", f"b_k_{gih}_{hh}")
                        for c in range(nc_):
                            for s_ in range(2):
                                prev = (f"b_sev_{gih}_{c - 1}_{s_ * 2 + 1}" if c > 0 else
                                        f"b_sev_{gih - 1}_{ncount((gih - 1) // 8) - 1}_3"
                                        if gih > 0 else None)
                                if prev:
                                    kb.wgm(e, "v", prev)
                                qe = qpg[gi % 2]
                                base = hp * 256 + s_ * 128
                                kv_c = slice(c * 512, (c + 1) * 512)
                                for hh in range(2):
                                    h = 2 * hp + hh
                                    qn_t = qng[gi % 2][:, h * 256 + s_ * 128:h * 256 + (s_ + 1) * 128]
                                    e.matmul(ps_s[s_ * 2 + hh][:], qn_t,
                                             kbuf[hp % 2][hh][:, kv_c],
                                             start=True, stop=False,
                                             skip_group_check=True)
                                e.matmul(ps_s[s_ * 2][:], qe[0:64, base:base + 128],
                                         kro[0:64, kv_c], start=False, stop=True,
                                         skip_group_check=True)
                                mm_ = e.matmul(ps_s[s_ * 2 + 1][:],
                                               qe[64:128, base:base + 128],
                                               kro[64:128, kv_c], start=False, stop=True,
                                               skip_group_check=True)
                                kb.inc(mm_, "p")
                                kb.mark(f"b_smm_{gih}_{c}_{s_ * 2}", "p")
                                kb.mark(f"b_smm_{gih}_{c}_{s_ * 2 + 1}", "p")
                        kb.mark(f"b_smmd_{gih}_0", "p")
                        kb.mark(f"b_smmd_{gih}_1", "p")
                        combo_order = [0, 2, 1, 3]
                        for ci_, combo in enumerate(combo_order):
                            s_, hh = combo // 2, combo % 2
                            if True:
                                kb.wgm(e, "v", f"b_pn_{gih}_{combo}")
                                for bq in range(nc_):
                                    if bq < 2:
                                        pc_ = (combo_order[ci_ - 1] if ci_ > 0 else 3)
                                        pg_ = gih if ci_ > 0 else gih - 1
                                        pn_ = ncount(pg_ // 8) if pg_ >= 0 else 1
                                        kb.wgm(e, "v", f"b_ptev_{pg_}_{pc_}_{pn_ - 1}")
                                    else:
                                        kb.wgm(e, "v", f"b_ptev_{gih}_{combo}_{bq - 2}")
                                    for u in range(4):
                                        kc = bq * 4 + u
                                        mm_ = e.transpose(
                                            ps_pt[bq % 2][:, u * 128:(u + 1) * 128],
                                            pb[combo][:, kc * 128:(kc + 1) * 128], i16[:])
                                    kb.inc(mm_, "p")
                                    kb.mark(f"b_tr_{gih}_{combo}_{bq}", "p")
                                kb.mark(f"b_trd_{gih}_{combo}", "p")
                            if s_ == 1:
                                kb.wgm(e, "v", f"b_ptev_{gih}_{2 + hh}_{nc_ - 1}")
                                kb.wgm(e, "v", f"b_ptev_{gih}_{0 + hh}_{nc_ - 1}")
                                kb.wgm(e, f"v0{hh}", f"b_v_{gih}_{hh}")
                                kb.wgm(e, "v", f"b_oev_{gih - 1}_{hh}")
                                ptr = ptb[:].rearrange("p (s c q) -> p c s q",
                                                       s=2, c=32, q=128)
                                for kc in range(nc_ * 4):
                                    mm_ = e.matmul(ps_pv[hh][:],
                                                   vbuf[0][hh][:, kc * 128:(kc + 1) * 128],
                                                   ptr[:, kc],
                                                   start=(kc == 0),
                                                   stop=(kc == nc_ * 4 - 1))
                                kb.inc(mm_, "p")
                                kb.mark(f"b_pv_{gih}_{hh}", "p")

            def b_vec(e):
                for gi in range(8):
                    nc_ = ncount(gi)
                    for hp in range(8):
                        gih = gi * 8 + hp
                        for c in range(nc_):
                            for combo in range(4):
                                kb.wgm(e, "p", f"b_smm_{gih}_{c}_{combo}")
                                if c == 0:
                                    kb.wgm(e, "a", f"b_exp_{gih - 1}_{combo}")
                                kb.inc(e.tensor_scalar(
                                    srow[combo][:, c * 512:(c + 1) * 512],
                                    ps_s[combo][:], SCALE, None, ALU.mult, ALU.max,
                                    accum_out=mxb[combo][:, c:c + 1]), "v")
                                kb.mark(f"b_sev_{gih}_{c}_{combo}", "v")
                        for combo in range(4):
                            s_ = combo // 2
                            kb.wgm(e, "qm", f"b_m_{gi}")
                            kb.selfw(e, "v")
                            ch(e, e.tensor_add(
                                srow[combo][:, (nc_ - 1) * 512:nc_ * 512],
                                srow[combo][:, (nc_ - 1) * 512:nc_ * 512],
                                msk[:, s_ * 512:(s_ + 1) * 512]))
                            ch(e, e.reduce_max(mcol[combo][:], mxb[combo][:, 0:nc_],
                                               AX.X))
                            kb.selfw(e, "v")
                            kb.inc(e.tensor_scalar(bias[combo][:], mcol[combo][:], -1.0,
                                                   None, ALU.mult, ALU.bypass), "v")
                            kb.mark(f"b_bias_{gih}_{combo}", "v")
                        if hp == 7:
                            kb.mark(f"b_maskd_{gi}", "v")
                        combo_order = [0, 2, 1, 3]
                        for ci_, combo in enumerate(combo_order):
                            s_, hh = combo // 2, combo % 2
                            kb.wgm(e, "a", f"b_exp_{gih}_{combo}")
                            kb.inc(e.reciprocal(linv[combo][:], ell[combo][:]), "v")
                            kb.mark(f"b_linv_{gih}_{combo}", "v")
                            kb.selfw(e, "v")
                            kb.inc(e.tensor_scalar(pb[combo][:, 0:nc_ * 512],
                                                   pb[combo][:, 0:nc_ * 512],
                                                   linv[combo][:], None,
                                                   ALU.mult, ALU.bypass), "v")
                            kb.mark(f"b_pn_{gih}_{combo}", "v")
                            for bq in range(nc_):
                                kb.wgm(e, "p", f"b_tr_{gih}_{combo}_{bq}")
                                if bq == 0 and s_ == 0:
                                    kb.wgm(e, "p", (f"b_pv_{gih - 1}_1" if hh == 0
                                                    else f"b_pv_{gih}_0"))
                                kb.inc(e.tensor_copy(
                                    ptb[:, s_ * S + bq * 512:s_ * S + (bq + 1) * 512],
                                    ps_pt[bq % 2][:]), "v")
                                kb.mark(f"b_ptev_{gih}_{combo}_{bq}", "v")
                            if s_ == 1:
                                kb.wgm(e, "p", f"b_pv_{gih}_{hh}")
                                kb.wgm(e, f"sto{hh}", f"b_ost_{gih - 1}_{hh}")
                                kb.inc(e.tensor_copy(ostg[hh][:], ps_pv[hh][:]), "v")
                                kb.mark(f"b_oev_{gih}_{hh}", "v")

            def b_scalar(e):
                for gi in range(8):
                    nc_ = ncount(gi)
                    for hp in range(8):
                        gih = gi * 8 + hp
                        for combo in [0, 2, 1, 3]:
                            s_, hh = combo // 2, combo % 2
                            kb.wgm(e, "v", f"b_bias_{gih}_{combo}")
                            kb.wgm(e, "p", f"b_trd_{gih - 1}_{combo}")
                            kb.inc(e.activation(pb[combo][:, 0:nc_ * 512],
                                                srow[combo][:, 0:nc_ * 512], AF.Exp,
                                                bias=bias[combo][:], scale=1.0,
                                                accum_out=ell[combo][:]), "a")
                            kb.mark(f"b_exp_{gih}_{combo}", "a")
                        for hh in range(2):
                            h = 2 * hp + hh
                            kb.wgm(e, "v", f"b_oev_{gih}_{hh}")
                            kb.inc(e.dma_start(
                                out=oscr[h * 128:(h + 1) * 128, gi * 256:(gi + 1) * 256],
                                in_=ostg[hh][:]), f"sto{hh}", 16)
                            kb.mark(f"b_ost_{gih}_{hh}", f"sto{hh}")
                for s_ in ["sto0", "sto1"]:
                    kb.mark(f"b_done_{s_}", s_)

            def b_gp(e):
                kb.inc(e.dma_start(out=g16, in_=gateT_d), "w16b", 16)
                kb.inc(e.dma_start(out=u16, in_=upT_d), "w16b", 16)
                kb.mark("a1_gu16", "w16b")

            if upto >= 4:
                run_phase([("sync", b_sync), ("gpsimd", b_gp), ("tensor", b_tensor),
                           ("vector", b_vec), ("scalar", b_scalar)])

        with ExitStack() as ctx:
            sb = lambda n, sh, d: ctx.enter_context(nc.sbuf_tensor(n, sh, d))
            ps = lambda n, sh, d: ctx.enter_context(nc.psum_tensor(n, sh, d))
            wo = sb("b2_wo", [128, NH * D], BF16)
            ob = [sb(f"b2_ob{i}", [128, NH * 512], BF16) for i in range(2)]
            sstg = [sb(f"b2_sstg{i}", [128, 512], F32) for i in range(2)]
            ps_wo = [ps(f"b2_pswo{i}", [128, 512], F32) for i in range(2)]

            def b2_sync(e):
                for s_ in ["sto0", "sto1"]:
                    kb.wgm(e, s_, f"b_done_{s_}")
                for tc in range(4):
                    kb.wgm(e, "p", f"b2_mmd_{tc - 2}")
                    so_ = f"ob{tc % 2}"
                    kb.inc(e.dma_start(
                        out=r3(ob[tc % 2][:], NH),
                        in_=dr3(oscr)[:, :, tc * 512:(tc + 1) * 512]), so_, 16)
                    kb.mark(f"b2_o_{tc}", so_)

            def b2_gp(e):
                kb.wgm(e, "w16a", "a1_wo16")
                kb.inc(e.dma_start(out=r3(wo[:], NH), in_=dr3(wo16)), "qw", 16)
                kb.mark("b2_w", "qw")

            def b2_tensor(e):
                kb.wgm(e, "qw", "b2_w")
                for tc in range(4):
                    kb.wgm(e, f"ob{tc % 2}", f"b2_o_{tc}")
                    for do in range(NDC):
                        g = tc * NDC + do
                        kb.wgm(e, "v", f"b2_sev_{g - 2}")
                        for hd in range(NH):
                            mm_ = e.matmul(ps_wo[g % 2][:],
                                           wo[:, hd * D + do * 128:hd * D + (do + 1) * 128],
                                           ob[tc % 2][:, hd * 512:(hd + 1) * 512],
                                           start=(hd == 0), stop=(hd == NH - 1))
                        kb.inc(mm_, "p")
                        kb.mark(f"b2_mm_{g}", "p")
                    kb.mark(f"b2_mmd_{tc}", "p")

            def b2_vec(e):
                for tc in range(4):
                    for do in range(NDC):
                        g = tc * NDC + do
                        kb.wgm(e, "p", f"b2_mm_{g}")
                        kb.wgm(e, f"sts{g % 2}", f"b2_sst_{g - 2}")
                        kb.inc(e.tensor_copy(sstg[g % 2][:], ps_wo[g % 2][:]), "v")
                        kb.mark(f"b2_sev_{g}", "v")

            def b2_scalar(e):
                for tc in range(4):
                    for do in range(NDC):
                        g = tc * NDC + do
                        kb.wgm(e, "v", f"b2_sev_{g}")
                        kb.inc(e.dma_start(
                            out=sscr[do * 128:(do + 1) * 128, tc * 512:(tc + 1) * 512],
                            in_=sstg[g % 2][:]), f"sts{g % 2}", 16)
                        kb.mark(f"b2_sst_{g}", f"sts{g % 2}")
                for s_ in ["sts0", "sts1"]:
                    kb.mark(f"b2_done_{s_}", s_)

            if upto >= 5:
                run_phase([("sync", b2_sync), ("gpsimd", b2_gp), ("tensor", b2_tensor),
                           ("vector", b2_vec), ("scalar", b2_scalar)])

        with ExitStack() as ctx:
            sb = lambda n, sh, d: ctx.enter_context(nc.sbuf_tensor(n, sh, d))
            ps = lambda n, sh, d: ctx.enter_context(nc.psum_tensor(n, sh, d))
            xn = sb("c_xn", [128, NDC * 1024], BF16)
            hdn = sb("c_hdn", [128, NIC * 1024], BF16)
            gw = [sb(f"c_gw{i}", [128, NDC * 128], BF16) for i in range(2)]
            uw = [sb(f"c_uw{i}", [128, NDC * 128], BF16) for i in range(2)]
            dw = [sb(f"c_dw{i}", [128, NIC * 128], BF16) for i in range(2)]
            gsb = sb("c_gs", [128, NIC], F32)
            usb = sb("c_us", [128, NIC], F32)
            dsb = sb("c_ds", [128, NDC], F32)
            sg = [sb(f"c_sg{i}", [128, 512], F32) for i in range(2)]
            su = [sb(f"c_su{i}", [128, 512], F32) for i in range(2)]
            tsil = [sb(f"c_tsil{i}", [128, 512], F32) for i in range(2)]
            ssb = [sb(f"c_ssb{i}", [128, 1024], F32) for i in range(2)]
            xfb = [sb(f"c_xfb{i}", [128, 1024], F32) for i in range(2)]
            ee = [sb(f"c_ee{i}", [128, 512], F32) for i in range(2)]
            ostg = [sb(f"c_ostg{i}", [128, 512], F32) for i in range(2)]
            ps_g = [ps(f"c_psg{i}", [128, 512], F32) for i in range(2)]
            ps_u = [ps(f"c_psu{i}", [128, 512], F32) for i in range(2)]
            ps_d = [ps(f"c_psd{i}", [128, 512], F32) for i in range(2)]
            ps_t = [ps(f"c_pst{i}", [128, 512], F32) for i in range(2)]

            def c_sync(e):
                kb.inc(e.dma_start(out=gsb[:], in_=gs_d), "qc", 16)
                kb.inc(e.dma_start(out=usb[:], in_=us_d), "qc", 16)
                kb.inc(e.dma_start(out=dsb[:], in_=ds_d), "qc", 16)
                kb.mark("c_sc", "qc")
                for s_ in ["stn0", "stn1", "stf0", "stf1"]:
                    kb.wgm(e, s_, f"a2_done_{s_}")
                for s_ in ["sts0", "sts1"]:
                    kb.wgm(e, s_, f"b2_done_{s_}")
                for ft in range(2):
                    kb.wgm(e, "p", f"c_gumm_{(ft - 1) * NIC + NIC - 1}_1")
                    kb.inc(e.dma_start(
                        out=r3(xn[:], NDC),
                        in_=dr3(xnscr)[:, :, ft * 1024:(ft + 1) * 1024]), "cxn", 16)
                    kb.mark(f"c_xn_{ft}", "cxn")
                    for do in range(NDC):
                        g = ft * NDC + do
                        sx_ = f"sx{do % 2}"
                        kb.wgm(e, "v", f"c_ef_{g - 2}_1")
                        kb.inc(e.dma_start(
                            out=ssb[do % 2][:],
                            in_=sscr[do * 128:(do + 1) * 128, ft * 1024:(ft + 1) * 1024]),
                            sx_, 16)
                        kb.inc(e.dma_start(
                            out=xfb[do % 2][:],
                            in_=xfscr[do * 128:(do + 1) * 128, ft * 1024:(ft + 1) * 1024]),
                            sx_, 16)
                        kb.mark(f"c_sx_{g}", sx_)

            def c_gp(e):
                for ft in range(2):
                    for i in range(NIC):
                        g = ft * NIC + i
                        sg_ = f"gu{i % 2}"
                        kb.wgm(e, "w16b", "a1_gu16")
                        kb.wgm(e, "p", f"c_gumm_{g - 2}_1")
                        kb.inc(e.dma_start(
                            out=r3(gw[i % 2][:], NDC),
                            in_=dr3(g16)[:, :, i * 128:(i + 1) * 128]), sg_, 16)
                        kb.inc(e.dma_start(
                            out=r3(uw[i % 2][:], NDC),
                            in_=dr3(u16)[:, :, i * 128:(i + 1) * 128]), sg_, 16)
                        kb.mark(f"c_guw_{g}", sg_)
                    for do in range(NDC):
                        g = ft * NDC + do
                        sd_ = f"dw{do % 2}"
                        kb.wgm(e, "w16b", "a1_gu16")
                        kb.wgm(e, "p", f"c_dmm_{g - 2}_1")
                        kb.inc(e.dma_start(
                            out=r3(dw[do % 2][:], NIC),
                            in_=dr3(d16)[:, :, do * 128:(do + 1) * 128]), sd_, 16)
                        kb.mark(f"c_dw_{g}", sd_)

            def c_tensor(e):
                for ft in range(2):
                    kb.wgm(e, "cxn", f"c_xn_{ft}")
                    for i in range(NIC):
                        g = ft * NIC + i
                        kb.wgm(e, f"gu{i % 2}", f"c_guw_{g}")
                        for half in range(2):
                            kb.wgm(e, "a", f"c_gu_{g - 1}_{half}")
                            for dc in range(NDC):
                                e.matmul(ps_g[half][:],
                                         gw[i % 2][:, dc * 128:(dc + 1) * 128],
                                         xn[:, dc * 1024 + half * 512:dc * 1024 + (half + 1) * 512],
                                         start=(dc == 0), stop=(dc == NDC - 1))
                            for dc in range(NDC):
                                mm_ = e.matmul(ps_u[half][:],
                                               uw[i % 2][:, dc * 128:(dc + 1) * 128],
                                               xn[:, dc * 1024 + half * 512:dc * 1024 + (half + 1) * 512],
                                               start=(dc == 0), stop=(dc == NDC - 1))
                            kb.inc(mm_, "p")
                            kb.mark(f"c_gumm_{g}_{half}", "p")
                    for do in range(NDC):
                        g = ft * NDC + do
                        kb.wgm(e, f"dw{do % 2}", f"c_dw_{g}")
                        kb.wgm(e, "v", f"c_hdn_{ft}")
                        for half in range(2):
                            kb.wgm(e, "a", f"c_sil_{g - 1}_{half}")
                            for ic in range(NIC):
                                mm_ = e.matmul(ps_d[half][:],
                                               dw[do % 2][:, ic * 128:(ic + 1) * 128],
                                               hdn[:, ic * 1024 + half * 512:ic * 1024 + (half + 1) * 512],
                                               start=(ic == 0), stop=(ic == NIC - 1))
                            kb.inc(mm_, "p")
                            kb.mark(f"c_dmm_{g}_{half}", "p")
                        for half in range(2):
                            kb.wgm(e, "v", f"c_ef_{g}_{half}")
                            kb.wgm(e, "v", f"c_ep_{g - 1}_{half}_3")
                            for u in range(4):
                                mm_ = e.transpose(ps_t[half][:, u * 128:(u + 1) * 128],
                                                  ee[half][:, u * 128:(u + 1) * 128],
                                                  i32[:])
                            kb.inc(mm_, "p")
                            kb.mark(f"c_tr_{g}_{half}", "p")

            def c_vec(e):
                for ft in range(2):
                    for i in range(NIC):
                        g = ft * NIC + i
                        for half in range(2):
                            kb.wgm(e, "a", f"c_gu_{g}_{half}")
                            kb.wgm(e, "p", f"c_dmm_{(ft - 1) * NDC + NDC - 1}_1")
                            kb.inc(e.tensor_mul(
                                hdn[:, i * 1024 + half * 512:i * 1024 + (half + 1) * 512],
                                sg[half][:], su[half][:]), "v")
                            kb.mark(f"c_hd_{g}_{half}", "v")
                    kb.mark(f"c_hdn_{ft}", "v")
                    for do in range(NDC):
                        g = ft * NDC + do
                        for half in range(2):
                            kb.wgm(e, "a", f"c_sil_{g}_{half}")
                            kb.wgm(e, f"sx{do % 2}", f"c_sx_{g}")
                            kb.wgm(e, "p", f"c_tr_{g - 1}_{half}")
                            ch(e, e.tensor_mul(ee[half][:],
                                               ssb[do % 2][:, half * 512:(half + 1) * 512],
                                               tsil[half][:]))
                            kb.selfw(e, "v")
                            kb.inc(e.tensor_add(ee[half][:], ee[half][:],
                                                xfb[do % 2][:, half * 512:(half + 1) * 512]),
                                   "v")
                            kb.mark(f"c_ef_{g}_{half}", "v")
                        for half in range(2):
                            kb.wgm(e, "p", f"c_tr_{g}_{half}")
                            kb.wgm(e, f"out{half}", f"c_ost_{g - 1}_{half}_3")
                            for j in range(4):
                                kb.inc(e.tensor_copy(
                                    ostg[half][:, j * 128:(j + 1) * 128],
                                    ps_t[half][:, j * 128:(j + 1) * 128]), "v")
                                kb.mark(f"c_ep_{g}_{half}_{j}", "v")

            def c_scalar(e):
                kb.wgm(e, "qc", "c_sc")
                for ft in range(2):
                    for i in range(NIC):
                        g = ft * NIC + i
                        for half in range(2):
                            kb.wgm(e, "p", f"c_gumm_{g}_{half}")
                            kb.wgm(e, "v", f"c_hd_{g - 1}_{half}")
                            e.activation(sg[half][:], ps_g[half][:], AF.Silu,
                                         bias=0.0, scale=gsb[:, i:i + 1])
                            kb.inc(e.activation(su[half][:], ps_u[half][:], AF.Copy,
                                                bias=0.0, scale=usb[:, i:i + 1]), "a")
                            kb.mark(f"c_gu_{g}_{half}", "a")
                    for do in range(NDC):
                        g = ft * NDC + do
                        for half in range(2):
                            kb.wgm(e, "p", f"c_dmm_{g}_{half}")
                            kb.wgm(e, "v", f"c_ef_{g - 1}_{half}")
                            kb.inc(e.activation(tsil[half][:], ps_d[half][:], AF.Silu,
                                                bias=0.0, scale=dsb[:, do:do + 1]), "a")
                            kb.mark(f"c_sil_{g}_{half}", "a")
                        for half in range(2):
                            for j in range(4):
                                kb.wgm(e, "v", f"c_ep_{g}_{half}_{j}")
                                row = ft * 1024 + half * 512 + j * 128
                                kb.inc(e.dma_start(
                                    out=out_d[row:row + 128, do * 128:(do + 1) * 128],
                                    in_=ostg[half][:, j * 128:(j + 1) * 128]),
                                    f"out{half}", 16)
                                kb.mark(f"c_ost_{g}_{half}_{j}", f"out{half}")
                for s_ in ["out0", "out1"]:
                    kb.mark(f"c_done_{s_}", s_)

            if upto >= 6:
                run_phase([("sync", c_sync), ("gpsimd", c_gp), ("tensor", c_tensor),
                           ("vector", c_vec), ("scalar", c_scalar)])

        def fin_sync(e):
            for s_ in ["out0", "out1"]:
                kb.wgm(e, s_, f"c_done_{s_}")

        run_phase([("sync", fin_sync)])

    return nc, kb


_BUILD_CACHE = {}


def kernel(**inputs):
    maps, metas = _prep(inputs)
    if "nc" not in _BUILD_CACHE:
        _BUILD_CACHE["nc"] = build(debug=False)[0]
    nc = _BUILD_CACHE["nc"]
    res = run_bass_kernel_spmd(nc, maps, core_ids=list(range(8)))
    y = np.empty((B_, S, D), dtype=np.float32)
    for c in range(8):
        y[c // 2][metas[c]] = res.results[c]["out"]
    return y


# version 14 (final)
# speedup vs baseline: 1.0934x; 1.0475x over previous
import math
import numpy as np
import ml_dtypes
from contextlib import ExitStack

import concourse.bass as bass
import concourse.mybir as mybir
from concourse.bass_utils import run_bass_kernel_spmd

F32 = mybir.dt.float32
BF16 = mybir.dt.bfloat16
AF = mybir.ActivationFunctionType
ALU = mybir.AluOpType
AX = mybir.AxisListType
BF = ml_dtypes.bfloat16

B_, S, D = 4, 4096, 2048
NH, DN, DR, DV = 16, 128, 64, 128
QR, KVR, INTER = 1536, 512, 5632
NDC = D // 128
NRQ = QR // 128
NRKV = KVR // 128
NIC = INTER // 128
SCALE = 1.0 / math.sqrt(DN + DR)
EPS_LN, EPS_RMS = 1e-5, 1e-6
NEG = -1.0e30
THETA = 10000.0
OWN = S // 2



def _own_blocks(par):
    return [j for j in range(32) if ((j // 4 + j % 4) % 2) == par]


def _rope_perm_q():
    perm = []
    for c in range(NH // 2):
        for h in (2 * c, 2 * c + 1):
            perm += [h * DR + 2 * i for i in range(32)]
            perm += [h * DR + 2 * i + 1 for i in range(32)]
    return perm


def _rope_perm_k():
    ev = [2 * i for i in range(32)]
    od = [2 * i + 1 for i in range(32)]
    return ev + od + ev + od


def _cs_table(positions):
    invf = THETA ** (-np.arange(0, DR, 2, dtype=np.float64) / DR)
    t = np.asarray(positions, dtype=np.float64)
    ang = t[None, :] * invf[:, None]
    c = np.cos(ang).astype(np.float32)
    s = np.sin(ang).astype(np.float32)
    return np.ascontiguousarray(np.concatenate([c, c, s, s], axis=0))


def _prep(inputs):
    a = lambda x: np.ascontiguousarray(np.asarray(x, dtype=np.float32))
    w = {}
    w["wqdT"] = a(np.asarray(inputs["L0_attn_wq_down"]).T)
    w["wquT"] = a(np.asarray(inputs["L0_attn_wq_up"]).T)
    w["wqrT"] = a(np.asarray(inputs["L0_attn_wq_rope"])[_rope_perm_q()].T)
    w["wkvdT"] = a(np.asarray(inputs["L0_attn_wkv_down"]).T)
    kvu = np.asarray(inputs["L0_attn_wkv_up"])
    k_rows = [h * (DN + DV) + d for h in range(NH) for d in range(DN)]
    v_rows = [h * (DN + DV) + DN + d for h in range(NH) for d in range(DV)]
    w["wkvukT"] = a(kvu[k_rows].T)
    w["wkvuvT"] = a(kvu[v_rows].T)
    w["wkrT"] = a(np.asarray(inputs["L0_attn_wk_rope"])[_rope_perm_k()].T)
    w["woT"] = a(np.asarray(inputs["L0_attn_wo"]).T)
    w["gateT"] = a(np.asarray(inputs["L0_ffn_gate_w"]).T)
    w["upT"] = a(np.asarray(inputs["L0_ffn_up_w"]).T)
    w["downT"] = a(np.asarray(inputs["L0_ffn_down_w"]).T)
    w["qnw"] = a(np.asarray(inputs["L0_attn_q_norm_w"]).reshape(NRQ, 128).T)
    w["kvnw"] = a(np.asarray(inputs["L0_attn_kv_norm_w"]).reshape(NRKV, 128).T)
    w["gs"] = a(np.asarray(inputs["L0_ffn_gate_s"]).reshape(NIC, 128).T)
    w["us"] = a(np.asarray(inputs["L0_ffn_up_s"]).reshape(NIC, 128).T)
    w["ds"] = a(np.asarray(inputs["L0_ffn_down_s"]).reshape(NDC, 128).T)
    w["cs_tab"] = _cs_table(np.arange(S))
    w["ident16"] = np.eye(128, dtype=BF)
    w["ident32"] = np.eye(128, dtype=np.float32)
    x = np.asarray(inputs["L0_attn_x"], dtype=np.float32)
    maps, metas = [], []
    for c in range(8):
        b, par = c // 2, c % 2
        blocks = _own_blocks(par)
        rows = np.concatenate([np.arange(j * 128, (j + 1) * 128) for j in blocks])
        m = dict(w)
        m["xb"] = np.ascontiguousarray(x[b])
        m["x_own"] = np.ascontiguousarray(x[b][rows])
        m["cs_q"] = _cs_table(rows)
        mask = np.empty((16 * 128, 512), dtype=np.float32)
        for si, j in enumerate(blocks):
            nc_ = j // 4 + 1
            kcol = (nc_ - 1) * 512 + np.arange(512)[None, :]
            qrow = j * 128 + np.arange(128)[:, None]
            mask[si * 128:(si + 1) * 128] = np.where(kcol <= qrow, 0.0, NEG)
        m["masks"] = mask
        maps.append(m)
        metas.append(rows)
    return maps, metas



class KB:
    SEMS = (["ph", "p", "v", "a", "gq", "qc", "qw",
             "xb0", "xb1", "stk0", "stk1", "stv0", "stv1",
             "stn0", "stn1", "stf0", "stf1",
             "xnf", "csq0", "csq1", "stg0", "stg1",
             "k00", "k01", "k10", "k11", "v00", "v01", "v10", "v11",
             "q0", "q1", "qm", "sto0", "sto1",
             "ob0", "ob1", "sts0", "sts1",
             "cxn", "sx0", "sx1", "gu0", "gu1", "dw0", "dw1", "w16a", "w16b",
             "out0", "out1"])

    def __init__(self):
        self.dry = True
        self.n = {s: 0 for s in self.SEMS}
        self.ms = {}
        self.sh = {}
        self.lastwait = {}

    def inc(self, instr, sem, k=1):
        self.n[sem] += k
        if not self.dry:
            instr.then_inc(self.sh[sem], k)

    def mark(self, key, sem):
        if self.dry:
            self.ms[key] = self.n[sem]

    def wg(self, eng, sem, val):
        if self.dry or val is None or val <= 0:
            return
        k = (id(eng), sem)
        if self.lastwait.get(k, -1) >= val:
            return
        self.lastwait[k] = val
        eng.wait_ge(self.sh[sem], val)

    def wgm(self, eng, sem, key):
        if self.dry:
            return
        self.wg(eng, sem, self.ms.get(key))

    def selfw(self, eng, sem):
        self.wg(eng, sem, self.n[sem])


class _DryInstr:
    def then_inc(self, *a, **k):
        return self


class DryEng:
    def __getattr__(self, name):
        return lambda *a, **k: _DryInstr()


def r3(ap2d, nchunks):
    return ap2d.rearrange("p (a f) -> p a f", a=nchunks)


def dr3(dram_ap, p=128):
    return dram_ap.rearrange("(a p) f -> p a f", p=p)



def build(debug=False, upto=99):
    nc = bass.Bass("TRN2", target_bir_lowering=False)
    dt = lambda name, shape, dtype, kind: nc.dram_tensor(name, shape, dtype, kind=kind).ap()

    xb_d = dt("xb", [S, D], F32, "ExternalInput")
    xown_d = dt("x_own", [OWN, D], F32, "ExternalInput")
    csq_d = dt("cs_q", [128, OWN], F32, "ExternalInput")
    cst_d = dt("cs_tab", [128, S], F32, "ExternalInput")
    mask_d = dt("masks", [16 * 128, 512], F32, "ExternalInput")
    wqdT_d = dt("wqdT", [D, QR], F32, "ExternalInput")
    wquT_d = dt("wquT", [QR, D], F32, "ExternalInput")
    wqrT_d = dt("wqrT", [QR, NH * DR], F32, "ExternalInput")
    wkvdT_d = dt("wkvdT", [D, KVR], F32, "ExternalInput")
    wkvukT_d = dt("wkvukT", [KVR, NH * DN], F32, "ExternalInput")
    wkvuvT_d = dt("wkvuvT", [KVR, NH * DV], F32, "ExternalInput")
    wkrT_d = dt("wkrT", [D, 128], F32, "ExternalInput")
    woT_d = dt("woT", [D, D], F32, "ExternalInput")
    gateT_d = dt("gateT", [D, INTER], F32, "ExternalInput")
    upT_d = dt("upT", [D, INTER], F32, "ExternalInput")
    downT_d = dt("downT", [INTER, D], F32, "ExternalInput")
    qnw_d = dt("qnw", [128, NRQ], F32, "ExternalInput")
    kvnw_d = dt("kvnw", [128, NRKV], F32, "ExternalInput")
    gs_d = dt("gs", [128, NIC], F32, "ExternalInput")
    us_d = dt("us", [128, NIC], F32, "ExternalInput")
    ds_d = dt("ds", [128, NDC], F32, "ExternalInput")
    i16_d = dt("ident16", [128, 128], BF16, "ExternalInput")
    i32_d = dt("ident32", [128, 128], F32, "ExternalInput")
    out_d = dt("out", [OWN, D], F32, "ExternalOutput")

    sk = "ExternalOutput" if debug else "Internal"
    kscr = dt("kscr", [NH * DN, S], BF16, sk)
    vscr = dt("vscr", [NH * 32 * 128, 128], BF16, sk)
    qnscr = dt("qnscr", [NH * DN, OWN], BF16, sk)
    qpscr = dt("qpscr", [(NH // 2) * 128, OWN], BF16, sk)
    xnscr = dt("xnscr", [D, OWN], BF16, sk)
    xfscr = dt("xfscr", [D, OWN], F32, sk)
    oscr = dt("oscr", [NH * DV, OWN], BF16, sk)
    sscr = dt("sscr", [D, OWN], F32, sk)

    kb = KB()

    with ExitStack() as gctx:
        kro = gctx.enter_context(nc.sbuf_tensor("kro", [128, S], BF16))
        i16 = gctx.enter_context(nc.sbuf_tensor("i16", [128, 128], BF16))
        i32 = gctx.enter_context(nc.sbuf_tensor("i32", [128, 128], F32))
        ones_c = gctx.enter_context(nc.sbuf_tensor("ones_c", [128, 1], F32))
        ones_r = gctx.enter_context(nc.sbuf_tensor("ones_r", [1, 128], F32))
        for s in KB.SEMS:
            kb.sh[s] = gctx.enter_context(nc.semaphore("sem_" + s))
        blk = gctx.enter_context(nc.Block())

        phase_idx = [0]

        DMA_SEMS = [s for s in KB.SEMS if s not in ("ph", "p", "v", "a", "gq")]

        OWN_SEM = {"sync": None, "gpsimd": "gq", "tensor": "p",
                   "vector": "v", "scalar": "a"}

        def run_phase(sections):
            def wrap(name, fn):
                own = OWN_SEM[name]

                def g(e):
                    bw(e)
                    pre = dict(kb.n)
                    fn(e)
                    for s_ in DMA_SEMS:
                        if kb.n[s_] != pre[s_]:
                            kb.selfw(e, s_)
                    if own:
                        kb.selfw(e, own)
                    bi(e)
                return g
            wsecs = [(n_, wrap(n_, f_)) for n_, f_ in sections]
            snap = dict(kb.n)
            kb.dry = True
            for n_, f_ in wsecs:
                f_(DryEng())
            kb.n = dict(snap)
            kb.dry = False
            for n_, f_ in wsecs:
                getattr(blk, n_)(f_)
            kb.dry = True
            phase_idx[0] += 1

        def bw(eng):
            kb.wg(eng, "ph", 5 * phase_idx[0])

        def bi(eng):
            kb.inc(eng.nop(), "ph")

        def ch(e, instr, sem="v"):
            kb.inc(instr, sem)

        def ln_stats_vec(e, xb_ap, xsq_ap, stt_h, tag):
            stt = {k: v[:] for k, v in stt_h.items()}
            kb.selfw(e, "v")
            ch(e, e.reduce_sum(stt["s1"], xb_ap, AX.X))
            kb.wgm(e, "a", f"{tag}_xsq")
            ch(e, e.reduce_sum(stt["s2"], xsq_ap, AX.X))
            kb.selfw(e, "v")
            ch(e, e.tensor_scalar(stt["mean"], stt["s1"], 1.0 / D, None,
                                  ALU.mult, ALU.bypass))
            ch(e, e.tensor_scalar(stt["s2"], stt["s2"], 1.0 / D, None,
                                  ALU.mult, ALU.bypass))
            kb.selfw(e, "v")
            ch(e, e.tensor_mul(stt["t2"], stt["mean"], stt["mean"]))
            kb.selfw(e, "v")
            ch(e, e.scalar_tensor_tensor(stt["u"], stt["s2"], EPS_LN, stt["t2"],
                                         ALU.add, ALU.subtract))
            kb.selfw(e, "v")
            kb.inc(e.reciprocal(stt["w1"], stt["u"]), "v")
            kb.mark(f"{tag}_w1", "v")
            kb.wgm(e, "a", f"{tag}_r0")
            ch(e, e.tensor_mul(stt["t2"], stt["r0"], stt["r0"]))
            kb.selfw(e, "v")
            ch(e, e.tensor_mul(stt["t2"], stt["t2"], stt["u"]))
            kb.selfw(e, "v")
            ch(e, e.tensor_scalar(stt["t2"], stt["t2"], -0.5, 1.5, ALU.mult, ALU.add))
            kb.selfw(e, "v")
            ch(e, e.tensor_mul(stt["rstd"], stt["r0"], stt["t2"]))
            kb.selfw(e, "v")
            ch(e, e.tensor_mul(stt["nb"], stt["mean"], stt["rstd"]))
            kb.selfw(e, "v")
            kb.inc(e.tensor_scalar(stt["nb"], stt["nb"], -1.0, None, ALU.mult,
                                   ALU.bypass), "v")
            kb.mark(f"{tag}_ln", "v")

        def rms_row_vec(e, R, ps_st, rank, tag):
            kb.selfw(e, "v")
            ch(e, e.tensor_scalar(R(0), ps_st[:], 1.0 / rank, EPS_RMS, ALU.mult, ALU.add))
            kb.selfw(e, "v")
            kb.inc(e.reciprocal(R(1), R(0)), "v")
            kb.mark(f"{tag}_w1", "v")
            kb.wgm(e, "a", f"{tag}_r0")
            ch(e, e.tensor_mul(R(3), R(2), R(2)))
            kb.selfw(e, "v")
            ch(e, e.tensor_mul(R(3), R(3), R(0)))
            kb.selfw(e, "v")
            ch(e, e.tensor_scalar(R(3), R(3), -0.5, 1.5, ALU.mult, ALU.add))
            kb.selfw(e, "v")
            kb.inc(e.tensor_mul(R(4), R(2), R(3)), "v")
            kb.mark(f"{tag}_rstd", "v")

        def rope_apply(e, dst, ps_ap, cc, ss, tmp1, tmp2):
            t1, t2 = tmp1[0:32, :], tmp2[0:32, :]
            last = None
            for half in (0, 64):
                xe = ps_ap[half:half + 32, :]
                xo = ps_ap[half + 32:half + 64, :]
                kb.selfw(e, "v")
                ch(e, e.tensor_mul(t1, xe, cc))
                ch(e, e.tensor_mul(t2, xo, ss))
                kb.selfw(e, "v")
                ch(e, e.tensor_sub(dst[half:half + 32, :], t1, t2))
                kb.selfw(e, "v")
                ch(e, e.tensor_mul(t1, xo, cc))
                ch(e, e.tensor_mul(t2, xe, ss))
                kb.selfw(e, "v")
                last = e.tensor_add(dst[half + 32:half + 64, :], t1, t2)
                if half == 0:
                    ch(e, last)
            return last

        def ph0_sync(e):
            kb.inc(e.dma_start(out=i16[:], in_=i16_d), "qc", 16)
            kb.inc(e.dma_start(out=i32[:], in_=i32_d), "qc", 16)
            kb.mark("consts", "qc")

        def ph0_vec(e):
            kb.wgm(e, "qc", "consts")
            e.memset(ones_c[:], 1.0)
            kb.inc(e.memset(ones_r[:], 1.0), "v")
            kb.mark("ones", "v")

        def ph0_other(e):
            pass

        run_phase([("sync", ph0_sync), ("vector", ph0_vec), ("tensor", ph0_other),
                   ("scalar", ph0_other), ("gpsimd", ph0_other)])

        with ExitStack() as ctx:
            sb = lambda n, sh, d: ctx.enter_context(nc.sbuf_tensor(n, sh, d))
            ps = lambda n, sh, d: ctx.enter_context(nc.psum_tensor(n, sh, d))
            xbuf = [sb(f"a1_xbuf{i}", [128, D], F32) for i in range(2)]
            xsq = [sb(f"a1_xsq{i}", [128, D], F32) for i in range(2)]
            xnt = [sb(f"a1_xnt{i}", [128, D], BF16) for i in range(2)]
            stt_t = [{k: sb(f"a1_st_{k}{i}", [128, 1], F32)
                      for k in ["s1", "s2", "mean", "u", "w1", "r0", "t2", "rstd", "nb"]}
                     for i in range(2)]
            xnf = sb("a1_xnf", [128, NDC * 512], BF16)
            ykv = sb("a1_ykv", [128, NRKV * 512], F32)
            ropesc = [sb(f"a1_rsc{i}", [64, 512], F32) for i in range(2)]
            sqkv = [sb(f"a1_sqkv{i}", [128, 512], F32) for i in range(2)]
            kvcn = sb("a1_kvcn", [128, NRKV * 512], BF16)
            rrow = sb("a1_rrow", [1, 512 * 5], F32)
            Rr = lambda i: rrow[:, i * 512:(i + 1) * 512]
            kst = [sb(f"a1_kst{i}", [128, 512], BF16) for i in range(2)]
            vst = [sb(f"a1_vst{i}", [128, NH * 128], BF16) for i in range(2)]
            wkvd = sb("a1_wkvd", [128, NDC * KVR], BF16)
            wkvuk = sb("a1_wkvuk", [128, NRKV * 2048], BF16)
            wkvuv = sb("a1_wkvuv", [128, NRKV * 2048], BF16)
            wkr = sb("a1_wkr", [128, NDC * 128], BF16)
            kvnw = sb("a1_kvnw", [128, NRKV], F32)
            cst = sb("a1_cst", [128, S], F32)
            ps_tr = [ps(f"a1_pstr{i}", [128, 512], BF16) for i in range(2)]
            ps_mm = [ps(f"a1_psmm{i}", [128, 512], F32) for i in range(3)]
            ps_st = ps("a1_psst", [1, 512], F32)
            ps_bc = ps("a1_psbc", [128, 512], F32)

            def mm_jobs():
                out = [("kv", rc) for rc in range(NRKV)]
                out.append(("kr",))
                out += [("k", hc) for hc in range(NH)]
                out += [("v", kt, dvc) for kt in range(4) for dvc in range(4)]
                return out

            JPC = 37

            def a1_sync(e):
                kb.inc(e.dma_start(out=cst[:], in_=cst_d), "qc", 16)
                kb.inc(e.dma_start(out=kvnw[:], in_=kvnw_d), "qc", 16)
                kb.mark("a1_cst", "qc")
                for bt in range(32):
                    kb.wgm(e, "a", f"a1t{bt - 2}_xn")
                    sx = f"xb{bt % 2}"
                    kb.inc(e.dma_start(out=xbuf[bt % 2][:],
                                       in_=xb_d[bt * 128:(bt + 1) * 128, :]), sx, 16)
                    kb.mark(f"a1_x_{bt}", sx)

            def a1_gp(e):
                kb.inc(e.dma_start(out=r3(wkvd[:], NDC), in_=dr3(wkvdT_d)), "qw", 16)
                kb.inc(e.dma_start(out=r3(wkvuk[:], NRKV), in_=dr3(wkvukT_d)), "qw", 16)
                kb.inc(e.dma_start(out=r3(wkvuv[:], NRKV), in_=dr3(wkvuvT_d)), "qw", 16)
                kb.inc(e.dma_start(out=r3(wkr[:], NDC), in_=dr3(wkrT_d)), "qw", 16)
                kb.mark("a1_w", "qw")

            def a1_tensor(e):
                kb.wgm(e, "qw", "a1_w")
                for c in range(8):
                    for t in range(4):
                        bt = 4 * c + t
                        kb.wgm(e, "a", f"a1t{bt}_xn")
                        for q in range(4):
                            tb = bt * 4 + q
                            kb.wgm(e, "v", f"a1_trev_{tb - 2}")
                            for u in range(4):
                                dc = q * 4 + u
                                mm_ = e.transpose(ps_tr[tb % 2][:, u * 128:(u + 1) * 128],
                                                  xnt[bt % 2][:, dc * 128:(dc + 1) * 128],
                                                  i16[:])
                            kb.inc(mm_, "p")
                            kb.mark(f"a1_tr_{tb}", "p")
                    kb.wgm(e, "v", f"a1_xnf_{c}")
                    for i, job in enumerate(mm_jobs()):
                        g = c * JPC + i
                        slot = ps_mm[g % 3]
                        if job[0] == "k" and job[1] == 0:
                            kb.wgm(e, "v", f"a1_rmsrow{c - 1}_rstd")
                            for rc in range(NRKV):
                                kb.wgm(e, "a", f"a1_sq_{c}_{rc}")
                                mm_ = e.matmul(ps_st[:], ones_c[:], sqkv[rc % 2][:],
                                               start=(rc == 0), stop=(rc == NRKV - 1))
                                kb.inc(mm_, "p")
                                kb.mark(f"a1_stp_{c}_{rc}", "p")
                            kb.mark(f"a1_stmm_{c}", "p")
                            kb.wgm(e, "v", f"a1_rmsrow{c}_rstd")
                            kb.wgm(e, "v", f"a1_kvcn_{c - 1}")
                            kb.inc(e.matmul(ps_bc[:], ones_r[:], Rr(4),
                                            start=True, stop=True), "p")
                            kb.mark(f"a1_bc_{c}", "p")
                        kb.wgm(e, "v", f"a1_mmev_{g - 3}")
                        if job[0] == "kv":
                            rc = job[1]
                            for dc in range(NDC):
                                mm_ = e.matmul(slot[:],
                                               wkvd[:, dc * KVR + rc * 128:dc * KVR + (rc + 1) * 128],
                                               xnf[:, dc * 512:(dc + 1) * 512],
                                               start=(dc == 0), stop=(dc == NDC - 1))
                        elif job[0] == "kr":
                            for dc in range(NDC):
                                mm_ = e.matmul(slot[:],
                                               wkr[:, dc * 128:(dc + 1) * 128],
                                               xnf[:, dc * 512:(dc + 1) * 512],
                                               start=(dc == 0), stop=(dc == NDC - 1))
                        elif job[0] == "k":
                            hc = job[1]
                            if hc == 0:
                                kb.wgm(e, "v", f"a1_kvcn_{c}")
                            for rc in range(NRKV):
                                mm_ = e.matmul(slot[:],
                                               wkvuk[:, rc * 2048 + hc * 128:rc * 2048 + (hc + 1) * 128],
                                               kvcn[:, rc * 512:(rc + 1) * 512],
                                               start=(rc == 0), stop=(rc == NRKV - 1))
                        else:
                            _, kt, dvc = job
                            for rc in range(NRKV):
                                mm_ = e.matmul(slot[:],
                                               kvcn[:, rc * 512 + kt * 128:rc * 512 + (kt + 1) * 128],
                                               wkvuv[:, rc * 2048 + dvc * 512:rc * 2048 + (dvc + 1) * 512],
                                               start=(rc == 0), stop=(rc == NRKV - 1))
                        kb.inc(mm_, "p")
                        kb.mark(f"a1_mm_{g}", "p")

            def a1_vec(e):
                kb.wgm(e, "v", "ones")
                for c in range(8):
                    for t in range(4):
                        bt = 4 * c + t
                        kb.wgm(e, f"xb{bt % 2}", f"a1_x_{bt}")
                        ln_stats_vec(e, xbuf[bt % 2][:], xsq[bt % 2][:],
                                     stt_t[bt % 2], f"a1t{bt}")
                        for q in range(4):
                            tb = bt * 4 + q
                            kb.wgm(e, "p", f"a1_tr_{tb}")
                            kb.wgm(e, "p", f"a1_mm_{(c - 1) * JPC + 4}")
                            kb.inc(e.tensor_copy(
                                r3(xnf[:], NDC)[:, q * 4:(q + 1) * 4, t * 128:(t + 1) * 128],
                                r3(ps_tr[tb % 2][:], 4)), "v")
                            kb.mark(f"a1_trev_{tb}", "v")
                    kb.mark(f"a1_xnf_{c}", "v")
                    for i, job in enumerate(mm_jobs()):
                        g = c * JPC + i
                        slot = ps_mm[g % 3]
                        kb.wgm(e, "p", f"a1_mm_{g}")
                        if job[0] == "kv":
                            rc = job[1]
                            kb.inc(e.tensor_copy(ykv[:, rc * 512:(rc + 1) * 512], slot[:]),
                                   "v")
                            kb.mark(f"a1_ykv_{c}_{rc}", "v")
                            if rc == NRKV - 1:
                                pass
                        elif job[0] == "kr":
                            kb.wgm(e, "p", f"a1_stmm_{c}")
                            rms_row_vec(e, Rr, ps_st, KVR, f"a1_rmsrow{c}")
                            kb.wgm(e, "p", f"a1_bc_{c}")
                            kb.wgm(e, "qc", "a1_cst")
                            kb.selfw(e, "v")
                            for rc in range(NRKV):
                                kb.inc(e.scalar_tensor_tensor(
                                    kvcn[:, rc * 512:(rc + 1) * 512],
                                    ykv[:, rc * 512:(rc + 1) * 512],
                                    kvnw[:, rc:rc + 1], ps_bc[:],
                                    ALU.mult, ALU.mult), "v")
                            kb.mark(f"a1_kvcn_{c}", "v")
                            cc = cst[0:32, c * 512:(c + 1) * 512]
                            ss = cst[64:96, c * 512:(c + 1) * 512]
                            kb.inc(rope_apply(e, kro[:, c * 512:(c + 1) * 512], slot[:],
                                              cc, ss, ropesc[0][:], ropesc[1][:]), "v")
                            kb.mark(f"a1_kro_{c}", "v")
                        elif job[0] == "k":
                            hc = job[1]
                            kidx = c * NH + hc
                            kb.wgm(e, f"stk{kidx % 2}", f"a1_kstf_{kidx - 2}")
                            kb.inc(e.tensor_copy(kst[hc % 2][:], slot[:]), "v")
                            kb.mark(f"a1_kev_{kidx}", "v")
                        else:
                            _, kt, dvc = job
                            vidx = c * 4 + kt
                            kb.wgm(e, f"stv{vidx % 2}", f"a1_vstf_{vidx - 2}")
                            kb.inc(e.tensor_copy(
                                vst[kt % 2][:, dvc * 512:(dvc + 1) * 512], slot[:]), "v")
                            if dvc == 3:
                                kb.mark(f"a1_vrdy_{vidx}", "v")
                        kb.mark(f"a1_mmev_{g}", "v")

            def a1_scalar(e):
                for c in range(8):
                    for t in range(4):
                        bt = 4 * c + t
                        kb.wgm(e, f"xb{bt % 2}", f"a1_x_{bt}")
                        kb.wgm(e, "v", f"a1t{bt - 2}_w1")
                        kb.inc(e.activation(xsq[bt % 2][:], xbuf[bt % 2][:], AF.Square),
                               "a")
                        kb.mark(f"a1t{bt}_xsq", "a")
                        kb.wgm(e, "v", f"a1t{bt}_w1")
                        kb.inc(e.sqrt(stt_t[bt % 2]["r0"][:], stt_t[bt % 2]["w1"][:]), "a")
                        kb.mark(f"a1t{bt}_r0", "a")
                        kb.wgm(e, "v", f"a1t{bt}_ln")
                        kb.wgm(e, "p", f"a1_tr_{(bt - 2) * 4 + 3}")
                        kb.inc(e.activation(xnt[bt % 2][:], xbuf[bt % 2][:], AF.Identity,
                                            bias=stt_t[bt % 2]["nb"][:],
                                            scale=stt_t[bt % 2]["rstd"][:]), "a")
                        kb.mark(f"a1t{bt}_xn", "a")
                    for rc in range(NRKV):
                        kb.wgm(e, "v", f"a1_ykv_{c}_{rc}")
                        if rc >= 2:
                            kb.wgm(e, "p", f"a1_stp_{c}_{rc - 2}")
                        else:
                            kb.wgm(e, "p", f"a1_stp_{c - 1}_{rc + 2}")
                        kb.inc(e.activation(sqkv[rc % 2][:], ykv[:, rc * 512:(rc + 1) * 512],
                                            AF.Square), "a")
                        kb.mark(f"a1_sq_{c}_{rc}", "a")
                    kb.wgm(e, "v", f"a1_rmsrow{c}_w1")
                    kb.inc(e.sqrt(Rr(2), Rr(1)), "a")
                    kb.mark(f"a1_rmsrow{c}_r0", "a")
                    for hc in range(NH):
                        kidx = c * NH + hc
                        kb.wgm(e, "v", f"a1_kev_{kidx}")
                        kb.inc(e.dma_start(
                            out=kscr[hc * 128:(hc + 1) * 128, c * 512:(c + 1) * 512],
                            in_=kst[hc % 2][:]), f"stk{kidx % 2}", 16)
                        kb.mark(f"a1_kstf_{kidx}", f"stk{kidx % 2}")
                    for kt in range(4):
                        vidx = c * 4 + kt
                        kb.wgm(e, "v", f"a1_vrdy_{vidx}")
                        ktg = c * 4 + kt
                        for h in range(NH):
                            kb.inc(e.dma_start(
                                out=vscr[(h * 32 + ktg) * 128:(h * 32 + ktg + 1) * 128, :],
                                in_=vst[kt % 2][:, h * 128:(h + 1) * 128]),
                                f"stv{vidx % 2}", 16)
                        kb.mark(f"a1_vstf_{vidx}", f"stv{vidx % 2}")
                for s_ in ["stk0", "stk1", "stv0", "stv1"]:
                    kb.mark(f"a1_done_{s_}", s_)

            if upto >= 1:
                run_phase([("sync", a1_sync), ("gpsimd", a1_gp), ("tensor", a1_tensor),
                           ("vector", a1_vec), ("scalar", a1_scalar)])

        with ExitStack() as ctx:
            sb = lambda n, sh, d: ctx.enter_context(nc.sbuf_tensor(n, sh, d))
            ps = lambda n, sh, d: ctx.enter_context(nc.psum_tensor(n, sh, d))
            xbuf = [sb(f"a2_xbuf{i}", [128, D], F32) for i in range(2)]
            xsq = [sb(f"a2_xsq{i}", [128, D], F32) for i in range(2)]
            xnt = [sb(f"a2_xnt{i}", [128, D], BF16) for i in range(2)]
            stt_t = [{k: sb(f"a2_st_{k}{i}", [128, 1], F32)
                      for k in ["s1", "s2", "mean", "u", "w1", "r0", "t2", "rstd", "nb"]}
                     for i in range(2)]
            xnstg = [sb(f"a2_xnstg{i}", [128, 512], BF16) for i in range(2)]
            xfstg = [sb(f"a2_xfstg{i}", [128, 512], F32) for i in range(2)]
            ps_tr = [ps(f"a2_pstr{i}", [128, 512], BF16) for i in range(2)]
            ps_tr32 = [ps(f"a2_pstr32{i}", [128, 512], F32) for i in range(2)]

            def a2a_sync(e):
                for bt in range(16):
                    kb.wgm(e, "a", f"a2t{bt - 2}_xn")
                    kb.wgm(e, "p", f"a2_tf_{(bt - 2) * 4 + 3}")
                    sx = f"xb{bt % 2}"
                    kb.inc(e.dma_start(out=xbuf[bt % 2][:],
                                       in_=xown_d[bt * 128:(bt + 1) * 128, :]), sx, 16)
                    kb.mark(f"a2_x_{bt}", sx)

            def a2a_tensor(e):
                for bt in range(16):
                    kb.wgm(e, "a", f"a2t{bt}_xn")
                    for q in range(4):
                        tb = bt * 4 + q
                        kb.wgm(e, "v", f"a2_trev_{tb - 2}")
                        for u in range(4):
                            dc = q * 4 + u
                            mm_ = e.transpose(ps_tr[tb % 2][:, u * 128:(u + 1) * 128],
                                              xnt[bt % 2][:, dc * 128:(dc + 1) * 128],
                                              i16[:])
                        kb.inc(mm_, "p")
                        kb.mark(f"a2_tr_{tb}", "p")
                        kb.wgm(e, "v", f"a2_tfev_{tb - 2}")
                        for u in range(4):
                            dc = q * 4 + u
                            mm_ = e.transpose(ps_tr32[tb % 2][:, u * 128:(u + 1) * 128],
                                              xbuf[bt % 2][:, dc * 128:(dc + 1) * 128],
                                              i32[:])
                        kb.inc(mm_, "p")
                        kb.mark(f"a2_tf_{tb}", "p")

            def a2a_vec(e):
                for bt in range(16):
                    kb.wgm(e, f"xb{bt % 2}", f"a2_x_{bt}")
                    ln_stats_vec(e, xbuf[bt % 2][:], xsq[bt % 2][:], stt_t[bt % 2],
                                 f"a2t{bt}")
                    for q in range(4):
                        tb = bt * 4 + q
                        kb.wgm(e, "p", f"a2_tr_{tb}")
                        kb.wgm(e, f"stn{tb % 2}", f"a2_xnst_{tb - 2}")
                        kb.inc(e.tensor_copy(xnstg[tb % 2][:], ps_tr[tb % 2][:]), "v")
                        kb.mark(f"a2_trev_{tb}", "v")
                        kb.wgm(e, "p", f"a2_tf_{tb}")
                        kb.wgm(e, f"stf{tb % 2}", f"a2_xfst_{tb - 2}")
                        kb.inc(e.tensor_copy(xfstg[tb % 2][:], ps_tr32[tb % 2][:]), "v")
                        kb.mark(f"a2_tfev_{tb}", "v")

            def a2a_scalar(e):
                for bt in range(16):
                    oc, t = bt // 4, bt % 4
                    kb.wgm(e, f"xb{bt % 2}", f"a2_x_{bt}")
                    kb.wgm(e, "v", f"a2t{bt - 2}_w1")
                    kb.inc(e.activation(xsq[bt % 2][:], xbuf[bt % 2][:], AF.Square), "a")
                    kb.mark(f"a2t{bt}_xsq", "a")
                    kb.wgm(e, "v", f"a2t{bt}_w1")
                    kb.inc(e.sqrt(stt_t[bt % 2]["r0"][:], stt_t[bt % 2]["w1"][:]), "a")
                    kb.mark(f"a2t{bt}_r0", "a")
                    kb.wgm(e, "v", f"a2t{bt}_ln")
                    kb.wgm(e, "p", f"a2_tr_{(bt - 2) * 4 + 3}")
                    kb.inc(e.activation(xnt[bt % 2][:], xbuf[bt % 2][:], AF.Identity,
                                        bias=stt_t[bt % 2]["nb"][:],
                                        scale=stt_t[bt % 2]["rstd"][:]), "a")
                    kb.mark(f"a2t{bt}_xn", "a")
                    for q in range(4):
                        tb = bt * 4 + q
                        kb.wgm(e, "v", f"a2_trev_{tb}")
                        dst = dr3(xnscr)
                        kb.inc(e.dma_start(
                            out=dst[:, q * 4:(q + 1) * 4, bt * 128:(bt + 1) * 128],
                            in_=r3(xnstg[tb % 2][:], 4)), f"stn{tb % 2}", 16)
                        kb.mark(f"a2_xnst_{tb}", f"stn{tb % 2}")
                        kb.wgm(e, "v", f"a2_tfev_{tb}")
                        dstf = dr3(xfscr)
                        kb.inc(e.dma_start(
                            out=dstf[:, q * 4:(q + 1) * 4, bt * 128:(bt + 1) * 128],
                            in_=r3(xfstg[tb % 2][:], 4)), f"stf{tb % 2}", 16)
                        kb.mark(f"a2_xfst_{tb}", f"stf{tb % 2}")
                for s_ in ["stn0", "stn1", "stf0", "stf1"]:
                    kb.mark(f"a2_done_{s_}", s_)

            def a2a_gp(e):
                pass

            if upto >= 2:
                run_phase([("sync", a2a_sync), ("gpsimd", a2a_gp), ("tensor", a2a_tensor),
                           ("vector", a2a_vec), ("scalar", a2a_scalar)])

        with ExitStack() as ctx:
            sb = lambda n, sh, d: ctx.enter_context(nc.sbuf_tensor(n, sh, d))
            ps = lambda n, sh, d: ctx.enter_context(nc.psum_tensor(n, sh, d))
            wqd = sb("a2b_wqd", [128, NDC * QR], BF16)
            wqu = sb("a2b_wqu", [128, NRQ * D], BF16)
            wqr = sb("a2b_wqr", [128, NRQ * NH * DR], BF16)
            qnw = sb("a2b_qnw", [128, NRQ], F32)
            csq = [sb(f"a2b_csq{i}", [128, 512], F32) for i in range(2)]
            xnfi = sb("a2b_xnfi", [128, NDC * 512], BF16)
            yq = sb("a2b_yq", [128, NRQ * 512], F32)
            sqq = [sb(f"a2b_sq{i}", [128, 512], F32) for i in range(2)]
            qcn = sb("a2b_qcn", [128, NRQ * 512], BF16)
            rrow = sb("a2b_rrow", [1, 512 * 5], F32)
            Rr = lambda i: rrow[:, i * 512:(i + 1) * 512]
            qst = [sb(f"a2b_qst{i}", [128, 512], BF16) for i in range(2)]
            ropesc = [sb(f"a2b_rsc{i}", [64, 512], F32) for i in range(2)]
            ps_mm = [ps(f"a2b_psmm{i}", [128, 512], F32) for i in range(4)]
            ps_st = ps("a2b_psst", [1, 512], F32)
            ps_bc = ps("a2b_psbc", [128, 512], F32)

            def q_jobs():
                return ([("qc", rc) for rc in range(NRQ)]
                        + [("qn", hc) for hc in range(NH)]
                        + [("qp", pc) for pc in range(NH // 2)])

            JQ = 36

            def a2b_sync(e):
                kb.inc(e.dma_start(out=qnw[:], in_=qnw_d), "qc", 16)
                kb.mark("a2b_c", "qc")
                for oc in range(4):
                    kb.wgm(e, "p", f"a2b_mm_{(oc - 1) * JQ + JQ - 1}")
                    for s_ in ["stn0", "stn1"]:
                        kb.wgm(e, s_, f"a2_done_{s_}")
                    kb.inc(e.dma_start(
                        out=r3(xnfi[:], NDC),
                        in_=dr3(xnscr)[:, :, oc * 512:(oc + 1) * 512]), "xnf", 16)
                    kb.mark(f"a2b_xn_{oc}", "xnf")
                    kb.wgm(e, "v", f"a2b_qpev_{(oc - 2) * 24 + 16 + 7}")
                    sc_ = f"csq{oc % 2}"
                    kb.inc(e.dma_start(out=csq[oc % 2][:],
                                       in_=csq_d[:, oc * 512:(oc + 1) * 512]), sc_, 16)
                    kb.mark(f"a2b_csq_{oc}", sc_)

            def a2b_gp(e):
                kb.inc(e.dma_start(out=r3(wqd[:], NDC), in_=dr3(wqdT_d)), "qw", 16)
                kb.inc(e.dma_start(out=r3(wqu[:], NRQ), in_=dr3(wquT_d)), "qw", 16)
                kb.inc(e.dma_start(out=r3(wqr[:], NRQ), in_=dr3(wqrT_d)), "qw", 16)
                kb.mark("a2b_w", "qw")

            def a2b_tensor(e):
                kb.wgm(e, "qw", "a2b_w")
                for oc in range(4):
                    kb.wgm(e, "xnf", f"a2b_xn_{oc}")
                    for i, job in enumerate(q_jobs()):
                        g = oc * JQ + i
                        slot = ps_mm[g % 4]
                        if job[0] == "qn" and job[1] == 0:
                            kb.wgm(e, "v", f"a2b_rms{oc - 1}_rstd")
                            for rc in range(NRQ):
                                kb.wgm(e, "a", f"a2b_sq_{oc}_{rc}")
                                mm_ = e.matmul(ps_st[:], ones_c[:], sqq[rc % 2][:],
                                               start=(rc == 0), stop=(rc == NRQ - 1))
                                kb.inc(mm_, "p")
                                kb.mark(f"a2b_stp_{oc}_{rc}", "p")
                            kb.mark(f"a2b_stmm_{oc}", "p")
                            kb.wgm(e, "v", f"a2b_rms{oc}_rstd")
                            kb.wgm(e, "v", f"a2b_qcn_{oc - 1}")
                            kb.inc(e.matmul(ps_bc[:], ones_r[:], Rr(4),
                                            start=True, stop=True), "p")
                            kb.mark(f"a2b_bc_{oc}", "p")
                        kb.wgm(e, "v", f"a2b_mmev_{g - 4}")
                        if job[0] == "qc":
                            rc = job[1]
                            for dc in range(NDC):
                                mm_ = e.matmul(slot[:],
                                               wqd[:, dc * QR + rc * 128:dc * QR + (rc + 1) * 128],
                                               xnfi[:, dc * 512:(dc + 1) * 512],
                                               start=(dc == 0), stop=(dc == NDC - 1))
                        elif job[0] == "qn":
                            hc = job[1]
                            if hc == 0:
                                kb.wgm(e, "v", f"a2b_qcn_{oc}")
                            for rc in range(NRQ):
                                mm_ = e.matmul(slot[:],
                                               wqu[:, rc * D + hc * 128:rc * D + (hc + 1) * 128],
                                               qcn[:, rc * 512:(rc + 1) * 512],
                                               start=(rc == 0), stop=(rc == NRQ - 1))
                        else:
                            pc = job[1]
                            for rc in range(NRQ):
                                mm_ = e.matmul(slot[:],
                                               wqr[:, rc * NH * DR + pc * 128:rc * NH * DR + (pc + 1) * 128],
                                               qcn[:, rc * 512:(rc + 1) * 512],
                                               start=(rc == 0), stop=(rc == NRQ - 1))
                        kb.inc(mm_, "p")
                        kb.mark(f"a2b_mm_{g}", "p")

            def a2b_vec(e):
                for oc in range(4):
                    for i, job in enumerate(q_jobs()):
                        g = oc * JQ + i
                        slot = ps_mm[g % 4]
                        kb.wgm(e, "p", f"a2b_mm_{g}")
                        if job[0] == "qc":
                            rc = job[1]
                            kb.inc(e.tensor_copy(yq[:, rc * 512:(rc + 1) * 512], slot[:]),
                                   "v")
                            kb.mark(f"a2b_yq_{oc}_{rc}", "v")
                            if rc == NRQ - 1:
                                kb.wgm(e, "p", f"a2b_stmm_{oc}")
                                rms_row_vec(e, Rr, ps_st, QR, f"a2b_rms{oc}")
                                kb.wgm(e, "p", f"a2b_bc_{oc}")
                                kb.wgm(e, "qc", "a2b_c")
                                kb.selfw(e, "v")
                                for rr in range(NRQ):
                                    kb.inc(e.scalar_tensor_tensor(
                                        qcn[:, rr * 512:(rr + 1) * 512],
                                        yq[:, rr * 512:(rr + 1) * 512],
                                        qnw[:, rr:rr + 1], ps_bc[:],
                                        ALU.mult, ALU.mult), "v")
                                kb.mark(f"a2b_qcn_{oc}", "v")
                        elif job[0] == "qn":
                            hc = job[1]
                            lin = oc * 24 + hc
                            kb.wgm(e, f"stg{lin % 2}", f"a2b_stg_{lin - 2}")
                            kb.inc(e.tensor_copy(qst[lin % 2][:], slot[:]), "v")
                            kb.mark(f"a2b_qnev_{lin}", "v")
                        else:
                            pc = job[1]
                            lin = oc * 24 + 16 + pc
                            kb.wgm(e, f"stg{lin % 2}", f"a2b_stg_{lin - 2}")
                            kb.wgm(e, f"csq{oc % 2}", f"a2b_csq_{oc}")
                            cc = csq[oc % 2][0:32, :]
                            ss = csq[oc % 2][64:96, :]
                            kb.inc(rope_apply(e, qst[lin % 2][:], slot[:], cc, ss,
                                              ropesc[0][:], ropesc[1][:]), "v")
                            kb.mark(f"a2b_qpev_{lin}", "v")
                        kb.mark(f"a2b_mmev_{g}", "v")

            def a2b_scalar(e):
                for oc in range(4):
                    for rc in range(NRQ):
                        kb.wgm(e, "v", f"a2b_yq_{oc}_{rc}")
                        if rc >= 2:
                            kb.wgm(e, "p", f"a2b_stp_{oc}_{rc - 2}")
                        else:
                            kb.wgm(e, "p", f"a2b_stp_{oc - 1}_{rc + 10}")
                        kb.inc(e.activation(sqq[rc % 2][:], yq[:, rc * 512:(rc + 1) * 512],
                                            AF.Square), "a")
                        kb.mark(f"a2b_sq_{oc}_{rc}", "a")
                    kb.wgm(e, "v", f"a2b_rms{oc}_w1")
                    kb.inc(e.sqrt(Rr(2), Rr(1)), "a")
                    kb.mark(f"a2b_rms{oc}_r0", "a")
                    for hc in range(NH):
                        lin = oc * 24 + hc
                        kb.wgm(e, "v", f"a2b_qnev_{lin}")
                        kb.inc(e.dma_start(
                            out=qnscr[hc * 128:(hc + 1) * 128, oc * 512:(oc + 1) * 512],
                            in_=qst[lin % 2][:]), f"stg{lin % 2}", 16)
                        kb.mark(f"a2b_stg_{lin}", f"stg{lin % 2}")
                    for pc in range(NH // 2):
                        lin = oc * 24 + 16 + pc
                        kb.wgm(e, "v", f"a2b_qpev_{lin}")
                        kb.inc(e.dma_start(
                            out=qpscr[pc * 128:(pc + 1) * 128, oc * 512:(oc + 1) * 512],
                            in_=qst[lin % 2][:]), f"stg{lin % 2}", 16)
                        kb.mark(f"a2b_stg_{lin}", f"stg{lin % 2}")
                for s_ in ["stg0", "stg1"]:
                    kb.mark(f"a2b_done_{s_}", s_)

            if upto >= 3:
                run_phase([("sync", a2b_sync), ("gpsimd", a2b_gp), ("tensor", a2b_tensor),
                           ("vector", a2b_vec), ("scalar", a2b_scalar)])

        with ExitStack() as ctx:
            sb = lambda n, sh, d: ctx.enter_context(nc.sbuf_tensor(n, sh, d))
            ps = lambda n, sh, d: ctx.enter_context(nc.psum_tensor(n, sh, d))
            qng = [sb(f"b_qng{i}", [128, NH * 256], BF16) for i in range(2)]
            qpg = [sb(f"b_qpg{i}", [128, (NH // 2) * 256], BF16) for i in range(2)]
            kbuf = [[sb(f"b_kb{i}_{h}", [128, S], BF16) for h in range(2)] for i in range(2)]
            vbuf = [[sb(f"b_vb0_{h}", [128, S], BF16) for h in range(2)]]
            srow = [sb(f"b_srow{i}", [128, S], F32) for i in range(4)]
            mxb = [sb(f"b_mxb{i}", [128, 8], F32) for i in range(4)]
            mcol = [sb(f"b_mcol{i}", [128, 1], F32) for i in range(4)]
            bias = [sb(f"b_bias{i}", [128, 1], F32) for i in range(4)]
            ell = [sb(f"b_ell{i}", [128, 1], F32) for i in range(4)]
            linv = [sb(f"b_linv{i}", [128, 1], F32) for i in range(4)]
            pb = [sb(f"b_pb{i}", [128, S], BF16) for i in range(4)]
            ptb = sb("b_ptb", [128, 2 * S], BF16)
            msk = sb("b_msk", [128, 1024], F32)
            ostg = [sb(f"b_ostg{i}", [128, 256], BF16) for i in range(2)]
            ps_s = [ps(f"b_pss{i}", [128, 512], F32) for i in range(4)]
            ps_pt = [ps(f"b_pspt{i}", [128, 512], BF16) for i in range(2)]
            ps_pv = [ps(f"b_pspv{i}", [128, 256], F32) for i in range(2)]

            def ncount(gi):
                return gi + 1

            def b_sync(e):
                for s_ in ["stg0", "stg1"]:
                    kb.wgm(e, s_, f"a2b_done_{s_}")
                for s_ in ["stk0", "stk1", "stv0", "stv1"]:
                    kb.wgm(e, s_, f"a1_done_{s_}")
                for gi in range(8):
                    kb.wgm(e, "p", f"b_smmd_{(gi - 2) * 8 + 7}_1")
                    sq_ = f"q{gi % 2}"
                    kb.inc(e.dma_start(
                        out=r3(qng[gi % 2][:], NH),
                        in_=dr3(qnscr)[:, :, gi * 256:(gi + 1) * 256]), sq_, 16)
                    kb.inc(e.dma_start(
                        out=r3(qpg[gi % 2][:], NH // 2),
                        in_=dr3(qpscr)[:, :, gi * 256:(gi + 1) * 256]), sq_, 16)
                    kb.mark(f"b_q_{gi}", sq_)
                    kb.wgm(e, "v", f"b_maskd_{gi - 1}")
                    kb.inc(e.dma_start(
                        out=r3(msk[:], 2),
                        in_=dr3(mask_d)[:, 2 * gi:2 * gi + 2, :]), "qm", 16)
                    kb.mark(f"b_m_{gi}", "qm")
                    nc_ = ncount(gi)
                    for hp in range(8):
                        gih = gi * 8 + hp
                        for hh in range(2):
                            h = 2 * hp + hh
                            sk_ = f"k{hp % 2}{hh}"
                            kb.wgm(e, "p", f"b_smmd_{gih - 2}_{hh}")
                            kb.inc(e.dma_start(
                                out=kbuf[hp % 2][hh][:, 0:nc_ * 512],
                                in_=kscr[h * 128:(h + 1) * 128, 0:nc_ * 512]), sk_, 16)
                            kb.mark(f"b_k_{gih}_{hh}", sk_)
                        for hh in range(2):
                            h = 2 * hp + hh
                            sv_ = f"v0{hh}"
                            kb.wgm(e, "p", f"b_pv_{gih - 1}_{hh}")
                            kb.inc(e.dma_start(
                                out=r3(vbuf[0][hh][:, 0:nc_ * 4 * 128], nc_ * 4),
                                in_=dr3(vscr[h * 32 * 128:(h * 32 + nc_ * 4) * 128, :])),
                                sv_, 16)
                            kb.mark(f"b_v_{gih}_{hh}", sv_)

            def b_tensor(e):
                for gi in range(8):
                    nc_ = ncount(gi)
                    for hp in range(8):
                        gih = gi * 8 + hp
                        kb.wgm(e, f"q{gi % 2}", f"b_q_{gi}")
                        for hh in range(2):
                            kb.wgm(e, f"k{hp % 2}{hh}", f"b_k_{gih}_{hh}")
                        for c in range(nc_):
                            for s_ in range(2):
                                prev = (f"b_sev_{gih}_{c - 1}_{s_ * 2 + 1}" if c > 0 else
                                        f"b_sev_{gih - 1}_{ncount((gih - 1) // 8) - 1}_3"
                                        if gih > 0 else None)
                                if prev:
                                    kb.wgm(e, "v", prev)
                                qe = qpg[gi % 2]
                                base = hp * 256 + s_ * 128
                                kv_c = slice(c * 512, (c + 1) * 512)
                                for hh in range(2):
                                    h = 2 * hp + hh
                                    qn_t = qng[gi % 2][:, h * 256 + s_ * 128:h * 256 + (s_ + 1) * 128]
                                    e.matmul(ps_s[s_ * 2 + hh][:], qn_t,
                                             kbuf[hp % 2][hh][:, kv_c],
                                             start=True, stop=False,
                                             skip_group_check=True)
                                e.matmul(ps_s[s_ * 2][:], qe[0:64, base:base + 128],
                                         kro[0:64, kv_c], start=False, stop=True,
                                         skip_group_check=True)
                                mm_ = e.matmul(ps_s[s_ * 2 + 1][:],
                                               qe[64:128, base:base + 128],
                                               kro[64:128, kv_c], start=False, stop=True,
                                               skip_group_check=True)
                                kb.inc(mm_, "p")
                                kb.mark(f"b_smm_{gih}_{c}_{s_ * 2}", "p")
                                kb.mark(f"b_smm_{gih}_{c}_{s_ * 2 + 1}", "p")
                        kb.mark(f"b_smmd_{gih}_0", "p")
                        kb.mark(f"b_smmd_{gih}_1", "p")
                        combo_order = [0, 2, 1, 3]
                        for ci_, combo in enumerate(combo_order):
                            s_, hh = combo // 2, combo % 2
                            if True:
                                kb.wgm(e, "v", f"b_pn_{gih}_{combo}")
                                for bq in range(nc_):
                                    if bq < 2:
                                        pc_ = (combo_order[ci_ - 1] if ci_ > 0 else 3)
                                        pg_ = gih if ci_ > 0 else gih - 1
                                        pn_ = ncount(pg_ // 8) if pg_ >= 0 else 1
                                        kb.wgm(e, "v", f"b_ptev_{pg_}_{pc_}_{pn_ - 1}")
                                    else:
                                        kb.wgm(e, "v", f"b_ptev_{gih}_{combo}_{bq - 2}")
                                    for u in range(4):
                                        kc = bq * 4 + u
                                        mm_ = e.transpose(
                                            ps_pt[bq % 2][:, u * 128:(u + 1) * 128],
                                            pb[combo][:, kc * 128:(kc + 1) * 128], i16[:])
                                    kb.inc(mm_, "p")
                                    kb.mark(f"b_tr_{gih}_{combo}_{bq}", "p")
                                kb.mark(f"b_trd_{gih}_{combo}", "p")
                            if s_ == 1:
                                kb.wgm(e, "v", f"b_ptev_{gih}_{2 + hh}_{nc_ - 1}")
                                kb.wgm(e, "v", f"b_ptev_{gih}_{0 + hh}_{nc_ - 1}")
                                kb.wgm(e, f"v0{hh}", f"b_v_{gih}_{hh}")
                                kb.wgm(e, "v", f"b_oev_{gih - 1}_{hh}")
                                ptr = ptb[:].rearrange("p (s c q) -> p c s q",
                                                       s=2, c=32, q=128)
                                for kc in range(nc_ * 4):
                                    mm_ = e.matmul(ps_pv[hh][:],
                                                   vbuf[0][hh][:, kc * 128:(kc + 1) * 128],
                                                   ptr[:, kc],
                                                   start=(kc == 0),
                                                   stop=(kc == nc_ * 4 - 1))
                                kb.inc(mm_, "p")
                                kb.mark(f"b_pv_{gih}_{hh}", "p")

            def b_vec(e):
                for gi in range(8):
                    nc_ = ncount(gi)
                    for hp in range(8):
                        gih = gi * 8 + hp
                        for c in range(nc_):
                            for combo in range(4):
                                kb.wgm(e, "p", f"b_smm_{gih}_{c}_{combo}")
                                if c == 0:
                                    kb.wgm(e, "a", f"b_exp_{gih - 1}_{combo}")
                                kb.inc(e.tensor_scalar(
                                    srow[combo][:, c * 512:(c + 1) * 512],
                                    ps_s[combo][:], SCALE, None, ALU.mult, ALU.max,
                                    accum_out=mxb[combo][:, c:c + 1]), "v")
                                kb.mark(f"b_sev_{gih}_{c}_{combo}", "v")
                        for combo in range(4):
                            s_ = combo // 2
                            kb.wgm(e, "qm", f"b_m_{gi}")
                            kb.selfw(e, "v")
                            ch(e, e.tensor_add(
                                srow[combo][:, (nc_ - 1) * 512:nc_ * 512],
                                srow[combo][:, (nc_ - 1) * 512:nc_ * 512],
                                msk[:, s_ * 512:(s_ + 1) * 512]))
                            ch(e, e.reduce_max(mcol[combo][:], mxb[combo][:, 0:nc_],
                                               AX.X))
                            kb.selfw(e, "v")
                            kb.inc(e.tensor_scalar(bias[combo][:], mcol[combo][:], -1.0,
                                                   None, ALU.mult, ALU.bypass), "v")
                            kb.mark(f"b_bias_{gih}_{combo}", "v")
                        if hp == 7:
                            kb.mark(f"b_maskd_{gi}", "v")
                        combo_order = [0, 2, 1, 3]
                        for ci_, combo in enumerate(combo_order):
                            s_, hh = combo // 2, combo % 2
                            kb.wgm(e, "a", f"b_exp_{gih}_{combo}")
                            kb.inc(e.reciprocal(linv[combo][:], ell[combo][:]), "v")
                            kb.mark(f"b_linv_{gih}_{combo}", "v")
                            kb.selfw(e, "v")
                            kb.inc(e.tensor_scalar(pb[combo][:, 0:nc_ * 512],
                                                   pb[combo][:, 0:nc_ * 512],
                                                   linv[combo][:], None,
                                                   ALU.mult, ALU.bypass), "v")
                            kb.mark(f"b_pn_{gih}_{combo}", "v")
                            for bq in range(nc_):
                                kb.wgm(e, "p", f"b_tr_{gih}_{combo}_{bq}")
                                if bq == 0 and s_ == 0:
                                    kb.wgm(e, "p", (f"b_pv_{gih - 1}_1" if hh == 0
                                                    else f"b_pv_{gih}_0"))
                                kb.inc(e.tensor_copy(
                                    ptb[:, s_ * S + bq * 512:s_ * S + (bq + 1) * 512],
                                    ps_pt[bq % 2][:]), "v")
                                kb.mark(f"b_ptev_{gih}_{combo}_{bq}", "v")
                            if s_ == 1:
                                kb.wgm(e, "p", f"b_pv_{gih}_{hh}")
                                kb.wgm(e, f"sto{hh}", f"b_ost_{gih - 1}_{hh}")
                                kb.inc(e.tensor_copy(ostg[hh][:], ps_pv[hh][:]), "v")
                                kb.mark(f"b_oev_{gih}_{hh}", "v")

            def b_scalar(e):
                for gi in range(8):
                    nc_ = ncount(gi)
                    for hp in range(8):
                        gih = gi * 8 + hp
                        for combo in [0, 2, 1, 3]:
                            s_, hh = combo // 2, combo % 2
                            kb.wgm(e, "v", f"b_bias_{gih}_{combo}")
                            kb.wgm(e, "p", f"b_trd_{gih - 1}_{combo}")
                            kb.inc(e.activation(pb[combo][:, 0:nc_ * 512],
                                                srow[combo][:, 0:nc_ * 512], AF.Exp,
                                                bias=bias[combo][:], scale=1.0,
                                                accum_out=ell[combo][:]), "a")
                            kb.mark(f"b_exp_{gih}_{combo}", "a")
                        for hh in range(2):
                            h = 2 * hp + hh
                            kb.wgm(e, "v", f"b_oev_{gih}_{hh}")
                            kb.inc(e.dma_start(
                                out=oscr[h * 128:(h + 1) * 128, gi * 256:(gi + 1) * 256],
                                in_=ostg[hh][:]), f"sto{hh}", 16)
                            kb.mark(f"b_ost_{gih}_{hh}", f"sto{hh}")
                for s_ in ["sto0", "sto1"]:
                    kb.mark(f"b_done_{s_}", s_)

            def b_gp(e):
                pass

            if upto >= 4:
                run_phase([("sync", b_sync), ("gpsimd", b_gp), ("tensor", b_tensor),
                           ("vector", b_vec), ("scalar", b_scalar)])

        with ExitStack() as ctx:
            sb = lambda n, sh, d: ctx.enter_context(nc.sbuf_tensor(n, sh, d))
            ps = lambda n, sh, d: ctx.enter_context(nc.psum_tensor(n, sh, d))
            wo = sb("b2_wo", [128, NH * D], BF16)
            ob = [sb(f"b2_ob{i}", [128, NH * 512], BF16) for i in range(2)]
            sstg = [sb(f"b2_sstg{i}", [128, 512], F32) for i in range(2)]
            ps_wo = [ps(f"b2_pswo{i}", [128, 512], F32) for i in range(2)]

            def b2_sync(e):
                for s_ in ["sto0", "sto1"]:
                    kb.wgm(e, s_, f"b_done_{s_}")
                for tc in range(4):
                    kb.wgm(e, "p", f"b2_mmd_{tc - 2}")
                    so_ = f"ob{tc % 2}"
                    kb.inc(e.dma_start(
                        out=r3(ob[tc % 2][:], NH),
                        in_=dr3(oscr)[:, :, tc * 512:(tc + 1) * 512]), so_, 16)
                    kb.mark(f"b2_o_{tc}", so_)

            def b2_gp(e):
                kb.inc(e.dma_start(out=r3(wo[:], NH), in_=dr3(woT_d)), "qw", 16)
                kb.mark("b2_w", "qw")

            def b2_tensor(e):
                kb.wgm(e, "qw", "b2_w")
                for tc in range(4):
                    kb.wgm(e, f"ob{tc % 2}", f"b2_o_{tc}")
                    for do in range(NDC):
                        g = tc * NDC + do
                        kb.wgm(e, "v", f"b2_sev_{g - 2}")
                        for hd in range(NH):
                            mm_ = e.matmul(ps_wo[g % 2][:],
                                           wo[:, hd * D + do * 128:hd * D + (do + 1) * 128],
                                           ob[tc % 2][:, hd * 512:(hd + 1) * 512],
                                           start=(hd == 0), stop=(hd == NH - 1))
                        kb.inc(mm_, "p")
                        kb.mark(f"b2_mm_{g}", "p")
                    kb.mark(f"b2_mmd_{tc}", "p")

            def b2_vec(e):
                for tc in range(4):
                    for do in range(NDC):
                        g = tc * NDC + do
                        kb.wgm(e, "p", f"b2_mm_{g}")
                        kb.wgm(e, f"sts{g % 2}", f"b2_sst_{g - 2}")
                        kb.inc(e.tensor_copy(sstg[g % 2][:], ps_wo[g % 2][:]), "v")
                        kb.mark(f"b2_sev_{g}", "v")

            def b2_scalar(e):
                for tc in range(4):
                    for do in range(NDC):
                        g = tc * NDC + do
                        kb.wgm(e, "v", f"b2_sev_{g}")
                        kb.inc(e.dma_start(
                            out=sscr[do * 128:(do + 1) * 128, tc * 512:(tc + 1) * 512],
                            in_=sstg[g % 2][:]), f"sts{g % 2}", 16)
                        kb.mark(f"b2_sst_{g}", f"sts{g % 2}")
                for s_ in ["sts0", "sts1"]:
                    kb.mark(f"b2_done_{s_}", s_)

            if upto >= 5:
                run_phase([("sync", b2_sync), ("gpsimd", b2_gp), ("tensor", b2_tensor),
                           ("vector", b2_vec), ("scalar", b2_scalar)])

        with ExitStack() as ctx:
            sb = lambda n, sh, d: ctx.enter_context(nc.sbuf_tensor(n, sh, d))
            ps = lambda n, sh, d: ctx.enter_context(nc.psum_tensor(n, sh, d))
            xn = sb("c_xn", [128, NDC * 1024], BF16)
            hdn = sb("c_hdn", [128, NIC * 1024], BF16)
            gw = [sb(f"c_gw{i}", [128, NDC * 128], BF16) for i in range(2)]
            uw = [sb(f"c_uw{i}", [128, NDC * 128], BF16) for i in range(2)]
            dw = [sb(f"c_dw{i}", [128, NIC * 128], BF16) for i in range(2)]
            gsb = sb("c_gs", [128, NIC], F32)
            usb = sb("c_us", [128, NIC], F32)
            dsb = sb("c_ds", [128, NDC], F32)
            sg = [sb(f"c_sg{i}", [128, 512], F32) for i in range(2)]
            su = [sb(f"c_su{i}", [128, 512], F32) for i in range(2)]
            tsil = [sb(f"c_tsil{i}", [128, 512], F32) for i in range(2)]
            ssb = [sb(f"c_ssb{i}", [128, 1024], F32) for i in range(2)]
            xfb = [sb(f"c_xfb{i}", [128, 1024], F32) for i in range(2)]
            ee = [sb(f"c_ee{i}", [128, 512], F32) for i in range(2)]
            ostg = [sb(f"c_ostg{i}", [128, 512], F32) for i in range(2)]
            ps_g = [ps(f"c_psg{i}", [128, 512], F32) for i in range(2)]
            ps_u = [ps(f"c_psu{i}", [128, 512], F32) for i in range(2)]
            ps_d = [ps(f"c_psd{i}", [128, 512], F32) for i in range(2)]
            ps_t = [ps(f"c_pst{i}", [128, 512], F32) for i in range(2)]

            def c_sync(e):
                kb.inc(e.dma_start(out=gsb[:], in_=gs_d), "qc", 16)
                kb.inc(e.dma_start(out=usb[:], in_=us_d), "qc", 16)
                kb.inc(e.dma_start(out=dsb[:], in_=ds_d), "qc", 16)
                kb.mark("c_sc", "qc")
                for s_ in ["stn0", "stn1", "stf0", "stf1"]:
                    kb.wgm(e, s_, f"a2_done_{s_}")
                for s_ in ["sts0", "sts1"]:
                    kb.wgm(e, s_, f"b2_done_{s_}")
                for ft in range(2):
                    kb.wgm(e, "p", f"c_gumm_{(ft - 1) * NIC + NIC - 1}_1")
                    kb.inc(e.dma_start(
                        out=r3(xn[:], NDC),
                        in_=dr3(xnscr)[:, :, ft * 1024:(ft + 1) * 1024]), "cxn", 16)
                    kb.mark(f"c_xn_{ft}", "cxn")
                    for do in range(NDC):
                        g = ft * NDC + do
                        sx_ = f"sx{do % 2}"
                        kb.wgm(e, "v", f"c_ef_{g - 2}_1")
                        kb.inc(e.dma_start(
                            out=ssb[do % 2][:],
                            in_=sscr[do * 128:(do + 1) * 128, ft * 1024:(ft + 1) * 1024]),
                            sx_, 16)
                        kb.inc(e.dma_start(
                            out=xfb[do % 2][:],
                            in_=xfscr[do * 128:(do + 1) * 128, ft * 1024:(ft + 1) * 1024]),
                            sx_, 16)
                        kb.mark(f"c_sx_{g}", sx_)

            def c_gp(e):
                for ft in range(2):
                    for i in range(NIC):
                        g = ft * NIC + i
                        sg_ = f"gu{i % 2}"
                        kb.wgm(e, "p", f"c_gumm_{g - 2}_1")
                        kb.inc(e.dma_start(
                            out=r3(gw[i % 2][:], NDC),
                            in_=dr3(gateT_d)[:, :, i * 128:(i + 1) * 128]), sg_, 16)
                        kb.inc(e.dma_start(
                            out=r3(uw[i % 2][:], NDC),
                            in_=dr3(upT_d)[:, :, i * 128:(i + 1) * 128]), sg_, 16)
                        kb.mark(f"c_guw_{g}", sg_)
                    for do in range(NDC):
                        g = ft * NDC + do
                        sd_ = f"dw{do % 2}"
                        kb.wgm(e, "p", f"c_dmm_{g - 2}_1")
                        kb.inc(e.dma_start(
                            out=r3(dw[do % 2][:], NIC),
                            in_=dr3(downT_d)[:, :, do * 128:(do + 1) * 128]), sd_, 16)
                        kb.mark(f"c_dw_{g}", sd_)

            def c_tensor(e):
                for ft in range(2):
                    kb.wgm(e, "cxn", f"c_xn_{ft}")
                    for i in range(NIC):
                        g = ft * NIC + i
                        kb.wgm(e, f"gu{i % 2}", f"c_guw_{g}")
                        for half in range(2):
                            kb.wgm(e, "a", f"c_gu_{g - 1}_{half}")
                            for dc in range(NDC):
                                e.matmul(ps_g[half][:],
                                         gw[i % 2][:, dc * 128:(dc + 1) * 128],
                                         xn[:, dc * 1024 + half * 512:dc * 1024 + (half + 1) * 512],
                                         start=(dc == 0), stop=(dc == NDC - 1))
                            for dc in range(NDC):
                                mm_ = e.matmul(ps_u[half][:],
                                               uw[i % 2][:, dc * 128:(dc + 1) * 128],
                                               xn[:, dc * 1024 + half * 512:dc * 1024 + (half + 1) * 512],
                                               start=(dc == 0), stop=(dc == NDC - 1))
                            kb.inc(mm_, "p")
                            kb.mark(f"c_gumm_{g}_{half}", "p")
                    for do in range(NDC):
                        g = ft * NDC + do
                        kb.wgm(e, f"dw{do % 2}", f"c_dw_{g}")
                        kb.wgm(e, "v", f"c_hdn_{ft}")
                        for half in range(2):
                            kb.wgm(e, "a", f"c_sil_{g - 1}_{half}")
                            for ic in range(NIC):
                                mm_ = e.matmul(ps_d[half][:],
                                               dw[do % 2][:, ic * 128:(ic + 1) * 128],
                                               hdn[:, ic * 1024 + half * 512:ic * 1024 + (half + 1) * 512],
                                               start=(ic == 0), stop=(ic == NIC - 1))
                            kb.inc(mm_, "p")
                            kb.mark(f"c_dmm_{g}_{half}", "p")
                        for half in range(2):
                            kb.wgm(e, "v", f"c_ef_{g}_{half}")
                            kb.wgm(e, "v", f"c_ep_{g - 1}_{half}_3")
                            for u in range(4):
                                mm_ = e.transpose(ps_t[half][:, u * 128:(u + 1) * 128],
                                                  ee[half][:, u * 128:(u + 1) * 128],
                                                  i32[:])
                            kb.inc(mm_, "p")
                            kb.mark(f"c_tr_{g}_{half}", "p")

            def c_vec(e):
                for ft in range(2):
                    for i in range(NIC):
                        g = ft * NIC + i
                        for half in range(2):
                            kb.wgm(e, "a", f"c_gu_{g}_{half}")
                            kb.wgm(e, "p", f"c_dmm_{(ft - 1) * NDC + NDC - 1}_1")
                            kb.inc(e.tensor_mul(
                                hdn[:, i * 1024 + half * 512:i * 1024 + (half + 1) * 512],
                                sg[half][:], su[half][:]), "v")
                            kb.mark(f"c_hd_{g}_{half}", "v")
                    kb.mark(f"c_hdn_{ft}", "v")
                    for do in range(NDC):
                        g = ft * NDC + do
                        for half in range(2):
                            kb.wgm(e, "a", f"c_sil_{g}_{half}")
                            kb.wgm(e, f"sx{do % 2}", f"c_sx_{g}")
                            kb.wgm(e, "p", f"c_tr_{g - 1}_{half}")
                            ch(e, e.tensor_mul(ee[half][:],
                                               ssb[do % 2][:, half * 512:(half + 1) * 512],
                                               tsil[half][:]))
                            kb.selfw(e, "v")
                            kb.inc(e.tensor_add(ee[half][:], ee[half][:],
                                                xfb[do % 2][:, half * 512:(half + 1) * 512]),
                                   "v")
                            kb.mark(f"c_ef_{g}_{half}", "v")
                        for half in range(2):
                            kb.wgm(e, "p", f"c_tr_{g}_{half}")
                            kb.wgm(e, f"out{half}", f"c_ost_{g - 1}_{half}_3")
                            for j in range(4):
                                kb.inc(e.tensor_copy(
                                    ostg[half][:, j * 128:(j + 1) * 128],
                                    ps_t[half][:, j * 128:(j + 1) * 128]), "v")
                                kb.mark(f"c_ep_{g}_{half}_{j}", "v")

            def c_scalar(e):
                kb.wgm(e, "qc", "c_sc")
                for ft in range(2):
                    for i in range(NIC):
                        g = ft * NIC + i
                        for half in range(2):
                            kb.wgm(e, "p", f"c_gumm_{g}_{half}")
                            kb.wgm(e, "v", f"c_hd_{g - 1}_{half}")
                            e.activation(sg[half][:], ps_g[half][:], AF.Silu,
                                         bias=0.0, scale=gsb[:, i:i + 1])
                            kb.inc(e.activation(su[half][:], ps_u[half][:], AF.Copy,
                                                bias=0.0, scale=usb[:, i:i + 1]), "a")
                            kb.mark(f"c_gu_{g}_{half}", "a")
                    for do in range(NDC):
                        g = ft * NDC + do
                        for half in range(2):
                            kb.wgm(e, "p", f"c_dmm_{g}_{half}")
                            kb.wgm(e, "v", f"c_ef_{g - 1}_{half}")
                            kb.inc(e.activation(tsil[half][:], ps_d[half][:], AF.Silu,
                                                bias=0.0, scale=dsb[:, do:do + 1]), "a")
                            kb.mark(f"c_sil_{g}_{half}", "a")
                        for half in range(2):
                            for j in range(4):
                                kb.wgm(e, "v", f"c_ep_{g}_{half}_{j}")
                                row = ft * 1024 + half * 512 + j * 128
                                kb.inc(e.dma_start(
                                    out=out_d[row:row + 128, do * 128:(do + 1) * 128],
                                    in_=ostg[half][:, j * 128:(j + 1) * 128]),
                                    f"out{half}", 16)
                                kb.mark(f"c_ost_{g}_{half}_{j}", f"out{half}")
                for s_ in ["out0", "out1"]:
                    kb.mark(f"c_done_{s_}", s_)

            if upto >= 6:
                run_phase([("sync", c_sync), ("gpsimd", c_gp), ("tensor", c_tensor),
                           ("vector", c_vec), ("scalar", c_scalar)])

        def fin_sync(e):
            for s_ in ["out0", "out1"]:
                kb.wgm(e, s_, f"c_done_{s_}")

        run_phase([("sync", fin_sync)])

    return nc, kb


_BUILD_CACHE = {}


def kernel(**inputs):
    maps, metas = _prep(inputs)
    if "nc" not in _BUILD_CACHE:
        _BUILD_CACHE["nc"] = build(debug=False)[0]
    nc = _BUILD_CACHE["nc"]
    res = run_bass_kernel_spmd(nc, maps, core_ids=list(range(8)))
    y = np.empty((B_, S, D), dtype=np.float32)
    for c in range(8):
        y[c // 2][metas[c]] = res.results[c]["out"]
    return y
